# version 20
# baseline (speedup 1.0000x reference)
import numpy as np
import ml_dtypes
from contextlib import ExitStack, contextmanager

import concourse.bass as bass
import concourse.mybir as mybir
from concourse.bass_utils import run_bass_kernel_spmd

F32 = mybir.dt.float32
BF16 = mybir.dt.bfloat16
I32 = mybir.dt.int32
AF = mybir.ActivationFunctionType
ALU = mybir.AluOpType
AX = mybir.AxisListType

D = 1024
NH = 16
DH = 64
ND = D // 128
E = 8
HMOE = 4096
H1 = 3072
QI = 1.0 / 127.0

ENGS = ("tensor", "vector", "scalar", "gpsimd", "sync")


class Em:

    def __init__(self, nc, gstack, block):
        self.nc = nc
        self.gstack = gstack
        self.block = block
        self.stack = gstack
        self.ops = {e: [] for e in ENGS}
        self._sems = {}
        self._n = {}
        self._slots = {}
        self._names = {}
        self.ph = self.sem("ph")

    def sem(self, name):
        if name not in self._sems:
            self._sems[name] = self.gstack.enter_context(self.nc.semaphore(name))
            self._n[name] = 0
        return self._sems[name]

    def bump(self, name, amt):
        self.sem(name)
        self._n[name] += amt
        return self._n[name]

    def n(self, name):
        self.sem(name)
        return self._n[name]

    def op(self, eng, fn, incs=(), drain=None):
        incs = [i for i in incs if i is not None]
        do_drain = bool(drain)
        def run(e, fn=fn, incs=tuple(incs), do_drain=do_drain):
            r = fn(e)
            if incs:
                r.then_inc(*incs[0])
            for s, a in incs[1:]:
                e.drain().then_inc(s, a)
            if do_drain and len(incs) <= 1:
                e.drain()
        self.ops[eng].append(run)

    def dr(self, eng):
        self.ops[eng].append(lambda e: e.drain())

    def wait(self, eng, sem, target):
        self.ops[eng].append(lambda e, s=sem, t=target: e.wait_ge(s, t))

    def inc(self, name, amt=1):
        h = self.sem(name)
        self.bump(name, amt)
        return (h, amt)

    def flush(self):
        for eng in ENGS:
            ops = self.ops[eng]
            if not ops:
                continue
            self.ops[eng] = []
            dec = getattr(self.block, eng)
            def body(e, ops=tuple(ops)):
                for fn in ops:
                    fn(e)
            dec(body)

    @contextmanager
    def scope(self):
        st = ExitStack()
        old = self.stack
        self.stack = st
        try:
            yield
        finally:
            try:
                self.flush()
            finally:
                st.close()
                self.stack = old

    def barrier(self, prods, waiters=ENGS):
        for p in prods:
            self.op(p, lambda e: e.drain(), incs=[self.inc("ph")], drain=False)
        t = self.n("ph")
        for w in waiters:
            self.wait(w, self.ph, t)

    def _uniq(self, name):
        n = self._names.get(name, 0)
        self._names[name] = n + 1
        return name if n == 0 else f"{name}__{n}"

    def sb(self, name, shape, dtype):
        return self.stack.enter_context(self.nc.sbuf_tensor(self._uniq(name), shape, dtype))

    def ps(self, name, shape, dtype=F32):
        return self.stack.enter_context(self.nc.psum_tensor(self._uniq(name), shape, dtype))

    def slot_state(self, name, k):
        if name not in self._slots:
            self._slots[name] = {
                "ld": [self.sem(f"{name}_ld{i}") for i in range(k)],
                "fr": [self.sem(f"{name}_fr{i}") for i in range(k)],
                "n_ld": [0] * k, "n_fr": [0] * k, "k": k,
            }
        st = self._slots[name]
        assert st["k"] == k
        return st


class DmaSlots:

    def __init__(self, em, name, k, alloc):
        self.em = em
        self.st = em.slot_state(name, k)
        self.bufs = [alloc(f"{name}_b{i}") for i in range(k)]
        self.k = k

    def load(self, qeng, i, out_fn, in_fn, transpose=False):
        st = self.st
        assert st["n_ld"][i] == st["n_fr"][i], "slot busy"
        if st["n_ld"][i] > 0:
            self.em.wait(qeng, st["fr"][i], st["n_fr"][i])
        b = self.bufs[i]
        if transpose:
            self.em.op(qeng, lambda e, b=b: e.dma_start_transpose(out_fn(b), in_fn(b)),
                       incs=[(st["ld"][i], 16)], drain=False)
        else:
            self.em.op(qeng, lambda e, b=b: e.dma_start(out=out_fn(b), in_=in_fn(b)),
                       incs=[(st["ld"][i], 16)], drain=False)
        st["n_ld"][i] += 1
        return b

    def wait_loaded(self, eng, i):
        self.em.wait(eng, self.st["ld"][i], 16 * self.st["n_ld"][i])

    def free_marker(self, i):
        self.st["n_fr"][i] += 1
        return (self.st["fr"][i], 1)


class PsumSlots:

    def __init__(self, em, name, k, alloc):
        self.em = em
        self.st = em.slot_state(name, k)
        self.bufs = [alloc(f"{name}_b{i}") for i in range(k)]
        self.k = k

    def begin(self, i):
        st = self.st
        assert st["n_ld"][i] == st["n_fr"][i], "psum slot busy"
        if st["n_ld"][i] > 0:
            self.em.wait("tensor", st["fr"][i], st["n_fr"][i])
        return self.bufs[i]

    def done_marker(self, i):
        self.st["n_ld"][i] += 1
        return (self.st["ld"][i], 1)

    def wait_full(self, eng, i):
        self.em.wait(eng, self.st["ld"][i], self.st["n_ld"][i])

    def free_marker(self, i):
        self.st["n_fr"][i] += 1
        return (self.st["fr"][i], 1)



def emit_ln(em, cst, x_ap, xnT, T, x_gate=None):
    NT = T // 128
    with em.scope():
        xs = DmaSlots(em, "lnx", 2, lambda n: em.sb(n, [128, D], F32))
        pt = PsumSlots(em, "psT", 2, lambda n: em.ps(n, [128, 128], BF16))
        xc = [em.sb(f"ln_xc{p}", [128, D], F32) for p in range(2)]
        xnt = [em.sb(f"ln_xnt{p}", [128, D], BF16) for p in range(2)]
        sq = em.sb("ln_sq", [128, D], F32)
        mean = [em.sb(f"ln_mean{p}", [128, 1], F32) for p in range(2)]
        var = [em.sb(f"ln_var{p}", [128, 1], F32) for p in range(2)]
        rstd = [em.sb(f"ln_rstd{p}", [128, 1], F32) for p in range(2)]
        for nt in range(NT):
            p = nt % 2
            if x_gate is not None:
                em.wait("sync", x_gate[nt][0], x_gate[nt][1])
            X = xs.load(
                "sync", p, lambda b: b[:, :],
                lambda b, nt=nt: x_ap[nt * 128:(nt + 1) * 128, :])
            xs.wait_loaded("vector", p)
            if nt >= 2:
                em.wait("vector", em.sem(f"lnxn{p}"), em.n(f"lnxn{p}"))
            em.op("vector", lambda e, X=X, p=p: e.reduce_sum(mean[p][:, :], X[:, :], axis=AX.X),
                  drain=True)
            em.op("vector", lambda e, p=p: e.tensor_scalar(
                mean[p][:, :], mean[p][:, :], 1.0 / D, None, op0=ALU.mult), drain=True)
            em.op("vector", lambda e, X=X, p=p: e.tensor_scalar(
                xc[p][:, :], X[:, :], mean[p][:, 0:1], None, op0=ALU.subtract),
                incs=[xs.free_marker(p), em.inc(f"lnv{p}")], drain=True)
            em.wait("scalar", em.sem(f"lnv{p}"), em.n(f"lnv{p}"))
            if nt >= 2:
                em.wait("scalar", em.sem(f"lnt{p}"), em.n(f"lnt{p}"))
            em.op("scalar", lambda e, p=p: e.activation(
                sq[:, :], xc[p][:, :], AF.Square, accum_out=var[p][:, :]), drain=True)
            em.op("scalar", lambda e, p=p: e.activation(
                var[p][:, :], var[p][:, :], AF.Ln, bias=cst["epsb"][:, 0:1], scale=1.0 / D),
                drain=True)
            em.op("scalar", lambda e, p=p: e.activation(
                rstd[p][:, :], var[p][:, :], AF.Exp, scale=-0.5), drain=True)
            em.op("scalar", lambda e, p=p: e.activation(
                xnt[p][:, :], xc[p][:, :], AF.Copy, scale=rstd[p][:, 0:1]),
                incs=[em.inc(f"lnxn{p}")], drain=True)
            em.wait("tensor", em.sem(f"lnxn{p}"), em.n(f"lnxn{p}"))
            for dt in range(ND):
                j = dt % 2
                P = pt.begin(j)
                last = (dt == ND - 1)
                em.op("tensor", lambda e, P=P, dt=dt, p=p: e.transpose(
                    P[:, :], xnt[p][:, dt * 128:(dt + 1) * 128], cst["ident_bf"][:, :]),
                    incs=[pt.done_marker(j), em.inc(f"lnt{p}") if last else None])
                eng = "vector" if j == 0 else "scalar"
                pt.wait_full(eng, j)
                if eng == "vector":
                    em.op(eng, lambda e, P=P, dt=dt, nt=nt: e.tensor_copy(
                        xnT[:, dt * T + nt * 128: dt * T + (nt + 1) * 128], P[:, :]),
                        incs=[pt.free_marker(j)])
                else:
                    em.op(eng, lambda e, P=P, dt=dt, nt=nt: e.activation(
                        xnT[:, dt * T + nt * 128: dt * T + (nt + 1) * 128], P[:, :], AF.Copy),
                        incs=[pt.free_marker(j)])


def emit_proj_T(em, cst, wT, col_w, xnT, outT, T, CW, ps, dt_waits=None):
    NC = T // CW
    for it in range(ND):
        for c in range(NC):
            j = (it * NC + c) % 3
            P = ps.begin(j)
            dm = ps.done_marker(j)
            for dt in range(ND):
                if dt_waits is not None and it == 0 and c == 0:
                    em.wait("tensor", dt_waits[dt][0], dt_waits[dt][1])
                sp = (dt == ND - 1)
                em.op("tensor", lambda e, P=P, dt=dt, it=it, c=c, sp=sp: e.matmul(
                    P[:, :],
                    wT[:, dt * col_w + it * 128: dt * col_w + (it + 1) * 128],
                    xnT[:, dt * T + c * CW: dt * T + (c + 1) * CW],
                    start=(dt == 0), stop=sp), incs=[dm] if sp else ())
            ps.wait_full("scalar", j)
            em.op("scalar", lambda e, P=P, it=it, c=c: e.activation(
                outT[:, it * T + c * CW: it * T + (c + 1) * CW], P[:, :], AF.Copy),
                incs=[ps.free_marker(j)])


def emit_v_tokmajor(em, cst, wT, xnT, v_buf, T, dt_waits=None):
    NT = T // 128
    em.op("vector", lambda e: e.memset(v_buf[:, :], 1.0), incs=[em.inc("cs_t")], drain=True)
    em.wait("vector", em.sem("cs_t"), em.n("cs_t"))
    ps = PsumSlots(em, "psD1", 2, lambda n: em.ps(n, [128, 1024], F32))
    for nt in range(NT):
        j = nt % 2
        P = ps.begin(j)
        dm = ps.done_marker(j)
        for dt in range(ND):
            if dt_waits is not None and nt == 0:
                em.wait("tensor", dt_waits[dt][0], dt_waits[dt][1])
            for half in range(2):
                last = (dt == ND - 1 and half == 1)
                em.op("tensor", lambda e, P=P, dt=dt, nt=nt, half=half, last=last: e.matmul(
                    P[:, half * 512:(half + 1) * 512],
                    xnT[:, dt * T + nt * 128: dt * T + (nt + 1) * 128],
                    wT[:, dt * 1024 + half * 512: dt * 1024 + (half + 1) * 512],
                    start=(dt == 0), stop=(dt == ND - 1)),
                    incs=[dm] if last else ())
        ps.wait_full("vector", j)
        for h in range(NH):
            mk = [ps.free_marker(j)] if h == NH - 1 else ()
            em.op("vector", lambda e, P=P, nt=nt, h=h: e.tensor_copy(
                v_buf[:, nt * (NH * 65) + 65 * h: nt * (NH * 65) + 65 * h + 64],
                P[:, h * 64:(h + 1) * 64]), incs=mk)


def emit_qk_attention(em, cst, wq_src, wk_src, xnT, v_buf, attn_buf, T, CW):
    NKT = T // 128
    NC = T // CW
    WAVE = 8
    iters = []
    gid = 0
    for it in range(ND):
        for c in range(NC):
            nki = min((c * CW + CW + 127) // 128, NKT)
            for w0 in range(0, nki, WAVE):
                iters.append((it, c, w0, min(w0 + WAVE, nki), w0 + WAVE >= nki, gid))
            gid += 1
    M = len(iters)
    with em.scope():
        wq = em.sb("wqT", [128, ND * 1024], BF16)
        wk = em.sb("wkT", [128, ND * 1024], BF16)
        qt = [em.sb(f"qt{p}", [128, T], BF16) for p in range(2)]
        kt = [em.sb(f"kt{p}", [128, T], BF16) for p in range(2)]
        sqq = em.sb("cs_sq", [128, T], BF16)
        m2b = em.sb("cs_m2", [2, 2], F32)
        mxb = em.sb("cs_mx", [2, NC], F32)
        prodb = em.sb("cs_prod", [2, 1], F32)
        negc = em.sb("negc", [128, NH], F32)
        expb = [[em.sb(f"at_e{p}{hl}", [128, WAVE * CW], BF16) for hl in range(2)]
                for p in range(2)]
        rec = [[em.sb(f"at_rec{p}{hl}", [1, CW], F32) for hl in range(2)]
               for p in range(2)]
        recb = em.sb("at_recb", [128, CW], F32)
        ps3 = PsumSlots(em, "ps3", 3, lambda n: em.ps(n, [128, CW], F32))
        psp = PsumSlots(em, "psP4", 4, lambda n: em.ps(n, [128, CW], F32))
        psc = PsumSlots(em, "psC1", 1, lambda n: em.ps(n, [128, CW], F32))
        qk_dtw = []
        for dt in range(ND):
            em.op("sync", lambda e, dt=dt: e.dma_start_transpose(
                wq[:, dt * 1024:(dt + 1) * 1024], wq_src[0:D, dt * 128:(dt + 1) * 128]),
                incs=[em.inc(f"wtrd{dt}", 16)])
            qk_dtw.append((em.sem(f"wtrd{dt}"), em.n(f"wtrd{dt}")))
            em.op("sync", lambda e, dt=dt: e.dma_start_transpose(
                wk[:, dt * 1024:(dt + 1) * 1024], wk_src[0:D, dt * 128:(dt + 1) * 128]),
                incs=[em.inc("w_pl", 16)])
        qk_total = (em.sem("w_pl"), em.n("w_pl"))

        pv_cnt = [0, 0]
        bc_cnt = [0, 0]
        qk_cnt = [0, 0]
        exp_frs = {}
        mask_tgt = {}
        pv_meta = {}

        def emit_qk_cs(it):
            p = it % 2
            if qk_cnt[p] > 0:
                em.wait("vector", em.sem(f"qdone{p}"), qk_cnt[p])
            for which, wbuf, obuf in (("q", wq, qt[p]), ("k", wk, kt[p])):
                if which == "k" and it == 0:
                    em.wait("tensor", qk_total[0], qk_total[1])
                for c4 in range(NC):
                    j = (c4 + (0 if which == "q" else NC)) % 3
                    P = ps3.begin(j)
                    dm = ps3.done_marker(j)
                    for dt in range(ND):
                        if which == "q" and it == 0 and c4 == 0:
                            em.wait("tensor", qk_dtw[dt][0], qk_dtw[dt][1])
                        sp = (dt == ND - 1)
                        em.op("tensor", lambda e, P=P, wbuf=wbuf, dt=dt, it=it, c4=c4, sp=sp: e.matmul(
                            P[:, :],
                            wbuf[:, dt * 1024 + it * 128: dt * 1024 + (it + 1) * 128],
                            xnT[:, dt * T + c4 * CW: dt * T + (c4 + 1) * CW],
                            start=(dt == 0), stop=sp), incs=[dm] if sp else ())
                    ps3.wait_full("vector", j)
                    em.op("vector", lambda e, P=P, obuf=obuf, c4=c4: e.tensor_copy(
                        obuf[:, c4 * CW:(c4 + 1) * CW], P[:, :]),
                        incs=[ps3.free_marker(j), em.inc("cs_v")])
            for which, obuf, col0 in (("q", qt[p], 0), ("k", kt[p], 1)):
                em.wait("vector", em.sem("cs_v"), em.n("cs_v"))
                em.op("vector", lambda e, obuf=obuf: e.tensor_tensor(
                    sqq[:, :], obuf[:, :], obuf[:, :], op=ALU.mult),
                    incs=[em.inc("cs_v")], drain=True)
                em.wait("tensor", em.sem("cs_v"), em.n("cs_v"))
                for c4 in range(NC):
                    j = c4 % 3
                    P = ps3.begin(j)
                    em.op("tensor", lambda e, P=P, c4=c4: e.matmul(
                        P[0:2, :], cst["sel2_bf"][:, 0:2], sqq[:, c4 * CW:(c4 + 1) * CW],
                        start=True, stop=True), incs=[ps3.done_marker(j)])
                    ps3.wait_full("vector", j)
                    em.op("vector", lambda e, P=P, c4=c4: e.reduce_max(
                        mxb[0:2, c4:c4 + 1], P[0:2, :], axis=AX.X),
                        incs=[ps3.free_marker(j)], drain=True)
                em.op("vector", lambda e, col0=col0: e.reduce_max(
                    m2b[0:2, col0:col0 + 1], mxb[0:2, 0:NC], axis=AX.X),
                    incs=[em.inc("cs_v")], drain=True)
            em.wait("vector", em.sem("cs_v"), em.n("cs_v"))
            em.op("vector", lambda e: e.tensor_tensor(
                prodb[0:2, 0:1], m2b[0:2, 0:1], m2b[0:2, 1:2], op=ALU.mult),
                incs=[em.inc("cs_v")], drain=True)
            em.wait("scalar", em.sem("cs_v"), em.n("cs_v"))
            em.op("scalar", lambda e: e.activation(prodb[0:2, 0:1], prodb[0:2, 0:1], AF.Sqrt),
                  incs=[em.inc("cs_s")], drain=True)
            em.wait("tensor", em.sem("cs_s"), em.n("cs_s"))
            for hl in range(2):
                j = hl % 3
                P = ps3.begin(j)
                sel = cst["bc2a_f32"] if hl == 0 else cst["bc2b_f32"]
                em.op("tensor", lambda e, P=P, sel=sel: e.matmul(
                    P[:, 0:1], sel[0:2, :], prodb[0:2, 0:1],
                    start=True, stop=True), incs=[ps3.done_marker(j)])
                ps3.wait_full("vector", j)
                em.op("vector", lambda e, P=P, it=it, hl=hl: e.tensor_scalar(
                    negc[:, 2 * it + hl: 2 * it + hl + 1], P[:, 0:1], -0.125, None,
                    op0=ALU.mult), incs=[ps3.free_marker(j), em.inc("cs_v")])
            em.wait("scalar", em.sem("cs_v"), em.n("cs_v"))

        def stage_S(m):
            it, c, w0, w1, lastw, gid = iters[m]
            p = m % 2
            pit = it % 2
            sc_blocks = []
            for ki in range(w0, w1):
                blk = []
                for hl in range(2):
                    lo = 64 * hl
                    h = 2 * it + hl
                    j = (2 * (ki - w0) + hl) % 3
                    qo = ki * 128 - c * CW

                    def emit_sc(j=j, ki=ki, hl=hl, lo=lo, h=h, qo=qo, it=it, c=c,
                                p=p, pit=pit, m=m, last_of_pair=(lastw and c == NC - 1 and ki == w1 - 1 and hl == 1)):
                        P = ps3.begin(j)
                        mks = [ps3.done_marker(j)]
                        if last_of_pair:
                            qk_cnt[pit] += 1
                            mks.append((em.sem(f"qdone{pit}"), 1))
                        em.op("tensor", lambda e, P=P, lo=lo: e.matmul(
                            P[:, :],
                            kt[pit][lo:lo + 64, ki * 128:(ki + 1) * 128],
                            qt[pit][lo:lo + 64, c * CW:(c + 1) * CW],
                            start=True, stop=True, tile_position=(lo, 0)), incs=mks)
                        if ki == w0 and hl == 0 and pv_cnt[p] > 0:
                            em.wait("scalar", em.sem(f"pvd{p}"), pv_cnt[p])
                            em.wait("vector", em.sem(f"pvd{p}"), pv_cnt[p])
                        ps3.wait_full("scalar", j)
                        eb = expb[p][hl]
                        kk = ki - w0
                        if qo < 0:
                            em.op("scalar", lambda e, P=P: e.activation(
                                eb[:, kk * CW:(kk + 1) * CW], P[:, :], AF.Exp,
                                bias=negc[:, h:h + 1], scale=0.125),
                                incs=[ps3.free_marker(j)])
                        else:
                            em.op("scalar", lambda e, P=P: e.activation(
                                eb[:, kk * CW + qo:(kk + 1) * CW], P[:, qo:CW], AF.Exp,
                                bias=negc[:, h:h + 1], scale=0.125),
                                incs=[ps3.free_marker(j)])
                            if qo > 0:
                                em.op("vector", lambda e: e.memset(
                                    eb[:, kk * CW: kk * CW + qo], 0.0))
                            st3 = ps3.st
                            em.wait("vector", st3["fr"][j], st3["n_fr"][j])
                            em.op("vector", lambda e: e.tensor_tensor(
                                eb[:, kk * CW + qo: kk * CW + qo + 128],
                                eb[:, kk * CW + qo: kk * CW + qo + 128],
                                cst["trimask_bf"][:, :], op=ALU.mult), incs=[em.inc("at_m")])
                        blk.append((ps3.st["fr"][j], ps3.st["n_fr"][j]))
                    emit_sc()
                sc_blocks.append((blk, em.n("at_m")))
            exp_frs[m] = sc_blocks

        def stage_P(m):
            it, c, w0, w1, lastw, gid = iters[m]
            p = m % 2
            pg = gid % 2
            blocks = exp_frs.pop(m)
            slots = []
            for hl in range(2):
                sl = 2 * pg + hl
                if w0 == 0:
                    Ppv = psp.begin(sl)
                else:
                    Ppv = psp.bufs[sl]
                slots.append(Ppv)
            for bi, (blk, mtgt) in enumerate(blocks):
                ki = w0 + bi
                for hl in range(2):
                    h = 2 * it + hl
                    s_, t_ = blk[hl]
                    em.wait("tensor", s_, t_)
                    em.wait("tensor", em.sem("at_m"), mtgt)
                    Ppv = slots[hl]
                    sl = 2 * pg + hl
                    sp_ = (lastw and ki == w1 - 1)
                    mks = []
                    if sp_:
                        mks.append(psp.done_marker(sl))
                        if hl == 1:
                            pv_cnt[p] += 1
                            mks.append((em.sem(f"pvd{p}"), 1))
                    em.op("tensor", lambda e, Ppv=Ppv, ki=ki, h=h, w0=w0, hl=hl, st_=(ki == 0), sp_=sp_, p=p: e.matmul(
                        Ppv[0:65, :],
                        v_buf[:, ki * (NH * 65) + 65 * h: ki * (NH * 65) + 65 * h + 65],
                        expb[p][hl][:, (ki - w0) * CW:(ki - w0 + 1) * CW],
                        start=st_, stop=sp_), incs=mks)
            if not lastw:
                pv_cnt[p] += 1
                em.op("tensor", lambda e: e.drain(), incs=[(em.sem(f"pvd{p}"), 1)])

        def stage_B(m):
            it, c, w0, w1, lastw, gid = iters[m]
            if not lastw:
                return
            p = gid % 2
            for hl in range(2):
                sl = 2 * p + hl
                psp.wait_full("scalar", sl)
            if bc_cnt[p] > 0:
                em.wait("scalar", em.sem(f"bcd{p}"), bc_cnt[p])
            for hl in range(2):
                sl = 2 * p + hl
                Ppv = psp.bufs[sl]
                em.op("scalar", lambda e, Ppv=Ppv, p=p, hl=hl: e.activation(
                    rec[p][hl][0:1, :], Ppv[64:65, :], AF.Ln), drain=True)
                em.op("scalar", lambda e, p=p, hl=hl: e.activation(
                    rec[p][hl][0:1, :], rec[p][hl][0:1, :], AF.Exp, scale=-1.0),
                    incs=[em.inc("at_r")], drain=True)
            em.wait("tensor", em.sem("at_r"), em.n("at_r"))
            Pbc = psc.begin(0)
            bc_cnt[p] += 1
            em.op("tensor", lambda e, Pbc=Pbc, p=p: e.matmul(
                Pbc[0:64, :], cst["ones_1x64_f32"][0:1, :], rec[p][0][0:1, :],
                start=True, stop=True))
            em.op("tensor", lambda e, Pbc=Pbc, p=p: e.matmul(
                Pbc[64:128, :], cst["ones_1x64_f32"][0:1, :], rec[p][1][0:1, :],
                start=True, stop=True), incs=[psc.done_marker(0), (em.sem(f"bcd{p}"), 1)])
            psc.wait_full("vector", 0)
            em.op("vector", lambda e, Pbc=Pbc: e.tensor_copy(recb[:, :], Pbc[:, :]),
                  incs=[psc.free_marker(0)], drain=True)
            for hl in range(2):
                sl = 2 * p + hl
                lo = 64 * hl
                Ppv = psp.bufs[sl]
                em.op("vector", lambda e, Ppv=Ppv, it=it, lo=lo, c=c: e.tensor_tensor(
                    attn_buf[lo:lo + 64, it * T + c * CW: it * T + (c + 1) * CW],
                    Ppv[0:64, :], recb[lo:lo + 64, :], op=ALU.mult),
                    incs=[psp.free_marker(sl)], drain=(hl == 1))

        emit_qk_cs(0)
        for t in range(M + 2):
            if t < M:
                it, c, w0, _, _, _ = iters[t]
                if c == 0 and w0 == 0 and it + 1 < ND:
                    emit_qk_cs(it + 1)
                stage_S(t)
            if 1 <= t <= M:
                stage_P(t - 1)
            if 2 <= t <= M + 1:
                stage_B(t - 2)


def emit_out_proj(em, cst, attn_buf, wo_aps, s_dram, T, dt_waits=None):
    NT = T // 128
    ps = PsumSlots(em, "psD1", 2, lambda n: em.ps(n, [128, 1024], F32))
    sout = DmaSlots(em, "wo_s", 2, lambda n: em.sb(n, [128, D], BF16))
    for nt in range(NT):
        j = nt % 2
        P = ps.begin(j)
        dm = ps.done_marker(j)
        for dt in range(ND):
            if dt_waits is not None and nt == 0:
                em.wait("tensor", dt_waits[dt][0], dt_waits[dt][1])
            for half in range(2):
                last = (dt == ND - 1 and half == 1)
                em.op("tensor", lambda e, P=P, dt=dt, half=half, nt=nt, last=last: e.matmul(
                    P[:, half * 512:(half + 1) * 512],
                    attn_buf[:, dt * T + nt * 128: dt * T + (nt + 1) * 128],
                    wo_aps[dt][:, half * 512:(half + 1) * 512],
                    start=(dt == 0), stop=(dt == ND - 1)), incs=[dm] if last else ())
        ps.wait_full("scalar", j)
        st = sout.st
        if st["n_ld"][j] > st["n_fr"][j]:
            em.wait("scalar", st["ld"][j], 16 * st["n_ld"][j])
            st["n_fr"][j] += 1
            em.op("scalar", lambda e: e.nop(), incs=[(st["fr"][j], 1)], drain=False)
        B = sout.bufs[j]
        em.op("scalar", lambda e, P=P, B=B: e.activation(B[:, :], P[:, :], AF.Copy),
              incs=[ps.free_marker(j)])
        em.wait("scalar", ps.st["fr"][j], ps.st["n_fr"][j])
        st["n_ld"][j] += 1
        em.op("scalar", lambda e, B=B, nt=nt: e.dma_start(
            out=s_dram[nt * 128:(nt + 1) * 128, :], in_=B[:, :]), incs=[(st["ld"][j], 16)])
    for j in range(2):
        st = sout.st
        while st["n_fr"][j] < st["n_ld"][j]:
            em.wait("scalar", st["ld"][j], 16 * st["n_ld"][j])
            st["n_fr"][j] += 1
            em.op("scalar", lambda e: e.nop(), incs=[(st["fr"][j], 1)], drain=False)


def emit_gates(em, cst, xnT, wg_bf, bg_bf, SEL, selT, T):
    NT = T // 128
    pg = PsumSlots(em, "psG", 2, lambda n: em.ps(n, [128, 8], F32))
    pt = PsumSlots(em, "psN", 2, lambda n: em.ps(n, [8, 128], F32))
    G = em.sb("gt_G", [128, 8], F32)
    Pr = em.sb("gt_P", [128, 8], F32)
    m1 = em.sb("gt_m1", [128, 1], F32)
    m1n = em.sb("gt_m1n", [128, 1], F32)
    eq = em.sb("gt_eq", [128, 8], F32)
    gm = em.sb("gt_gm", [128, 8], F32)
    m2 = em.sb("gt_m2", [128, 1], F32)
    msk = em.sb("gt_msk", [128, 8], F32)
    ssum = em.sb("gt_ss", [128, 1], F32)
    rden = em.sb("gt_rd", [128, 1], F32)
    for nt in range(NT):
        j = nt % 2
        P = pg.begin(j)
        for dt in range(ND):
            em.op("tensor", lambda e, P=P, dt=dt, nt=nt: e.matmul(
                P[:, :], xnT[:, dt * T + nt * 128: dt * T + (nt + 1) * 128],
                wg_bf[:, dt * 8:(dt + 1) * 8], start=(dt == 0), stop=False))
        em.op("tensor", lambda e, P=P: e.matmul(
            P[:, :], cst["ones_1x128_bf"][0:1, :], bg_bf[0:1, :], start=False, stop=True),
            incs=[pg.done_marker(j)])
        pg.wait_full("vector", j)
        if nt > 0:
            em.wait("vector", em.sem("cs_s"), em.n("cs_s"))
        em.op("vector", lambda e, P=P: e.tensor_copy(G[:, :], P[:, :]), incs=[pg.free_marker(j)], drain=True)
        em.op("vector", lambda e: e.reduce_max(m1n[:, :], G[:, :], axis=AX.X, negate=True), drain=True)
        em.op("vector", lambda e: e.reduce_max(m1[:, :], G[:, :], axis=AX.X), incs=[em.inc("cs_v")], drain=True)
        em.wait("scalar", em.sem("cs_v"), em.n("cs_v"))
        em.op("scalar", lambda e: e.activation(Pr[:, :], G[:, :], AF.Exp, bias=m1n[:, 0:1]),
              incs=[em.inc("cs_s")])
        em.wait("vector", em.sem("cs_s"), em.n("cs_s"))
        em.op("vector", lambda e: e.tensor_scalar(eq[:, :], G[:, :], m1[:, 0:1], None, op0=ALU.is_ge), drain=True)
        em.op("vector", lambda e: e.scalar_tensor_tensor(
            gm[:, :], eq[:, :], -1e30, G[:, :], op0=ALU.mult, op1=ALU.add), drain=True)
        em.op("vector", lambda e: e.reduce_max(m2[:, :], gm[:, :], axis=AX.X), drain=True)
        em.op("vector", lambda e: e.tensor_scalar(msk[:, :], G[:, :], m2[:, 0:1], None, op0=ALU.is_ge), drain=True)
        em.op("vector", lambda e: e.tensor_tensor(msk[:, :], msk[:, :], Pr[:, :], op=ALU.mult), drain=True)
        em.op("vector", lambda e: e.reduce_sum(ssum[:, :], msk[:, :], axis=AX.X), incs=[em.inc("cs_v")], drain=True)
        em.op("vector", lambda e: e.tensor_scalar(ssum[:, :], ssum[:, :], 1e-8, None, op0=ALU.add), drain=True)
        em.op("vector", lambda e: e.reciprocal(rden[:, :], ssum[:, :]), drain=True)
        em.op("vector", lambda e, nt=nt: e.tensor_scalar(
            SEL[:, nt * 8:(nt + 1) * 8], msk[:, :], rden[:, 0:1], None, op0=ALU.mult),
            incs=[em.inc("cs_v")], drain=True)
        em.wait("tensor", em.sem("cs_v"), em.n("cs_v"))
        Pt = pt.begin(j)
        em.op("tensor", lambda e, Pt=Pt, nt=nt: e.transpose(
            Pt[:, :], SEL[:, nt * 8:(nt + 1) * 8], cst["ident_f32"][:, :]),
            incs=[pt.done_marker(j)])
        pt.wait_full("vector", j)
        em.op("vector", lambda e, Pt=Pt, nt=nt: e.tensor_copy(
            selT[0:8, nt * 128:(nt + 1) * 128], Pt[0:8, :]), incs=[pt.free_marker(j)])


def emit_b1_prep(em, cst, w, b1_sb):
    HT = HMOE // 128
    with em.scope():
        b1n = em.sb("b1n", [8, HMOE], F32)
        pb = PsumSlots(em, "psG", 2, lambda n: em.ps(n, [128, 8], F32))
        em.op("sync", lambda e: e.dma_start(out=b1n[:, :], in_=w["b1"]),
              incs=[em.inc("w_pl", 16)])
        em.wait("tensor", em.sem("w_pl"), em.n("w_pl"))
        for t in range(HT):
            j = t % 2
            P = pb.begin(j)
            em.op("tensor", lambda e, P=P, t=t: e.transpose(
                P[:, :], b1n[0:8, t * 128:(t + 1) * 128], cst["ident_f32"][0:8, 0:8]),
                incs=[pb.done_marker(j)])
            pb.wait_full("vector", j)
            em.op("vector", lambda e, P=P, t=t: e.tensor_copy(
                b1_sb[:, t * 8:(t + 1) * 8], P[:, :]), incs=[pb.free_marker(j)])
        em.barrier(("vector",), ("tensor", "vector", "scalar"))


def _w1_src(w1_ap, e_, ht):
    r = w1_ap[e_].rearrange("(f p) h -> p f h", p=128)
    return r[:, :, ht * 128:(ht + 1) * 128]


def _w2_src(w2_ap, e_, grp):
    r = w2_ap[e_].rearrange("(g r p) f -> p g r f", p=128, g=4)
    return r[:, grp, :, :]


def emit_moe(em, cst, w, xnT, SEL, selT, b1_sb, b2_bf, g_acc, T, w1s, w2s):
    NT = T // 128
    CH = min(1024, T)
    NCH = T // CH
    NTL = CH // 128
    HT = HMOE // 128
    NHALF = CH // 512 if CH >= 512 else 1
    MOVW = min(512, CH)
    hT = em.sb("moe_hT", [128, HT * CH], BF16)
    ph1 = PsumSlots(em, "psD1", 2, lambda n: em.ps(n, [128, CH], F32))
    ph2 = PsumSlots(em, "psD2", 2, lambda n: em.ps(n, [128, 1024], F32))
    for cc in range(NCH):
        for e_ in range(E):
            if em.n("moe_b") > 0:
                em.wait("scalar", em.sem("moe_b"), em.n("moe_b"))
            h_ready = {}
            for ht in range(HT):
                i = ht % 2
                if cc == 0 and e_ == 0 and ht < 2:
                    W1b = w1s.bufs[i]
                else:
                    W1b = w1s.load(
                        "gpsimd", i,
                        lambda b: b[:, :].rearrange("p (f c) -> p f c", f=8),
                        lambda b, e_=e_, ht=ht: _w1_src(w["w1"], e_, ht))
                if ht in (2, 4):
                    grp0 = 0 if ht == 2 else 1
                    if not (cc == 0 and e_ == 0 and grp0 == 0):
                        w2s.load(
                            "gpsimd", grp0,
                            lambda b: b[:, :].rearrange("p (r f) -> p r f", r=8),
                            lambda b, e_=e_, grp0=grp0: _w2_src(w["w2"], e_, grp0))
                w1s.wait_loaded("tensor", i)
                j = ht % 2
                P = ph1.begin(j)
                dm = ph1.done_marker(j)
                fw = w1s.free_marker(i)
                for f in range(ND):
                    for half in range(NHALF):
                        last = (f == ND - 1 and half == NHALF - 1)
                        em.op("tensor", lambda e, P=P, W1b=W1b, f=f, half=half, cc=cc, last=last: e.matmul(
                            P[:, half * MOVW:(half + 1) * MOVW],
                            W1b[:, f * 128:(f + 1) * 128],
                            xnT[:, f * T + cc * CH + half * MOVW: f * T + cc * CH + (half + 1) * MOVW],
                            start=(f == 0), stop=(f == ND - 1)),
                            incs=[dm, fw] if last else ())
                ph1.wait_full("scalar", j)
                em.op("scalar", lambda e, P=P, ht=ht, e_=e_: e.activation(
                    hT[:, ht * CH:(ht + 1) * CH], P[:, :], AF.Gelu_apprx_tanh,
                    bias=b1_sb[:, ht * E + e_: ht * E + e_ + 1]),
                    incs=[ph1.free_marker(j)])
                h_ready[ht] = [(ph1.st["fr"][jj], ph1.st["n_fr"][jj]) for jj in range(2)]
            for grp in range(4):
                i = grp % 2
                for s_, t_ in h_ready[8 * grp + 7]:
                    em.wait("tensor", s_, t_)
                w2s.wait_loaded("tensor", i)
                W2b = w2s.bufs[i]
                fw2 = w2s.free_marker(i)
                em.dr("vector")
                for ntl in range(NTL):
                    nt = cc * NTL + ntl
                    j = ntl % 2
                    P = ph2.begin(j)
                    dm = ph2.done_marker(j)
                    for r in range(8):
                        hh = grp * 8 + r
                        for half in range(2):
                            last = (r == 7 and half == 1)
                            mks = [dm] if last else []
                            if last and ntl == NTL - 1:
                                mks.append(fw2)
                            if last and ntl == NTL - 1 and grp == 3:
                                mks.append(em.inc("moe_b"))
                            em.op("tensor", lambda e, P=P, W2b=W2b, hh=hh, r=r, half=half, ntl=ntl: e.matmul(
                                P[:, half * 512:(half + 1) * 512],
                                hT[:, hh * CH + ntl * 128: hh * CH + (ntl + 1) * 128],
                                W2b[:, r * 1024 + half * 512: r * 1024 + (half + 1) * 512],
                                start=(r == 0), stop=(r == 7)), incs=mks)
                    ph2.wait_full("vector", j)
                    if ntl == NTL - 1 and grp + 2 < 4:
                        w2s.load(
                            "gpsimd", i,
                            lambda b: b[:, :].rearrange("p (r f) -> p r f", r=8),
                            lambda b, e_=e_, g2=grp + 2: _w2_src(w["w2"], e_, g2))
                    first = (e_ == 0 and grp == 0)
                    for half in range(2):
                        mk = [ph2.free_marker(j)] if half == 1 else ()
                        if first:
                            em.op("vector", lambda e, P=P, nt=nt, e_=e_, half=half: e.tensor_scalar(
                                g_acc[:, nt * 1024 + half * 512: nt * 1024 + (half + 1) * 512],
                                P[:, half * 512:(half + 1) * 512],
                                SEL[:, nt * 8 + e_: nt * 8 + e_ + 1], None, op0=ALU.mult), incs=mk)
                        else:
                            em.op("vector", lambda e, P=P, nt=nt, e_=e_, half=half: e.scalar_tensor_tensor(
                                g_acc[:, nt * 1024 + half * 512: nt * 1024 + (half + 1) * 512],
                                P[:, half * 512:(half + 1) * 512],
                                SEL[:, nt * 8 + e_: nt * 8 + e_ + 1],
                                g_acc[:, nt * 1024 + half * 512: nt * 1024 + (half + 1) * 512],
                                op0=ALU.mult, op1=ALU.add), incs=mk)
    em.dr("vector")
    g_ready = []
    for nt in range(NT):
        j = nt % 2
        P = ph2.begin(j)
        dm = ph2.done_marker(j)
        for half in range(2):
            em.op("tensor", lambda e, P=P, nt=nt, half=half: e.matmul(
                P[:, half * 512:(half + 1) * 512],
                selT[0:8, nt * 128:(nt + 1) * 128],
                b2_bf[0:8, half * 512:(half + 1) * 512],
                start=True, stop=True), incs=[dm] if half == 1 else ())
        ph2.wait_full("vector", j)
        em.op("vector", lambda e, P=P, nt=nt: e.tensor_tensor(
            g_acc[:, nt * 1024:(nt + 1) * 1024], P[:, :],
            g_acc[:, nt * 1024:(nt + 1) * 1024], op=ALU.add), incs=[ph2.free_marker(j)])
        g_ready.append((ph2.st["fr"][j], ph2.st["n_fr"][j]))
    return g_ready


def emit_l1ffn(em, cst, xnT, g_dram, T, bf_dram):
    HT1 = H1 // 128
    TH = min(1024, T)
    NHALVES = T // TH
    NTH = TH // 128
    CL = min(1024, TH)
    NCL = TH // CL
    MW = min(512, CL)
    NHF = max(1, CL // 512)
    with em.scope():
        gs = em.sb("l1_gs", [128, HT1 * TH], BF16)
        wbuf = [em.sb(f"l1_w{b}", [128, HT1 * D], BF16) for b in range(2)]
        pp = PsumSlots(em, "psD1", 2, lambda n: em.ps(n, [128, 1024], F32))
        gout = DmaSlots(em, "rg", 2, lambda n: em.sb(n, [128, D], BF16))
        phases = []
        for th in range(NHALVES):
            phases += [("g", th), ("u", th), ("d", th)]
        g_done = {}
        u_done = {}
        for k, (phz, th) in enumerate(phases):
            b = k % 2
            wT = wbuf[b]
            t0 = th * TH
            if k >= 2:
                em.wait("sync", em.sem(f"l1wf{b}"), em.n(f"l1wf{b}"))
            dtw = None
            if phz in ("g", "u"):
                wname = "fg" if phz == "g" else "fu"
                if k == 0:
                    dtw = []
                for dt in range(ND):
                    semn = f"wtrd{dt}" if k == 0 else f"l1ld{b}"
                    em.op("sync", lambda e, wT=wT, dt=dt, wname=wname: e.dma_start_transpose(
                        wT[:, dt * H1:(dt + 1) * H1],
                        bf_dram[wname][0:H1, dt * 128:(dt + 1) * 128]),
                        incs=[em.inc(semn, 16)])
                    if k == 0:
                        dtw.append((em.sem(semn), em.n(semn)))
            else:
                for ht in range(HT1):
                    em.op("sync", lambda e, wT=wT, ht=ht: e.dma_start_transpose(
                        wT[:, ht * D:(ht + 1) * D],
                        bf_dram["fd"][0:D, ht * 128:(ht + 1) * 128]),
                        incs=[em.inc(f"l1ld{b}", 16)])
            if dtw is None:
                em.wait("tensor", em.sem(f"l1ld{b}"), em.n(f"l1ld{b}"))

            if phz in ("g", "u"):
                if phz == "g" and em.n("l1d") > 0:
                    em.wait("scalar", em.sem("l1d"), em.n("l1d"))
                for ht in range(HT1):
                    for cc in range(NCL):
                        j = (ht * NCL + cc) % 2
                        P = pp.begin(j)
                        dm = pp.done_marker(j)
                        for dt in range(ND):
                            if dtw is not None and ht == 0 and cc == 0:
                                em.wait("tensor", dtw[dt][0], dtw[dt][1])
                            for half in range(NHF):
                                last = (dt == ND - 1 and half == NHF - 1)
                                mks = [dm] if last else []
                                if last and ht == HT1 - 1 and cc == NCL - 1:
                                    mks.append(em.inc(f"l1wf{b}"))
                                em.op("tensor", lambda e, P=P, wT=wT, dt=dt, ht=ht, cc=cc, half=half, t0=t0: e.matmul(
                                    P[:, half * MW:(half + 1) * MW],
                                    wT[:, dt * H1 + ht * 128: dt * H1 + (ht + 1) * 128],
                                    xnT[:, dt * T + t0 + cc * CL + half * MW: dt * T + t0 + cc * CL + (half + 1) * MW],
                                    start=(dt == 0), stop=(dt == ND - 1)), incs=mks)
                        if phz == "g":
                            pp.wait_full("scalar", j)
                            em.op("scalar", lambda e, P=P, ht=ht, cc=cc: e.activation(
                                gs[:, ht * TH + cc * CL: ht * TH + (cc + 1) * CL], P[:, 0:CL],
                                AF.Silu, scale=QI), incs=[pp.free_marker(j)])
                            g_done[ht] = (pp.st["fr"][j], pp.st["n_fr"][j])
                        else:
                            pp.wait_full("vector", j)
                            s_, t_ = g_done[ht]
                            em.wait("vector", s_, t_)
                            em.op("vector", lambda e, P=P, ht=ht, cc=cc: e.tensor_tensor(
                                gs[:, ht * TH + cc * CL: ht * TH + (cc + 1) * CL], P[:, 0:CL],
                                gs[:, ht * TH + cc * CL: ht * TH + (cc + 1) * CL], op=ALU.mult),
                                incs=[pp.free_marker(j)])
                            u_done[ht] = (pp.st["fr"][j], pp.st["n_fr"][j])
            else:
                for ntl in range(NTH):
                    nt = th * NTH + ntl
                    j = ntl % 2
                    P = pp.begin(j)
                    dm = pp.done_marker(j)
                    for ht in range(HT1):
                        if ntl == 0:
                            s_, t_ = u_done[ht]
                            em.wait("tensor", s_, t_)
                        for half in range(2):
                            last = (ht == HT1 - 1 and half == 1)
                            mks = [dm] if last else []
                            if last and ntl == NTH - 1:
                                mks.append(em.inc(f"l1wf{b}"))
                                mks.append(em.inc("l1d"))
                            em.op("tensor", lambda e, P=P, wT=wT, ht=ht, ntl=ntl, half=half: e.matmul(
                                P[:, half * 512:(half + 1) * 512],
                                gs[:, ht * TH + ntl * 128: ht * TH + (ntl + 1) * 128],
                                wT[:, ht * D + half * 512: ht * D + (half + 1) * 512],
                                start=(ht == 0), stop=(ht == HT1 - 1)), incs=mks)
                    pp.wait_full("scalar", j)
                    gst = gout.st
                    if gst["n_ld"][j] > gst["n_fr"][j]:
                        em.wait("scalar", gst["ld"][j], 16 * gst["n_ld"][j])
                        gst["n_fr"][j] += 1
                        em.op("scalar", lambda e: e.nop(), incs=[(gst["fr"][j], 1)], drain=False)
                    B = gout.bufs[j]
                    em.op("scalar", lambda e, P=P, B=B: e.activation(
                        B[:, :], P[:, :], AF.Silu, scale=QI * QI), incs=[pp.free_marker(j)])
                    em.wait("scalar", pp.st["fr"][j], pp.st["n_fr"][j])
                    gst["n_ld"][j] += 1
                    em.op("scalar", lambda e, B=B, nt=nt: e.dma_start(
                        out=g_dram[nt * 128:(nt + 1) * 128, :], in_=B[:, :]),
                        incs=[(gst["ld"][j], 16)])
        for j in range(2):
            gst = gout.st
            while gst["n_fr"][j] < gst["n_ld"][j]:
                em.wait("scalar", gst["ld"][j], 16 * gst["n_ld"][j])
                gst["n_fr"][j] += 1
                em.op("scalar", lambda e: e.nop(), incs=[(gst["fr"][j], 1)], drain=False)
        em.barrier(("tensor", "scalar", "vector", "sync"), ENGS)


def emit_boundary(em, cst, x_prev, s_dram, g_dram, g_tgts, x_mid, xnT, T, g_needs_silu):
    NT = T // 128
    with em.scope():
        xs = DmaSlots(em, "lnx", 2, lambda n: em.sb(n, [128, D], F32))
        ss = DmaSlots(em, "wo_s", 2, lambda n: em.sb(n, [128, D], BF16))
        gg = DmaSlots(em, "rg", 2, lambda n: em.sb(n, [128, D], BF16))
        outs = DmaSlots(em, "ro", 2, lambda n: em.sb(n, [128, D], F32))
        pt = PsumSlots(em, "psT", 2, lambda n: em.ps(n, [128, 128], BF16))
        gsil = [em.sb(f"b_gsil{p}", [128, D], F32) for p in range(2)]
        tmp = [em.sb(f"b_tmp{p}", [128, D], F32) for p in range(2)]
        xc = [em.sb(f"b_xc{p}", [128, D], F32) for p in range(2)]
        xnt = [em.sb(f"b_xnt{p}", [128, D], BF16) for p in range(2)]
        sq = em.sb("b_sq", [128, D], F32)
        mean = [em.sb(f"b_mean{p}", [128, 1], F32) for p in range(2)]
        var = [em.sb(f"b_var{p}", [128, 1], F32) for p in range(2)]
        rstd = [em.sb(f"b_rstd{p}", [128, 1], F32) for p in range(2)]
        ost = outs.st
        ct = {}
        cssn = {}

        def emit_silu(k):
            i = k % 2
            if k >= 2:
                em.wait("scalar", ct[k - 2][0], ct[k - 2][1])
            if g_tgts is not None:
                em.wait("sync", g_tgts[k][0], g_tgts[k][1])
            G = gg.load("sync", i, lambda b: b[:, :],
                        lambda b, k=k: g_dram[k * 128:(k + 1) * 128, :])
            gg.wait_loaded("scalar", i)
            em.op("scalar", lambda e, G=G, p=i: e.activation(gsil[p][:, :], G[:, :], AF.Silu),
                  incs=[gg.free_marker(i), em.inc("cs_s")])
            cssn[k] = (em.sem("cs_s"), em.n("cs_s"))

        def stage_a(nt):
            i = nt % 2
            if ost["n_ld"][i] > ost["n_fr"][i]:
                em.wait("scalar", ost["ld"][i], 16 * ost["n_ld"][i])
                ost["n_fr"][i] += 1
                em.op("scalar", lambda e: e.nop(), incs=[(ost["fr"][i], 1)], drain=False)
            X = xs.load("sync", i, lambda b: b[:, :],
                        lambda b, nt=nt: x_prev[nt * 128:(nt + 1) * 128, :])
            S = ss.load("sync", i, lambda b: b[:, :],
                        lambda b, nt=nt: s_dram[nt * 128:(nt + 1) * 128, :])
            if g_needs_silu:
                if nt == 0:
                    emit_silu(0)
                if nt + 1 < NT:
                    emit_silu(nt + 1)
                em.wait("vector", cssn[nt][0], cssn[nt][1])
                ss.wait_loaded("vector", i)
                em.op("vector", lambda e, S=S, p=i: e.tensor_tensor(
                    tmp[p][:, :], S[:, :], gsil[p][:, :], op=ALU.mult),
                    incs=[ss.free_marker(i)], drain=True)
            else:
                if g_tgts is not None:
                    em.wait("sync", g_tgts[nt][0], g_tgts[nt][1])
                G = gg.load("sync", i, lambda b: b[:, :],
                            lambda b, nt=nt: g_dram[nt * 128:(nt + 1) * 128, :])
                ss.wait_loaded("vector", i)
                gg.wait_loaded("vector", i)
                em.op("vector", lambda e, S=S, G=G, p=i: e.tensor_tensor(
                    tmp[p][:, :], S[:, :], G[:, :], op=ALU.mult),
                    incs=[ss.free_marker(i), gg.free_marker(i)], drain=True)
            xs.wait_loaded("vector", i)
            if ost["n_fr"][i] > 0:
                em.wait("vector", ost["fr"][i], ost["n_fr"][i])
            O = outs.bufs[i]
            em.op("vector", lambda e, X=X, O=O, p=i: e.tensor_tensor(
                O[:, :], X[:, :], tmp[p][:, :], op=ALU.add),
                incs=[xs.free_marker(i), em.inc("cs_t")], drain=True)
            ct[nt] = (em.sem("cs_t"), em.n("cs_t"))

        def stage_b(nt):
            i = nt % 2
            O = outs.bufs[i]
            em.wait("scalar", ct[nt][0], ct[nt][1])
            ost["n_ld"][i] += 1
            em.op("scalar", lambda e, O=O, nt=nt: e.dma_start(
                out=x_mid[nt * 128:(nt + 1) * 128, :], in_=O[:, :]), incs=[(ost["ld"][i], 16)])
            em.op("vector", lambda e, O=O, p=i: e.reduce_sum(mean[p][:, :], O[:, :], axis=AX.X),
                  drain=True)
            em.op("vector", lambda e, p=i: e.tensor_scalar(
                mean[p][:, :], mean[p][:, :], 1.0 / D, None, op0=ALU.mult), drain=True)
            if nt >= 2:
                em.wait("vector", em.sem(f"lnxn{i}"), em.n(f"lnxn{i}"))
            em.op("vector", lambda e, O=O, p=i: e.tensor_scalar(
                xc[p][:, :], O[:, :], mean[p][:, 0:1], None, op0=ALU.subtract),
                incs=[em.inc(f"lnv{i}")], drain=True)

        def stage_c(nt):
            i = nt % 2
            em.wait("scalar", em.sem(f"lnv{i}"), em.n(f"lnv{i}"))
            if nt >= 2:
                em.wait("scalar", em.sem(f"lnt{i}"), em.n(f"lnt{i}"))
            em.op("scalar", lambda e, p=i: e.activation(
                sq[:, :], xc[p][:, :], AF.Square, accum_out=var[p][:, :]), drain=True)
            em.op("scalar", lambda e, p=i: e.activation(
                var[p][:, :], var[p][:, :], AF.Ln, bias=cst["epsb"][:, 0:1], scale=1.0 / D),
                drain=True)
            em.op("scalar", lambda e, p=i: e.activation(
                rstd[p][:, :], var[p][:, :], AF.Exp, scale=-0.5), drain=True)
            em.op("scalar", lambda e, p=i: e.activation(
                xnt[p][:, :], xc[p][:, :], AF.Copy, scale=rstd[p][:, 0:1]),
                incs=[em.inc(f"lnxn{i}")], drain=True)
            em.wait("tensor", em.sem(f"lnxn{i}"), em.n(f"lnxn{i}"))
            for dt in range(ND):
                j = dt % 2
                P = pt.begin(j)
                last = (dt == ND - 1)
                em.op("tensor", lambda e, P=P, dt=dt, p=i: e.transpose(
                    P[:, :], xnt[p][:, dt * 128:(dt + 1) * 128], cst["ident_bf"][:, :]),
                    incs=[pt.done_marker(j), em.inc(f"lnt{i}") if last else None])
                eng = "vector" if j == 0 else "scalar"
                pt.wait_full(eng, j)
                if eng == "vector":
                    em.op(eng, lambda e, P=P, dt=dt, nt=nt: e.tensor_copy(
                        xnT[:, dt * T + nt * 128: dt * T + (nt + 1) * 128], P[:, :]),
                        incs=[pt.free_marker(j)])
                else:
                    em.op(eng, lambda e, P=P, dt=dt, nt=nt: e.activation(
                        xnT[:, dt * T + nt * 128: dt * T + (nt + 1) * 128], P[:, :], AF.Copy),
                        incs=[pt.free_marker(j)])

        for it in range(NT + 2):
            if it < NT:
                stage_a(it)
            if 1 <= it <= NT:
                stage_b(it - 1)
            if 2 <= it <= NT + 1:
                stage_c(it - 2)
        for i in range(2):
            while ost["n_fr"][i] < ost["n_ld"][i]:
                em.wait("scalar", ost["ld"][i], 16 * ost["n_ld"][i])
                ost["n_fr"][i] += 1
                em.op("scalar", lambda e: e.nop(), incs=[(ost["fr"][i], 1)], drain=False)


def emit_residual(em, cst, x_ap, s_dram, g_buf, g_dram, x_out_ap, T, g_needs_silu):
    NT = T // 128
    xs = DmaSlots(em, "lnx", 2, lambda n: em.sb(n, [128, D], F32))
    ss = DmaSlots(em, "wo_s", 2, lambda n: em.sb(n, [128, D], BF16))
    outs = DmaSlots(em, "ro", 2, lambda n: em.sb(n, [128, D], F32))
    gg = DmaSlots(em, "rg", 2, lambda n: em.sb(n, [128, D], BF16)) if g_dram is not None else None
    gsil = em.sb("r_gsil", [128, D], F32)
    tmp = em.sb("r_tmp", [128, D], F32)
    ost = outs.st
    res_done = []
    for nt in range(NT):
        i = nt % 2
        if ost["n_ld"][i] > 0:
            em.wait("scalar", ost["ld"][i], 16 * ost["n_ld"][i])
            ost["n_fr"][i] += 1
            em.op("scalar", lambda e: e.nop(), incs=[(ost["fr"][i], 1)], drain=False)
        X = xs.load("sync", i, lambda b: b[:, :],
                    lambda b, nt=nt: x_ap[nt * 128:(nt + 1) * 128, :])
        S = ss.load("sync", i, lambda b: b[:, :],
                    lambda b, nt=nt: s_dram[nt * 128:(nt + 1) * 128, :])
        if gg is not None:
            G = gg.load("sync", i, lambda b: b[:, :],
                        lambda b, nt=nt: g_dram[nt * 128:(nt + 1) * 128, :])
        if g_needs_silu:
            if nt > 0:
                em.wait("scalar", em.sem("cs_v"), em.n("cs_v"))
            em.op("scalar", lambda e, nt=nt: e.activation(
                gsil[:, :], g_buf[:, nt * D:(nt + 1) * D], AF.Silu), incs=[em.inc("cs_s")])
            em.wait("vector", em.sem("cs_s"), em.n("cs_s"))
        ss.wait_loaded("vector", i)
        if g_needs_silu:
            em.op("vector", lambda e, S=S: e.tensor_tensor(
                tmp[:, :], S[:, :], gsil[:, :], op=ALU.mult),
                incs=[ss.free_marker(i), em.inc("cs_v")], drain=True)
        elif gg is not None:
            gg.wait_loaded("vector", i)
            em.op("vector", lambda e, S=S, G=G: e.tensor_tensor(
                tmp[:, :], S[:, :], G[:, :], op=ALU.mult),
                incs=[ss.free_marker(i), gg.free_marker(i)], drain=True)
        else:
            em.op("vector", lambda e, S=S, nt=nt: e.tensor_tensor(
                tmp[:, :], S[:, :], g_buf[:, nt * D:(nt + 1) * D], op=ALU.mult),
                incs=[ss.free_marker(i)], drain=True)
        xs.wait_loaded("vector", i)
        if ost["n_ld"][i] > 0:
            em.wait("vector", ost["fr"][i], ost["n_fr"][i])
        O = outs.bufs[i]
        em.op("vector", lambda e, X=X, O=O: e.tensor_tensor(
            O[:, :], X[:, :], tmp[:, :], op=ALU.add),
            incs=[xs.free_marker(i), em.inc("cs_t")], drain=True)
        em.wait("scalar", em.sem("cs_t"), em.n("cs_t"))
        ost["n_ld"][i] += 1
        em.op("scalar", lambda e, O=O, nt=nt: e.dma_start(
            out=x_out_ap[nt * 128:(nt + 1) * 128, :], in_=O[:, :]), incs=[(ost["ld"][i], 16)])
        res_done.append((ost["ld"][i], 16 * ost["n_ld"][i]))
    for i in range(2):
        em.wait("scalar", ost["ld"][i], 16 * ost["n_ld"][i])
    return res_done



def emit_weight_casts(em, w_all, layers, phase="all", prep=None):
    nc = em.nc
    prep = prep if prep is not None else {}
    for lay in layers:
        w = w_all[lay]
        sem = f"w_cast{lay}"
        if lay == 0 and phase == "post":
            sem = "w_cast0p"
        p = prep.setdefault(lay, {"bf_dram": {}, "sem": f"w_cast{lay}"})
        p.setdefault("sem_post", "w_cast0p" if lay == 0 else f"w_cast{lay}")
        bf_dram = p["bf_dram"]

        def cast_dma(dst, src, sem=sem):
            em.op("gpsimd", lambda e, dst=dst, src=src: e.dma_start(out=dst, in_=src),
                  incs=[em.inc(sem, 16)], drain=False)

        if lay == 0:
            if phase in ("pre", "all"):
                bf_dram["wv"] = _dram(nc, "wv_bf", [D, D], BF16)
                cast_dma(bf_dram["wv"], w["wv"])
                p["b2_bf"] = em.sb("b2c", [8, D], BF16)
                p["wg_bf"] = em.sb("wgc", [128, ND * 8], BF16)
                p["bg_bf"] = em.sb("bgc", [1, 8], BF16)
                cast_dma(p["b2_bf"][:, :], w["b2"])
                cast_dma(p["wg_bf"][:, :].rearrange("p (d e) -> p d e", d=ND),
                         w["wg"].rearrange("(d p) e -> p d e", p=128))
                cast_dma(p["bg_bf"][:, :], w["bg"])
            if phase in ("post", "all"):
                for nm in ("wq", "wk", "wo"):
                    bf_dram[nm] = _dram(nc, f"{nm}_bf", [D, D], BF16)
                    cast_dma(bf_dram[nm], w[nm])
        else:
            if phase in ("post", "all"):
                bf_dram["wqkv"] = _dram(nc, "wqkv_bf", [3 * D, D], BF16)
                cast_dma(bf_dram["wqkv"], w["wqkv"])
                bf_dram["wproj"] = _dram(nc, "wproj_bf", [D, D], BF16)
                cast_dma(bf_dram["wproj"], w["wproj"])
                for nm in ("fg", "fu"):
                    bf_dram[nm] = _dram(nc, f"{nm}_bf", [H1, D], BF16)
                    cast_dma(bf_dram[nm], w[nm])
                bf_dram["fd"] = _dram(nc, "fd_bf", [D, H1], BF16)
                cast_dma(bf_dram["fd"], w["fd"])
    return prep


def emit_layer(em, cst, w, lay, x_in, x_out, T, prep, boundary_in=None,
               post_ln=None, post_attn=None, is_last=True):
    nc = em.nc
    tag = f"L{lay}"
    NT = T // 128
    CW = min(512, T)
    bf_dram = prep["bf_dram"]
    cast_sem = prep["sem"]

    if lay == 0:
        b1_sb = em.sb(f"b1{tag}", [128, (HMOE // 128) * E], F32)
        b2_bf = prep["b2_bf"]
        wg_bf = prep["wg_bf"]
        bg_bf = prep["bg_bf"]

    xnT = em.sb(f"xnT{tag}", [128, ND * T], BF16)
    s_dram = _dram(nc, f"s_scr{tag}", [T, D], BF16)

    if boundary_in is None:
        emit_ln(em, cst, x_in, xnT, T)
    else:
        emit_boundary(em, cst, boundary_in["x_prev"], boundary_in["s_dram"],
                      boundary_in["g_dram"], boundary_in["g_tgts"], x_in, xnT, T,
                      boundary_in["silu"])
    if post_ln is not None:
        post_ln()
    em.barrier(("vector", "tensor", "scalar", "sync"), ENGS)

    with em.scope():
        v_buf = em.sb("vbuf", [128, NT * (NH * 65)], BF16)
        attn_buf = em.sb("attnT", [128, ND * T], BF16)
        woT = em.sb("woT", [128, 4 * D], BF16)
        with em.scope():
            wT = em.sb("wvT", [128, ND * 1024], BF16)
            em.wait("sync", em.sem(cast_sem), em.n(cast_sem))
            for dt in range(ND):
                if lay == 0:
                    srcw = bf_dram["wv"][0:D, dt * 128:(dt + 1) * 128]
                else:
                    srcw = bf_dram["wqkv"][2 * D:3 * D, dt * 128:(dt + 1) * 128]
                em.op("sync", lambda e, dt=dt, srcw=srcw: e.dma_start_transpose(
                    wT[:, dt * 1024:(dt + 1) * 1024], srcw), incs=[em.inc("w_pl", 16)])
            em.wait("tensor", em.sem("w_pl"), em.n("w_pl"))
            emit_v_tokmajor(em, cst, wT, xnT, v_buf, T, None)
            em.barrier(("tensor", "scalar", "vector", "gpsimd"),
                       ("tensor", "scalar", "vector", "sync", "gpsimd"))
        with em.scope():
            if lay == 0:
                wq_src, wk_src = bf_dram["wq"], bf_dram["wk"]
            else:
                wq_src = bf_dram["wqkv"][0:D, :]
                wk_src = bf_dram["wqkv"][D:2 * D, :]
            sp = prep["sem_post"]
            em.wait("sync", em.sem(sp), em.n(sp))
            emit_qk_attention(em, cst, wq_src, wk_src, xnT, v_buf, attn_buf, T, CW)
            em.wait("sync", em.sem("wo_tr"), em.n("wo_tr"))
            srcw = bf_dram["wo"] if lay == 0 else bf_dram["wproj"]
            for dt in range(4):
                em.op("sync", lambda e, dt=dt, srcw=srcw: e.dma_start_transpose(
                    woT[:, dt * D:(dt + 1) * D], srcw[0:D, dt * 128:(dt + 1) * 128]),
                    incs=[em.inc("wo_tr", 16)])
            wo_wA = (em.sem("wo_tr"), em.n("wo_tr"))
            if post_attn is not None:
                post_attn()
            em.barrier(("tensor", "vector", "scalar"), ENGS)
        with em.scope():
            woT2 = em.sb("woT2", [128, 4 * D], BF16)
            em.wait("sync", em.sem("wo_tr"), em.n("wo_tr"))
            for dt in range(4, ND):
                em.op("sync", lambda e, dt=dt, srcw=srcw: e.dma_start_transpose(
                    woT2[:, (dt - 4) * D:(dt - 3) * D], srcw[0:D, dt * 128:(dt + 1) * 128]),
                    incs=[em.inc("wo_tr", 16)])
            wo_wB = (em.sem("wo_tr"), em.n("wo_tr"))
            wo_aps = [woT[:, dt * D:(dt + 1) * D] for dt in range(4)] + \
                     [woT2[:, (dt - 4) * D:(dt - 3) * D] for dt in range(4, ND)]
            wo_w = [wo_wA] * 4 + [wo_wB] * 4
            emit_out_proj(em, cst, attn_buf, wo_aps, s_dram, T, wo_w)
            em.barrier(("tensor", "scalar", "sync", "vector"), ENGS)

    g_buf = g_dram = None
    g_tgts = None
    if lay == 0:
        g_buf = em.sb(f"gbuf{tag}", [128, NT * D], BF16)
        if not is_last:
            g_dram = _dram(nc, f"g_scr{tag}", [T, D], BF16)
        em.wait("tensor", em.sem(cast_sem), em.n(cast_sem))
        with em.scope():
            SEL = em.sb("SEL", [128, NT * 8], F32)
            selT = em.sb("selT", [8, T], BF16)
            w1s = DmaSlots(em, "w1s", 2, lambda n: em.sb(n, [128, 8 * 128], BF16))
            w2s = DmaSlots(em, "w2s", 2, lambda n: em.sb(n, [128, 8 * 1024], BF16))
            for ht0 in range(2):
                w1s.load("gpsimd", ht0,
                         lambda b: b[:, :].rearrange("p (f c) -> p f c", f=8),
                         lambda b, ht0=ht0: _w1_src(w["w1"], 0, ht0))
            w2s.load("gpsimd", 0,
                     lambda b: b[:, :].rearrange("p (r f) -> p r f", r=8),
                     lambda b: _w2_src(w["w2"], 0, 0))
            emit_b1_prep(em, cst, w, b1_sb)
            with em.scope():
                emit_gates(em, cst, xnT, wg_bf, bg_bf, SEL, selT, T)
                em.barrier(("vector", "tensor", "scalar"), ENGS)
            with em.scope():
                g_ready = emit_moe(em, cst, w, xnT, SEL, selT, b1_sb, b2_bf, g_buf, T, w1s, w2s)
                if not is_last:
                    g_tgts = []
                    gscnt = [0, 0]
                    for nt in range(NT):
                        i = nt % 2
                        s_, t_ = g_ready[nt]
                        em.wait("scalar", s_, t_)
                        if gscnt[i] > 0:
                            em.wait("scalar", em.sem(f"gs{i}"), 16 * gscnt[i])
                        em.op("scalar", lambda e, nt=nt: e.dma_start(
                            out=g_dram[nt * 128:(nt + 1) * 128, :],
                            in_=g_buf[:, nt * D:(nt + 1) * D]),
                            incs=[em.inc(f"gs{i}", 16)])
                        gscnt[i] += 1
                        g_tgts.append((em.sem(f"gs{i}"), 16 * gscnt[i]))
                    for i in range(2):
                        if gscnt[i]:
                            em.wait("scalar", em.sem(f"gs{i}"), 16 * gscnt[i])
                em.barrier(("tensor", "vector", "scalar", "gpsimd"), ENGS)
        g_needs_silu = True
    else:
        g_dram = _dram(nc, f"g_scr{tag}", [T, D], BF16)
        emit_l1ffn(em, cst, xnT, g_dram, T, bf_dram)
        g_needs_silu = False

    if is_last:
        emit_residual(em, cst, x_in, s_dram, g_buf, g_dram, x_out, T, g_needs_silu)
        em.barrier(("sync", "vector", "scalar"), ENGS)
        return None
    return {"x_prev": x_in, "s_dram": s_dram, "g_dram": g_dram,
            "g_tgts": g_tgts, "silu": g_needs_silu}



def _dram(nc, name, shape, dtype, kind="Internal"):
    return nc.dram_tensor(name, shape, dtype, kind=kind).ap()


_CONST_SPECS = {
    "ident_bf": ([128, 128], BF16), "ident_f32": ([128, 128], F32),
    "trimask_bf": ([128, 128], BF16),
    "selA_bf": ([128, 1], BF16), "selB_bf": ([128, 1], BF16),
    "ones_1x128_bf": ([1, 128], BF16), "ones_1x128_f32": ([1, 128], F32),
    "ones_1x64_f32": ([1, 64], F32), "epsb": ([128, 1], F32),
    "sel2_bf": ([128, 2], BF16), "bc2a_f32": ([2, 128], F32), "bc2b_f32": ([2, 128], F32),
}


def _const_arrays():
    bf = ml_dtypes.bfloat16
    tri = np.triu(np.ones((128, 128), dtype=np.float32)).astype(bf)
    selA = np.zeros((128, 1), dtype=np.float32); selA[:64] = 1.0
    selB = np.zeros((128, 1), dtype=np.float32); selB[64:] = 1.0
    return {
        "ident_bf": np.eye(128, dtype=np.float32).astype(bf),
        "ident_f32": np.eye(128, dtype=np.float32),
        "trimask_bf": tri,
        "selA_bf": selA.astype(bf),
        "selB_bf": selB.astype(bf),
        "ones_1x128_bf": np.ones((1, 128), dtype=np.float32).astype(bf),
        "ones_1x128_f32": np.ones((1, 128), dtype=np.float32),
        "ones_1x64_f32": np.ones((1, 64), dtype=np.float32),
        "epsb": np.full((128, 1), 1e-5, dtype=np.float32),
        "sel2_bf": np.stack([np.r_[np.ones(64), np.zeros(64)],
                             np.r_[np.zeros(64), np.ones(64)]], axis=1
                            ).astype(np.float32).astype(bf),
        "bc2a_f32": np.stack([np.ones(128), np.zeros(128)]).astype(np.float32),
        "bc2b_f32": np.stack([np.zeros(128), np.ones(128)]).astype(np.float32),
    }


def build_nc(layers, T=2048):
    nc = bass.Bass("TRN2", target_bir_lowering=False)
    cst_dram = {k: _dram(nc, k, shp, dt, kind="ExternalInput")
                for k, (shp, dt) in _CONST_SPECS.items()}
    x_in = _dram(nc, "x_in", [T, D], F32, kind="ExternalInput")
    x_out = _dram(nc, "x_out", [T, D], F32, kind="ExternalOutput")
    x_mid = _dram(nc, "x_mid", [T, D], F32) if len(layers) > 1 else None

    w_all = {}
    for lay in layers:
        if lay == 0:
            w_all[0] = {
                "wq": _dram(nc, "wq", [D, D], F32, kind="ExternalInput"),
                "wk": _dram(nc, "wk", [D, D], F32, kind="ExternalInput"),
                "wv": _dram(nc, "wv", [D, D], F32, kind="ExternalInput"),
                "wo": _dram(nc, "wo", [D, D], F32, kind="ExternalInput"),
                "wg": _dram(nc, "wg", [D, E], F32, kind="ExternalInput"),
                "bg": _dram(nc, "bg", [1, E], F32, kind="ExternalInput"),
                "w1": _dram(nc, "w1", [E, D, HMOE], F32, kind="ExternalInput"),
                "b1": _dram(nc, "b1", [E, HMOE], F32, kind="ExternalInput"),
                "w2": _dram(nc, "w2", [E, HMOE, D], F32, kind="ExternalInput"),
                "b2": _dram(nc, "b2", [E, D], F32, kind="ExternalInput"),
            }
        else:
            w_all[1] = {
                "wqkv": _dram(nc, "wqkv", [3 * D, D], F32, kind="ExternalInput"),
                "wproj": _dram(nc, "wproj", [D, D], F32, kind="ExternalInput"),
                "fg": _dram(nc, "fg", [H1, D], I32, kind="ExternalInput"),
                "fu": _dram(nc, "fu", [H1, D], I32, kind="ExternalInput"),
                "fd": _dram(nc, "fd", [D, H1], I32, kind="ExternalInput"),
            }

    with ExitStack() as gstack:
        block = gstack.enter_context(nc.Block())
        em = Em(nc, gstack, block)
        cst = {k: em.sb(f"c_{k}", shp, dt) for k, (shp, dt) in _CONST_SPECS.items()}
        for k in _CONST_SPECS:
            em.op("sync", lambda e, k=k: e.dma_start(out=cst[k][:, :], in_=cst_dram[k]),
                  incs=[em.inc("w_pl", 16)])
        em.wait("sync", em.sem("w_pl"), em.n("w_pl"))
        em.barrier(("sync",), ENGS)

        prep = emit_weight_casts(em, w_all, layers[:1], phase="pre")
        post_done = [False]
        if len(layers) == 1 and layers[0] == 1:
            emit_weight_casts(em, w_all, layers, phase="post", prep=prep)
            post_done[0] = True

        def post_ln():
            if post_done[0]:
                return
            post_done[0] = True
            tgt = max(0, em.n("lnv0") - 2)
            if tgt:
                em.wait("gpsimd", em.sem("lnv0"), tgt)
            emit_weight_casts(em, w_all, layers[:1], phase="post", prep=prep)

        def post_attn():
            if len(layers) <= 1 or 1 in prep and prep[1]["bf_dram"]:
                return
            tgt = em.n("at_r") // 2
            if tgt:
                em.wait("gpsimd", em.sem("at_r"), tgt)
            emit_weight_casts(em, w_all, layers[1:], phase="post", prep=prep)

        cur_in = x_in
        binfo = None
        for idx, lay in enumerate(layers):
            last = (idx == len(layers) - 1)
            out_ap = x_out if last else x_mid
            with em.scope():
                binfo = emit_layer(em, cst, w_all[lay], lay, cur_in, out_ap, T,
                                   prep[lay], boundary_in=binfo,
                                   post_ln=(post_ln if idx == 0 else None),
                                   post_attn=(post_attn if idx == 0 else None),
                                   is_last=last)
            cur_in = out_ap
        em.flush()
    return nc


def _in_map_for(layers, x_core, inputs):
    m = dict(_const_arrays())
    m["x_in"] = x_core
    if 0 in layers:
        m.update({
            "wq": inputs["L0_attn_Wq"], "wk": inputs["L0_attn_Wk"],
            "wv": inputs["L0_attn_Wv"], "wo": inputs["L0_attn_Wo"],
            "wg": inputs["L0_ffn_Wg"],
            "bg": np.ascontiguousarray(np.asarray(inputs["L0_ffn_bg"]).reshape(1, E)),
            "w1": inputs["L0_ffn_W1"], "b1": inputs["L0_ffn_b1"],
            "w2": inputs["L0_ffn_W2"], "b2": inputs["L0_ffn_b2"],
        })
    if 1 in layers:
        m.update({
            "wqkv": inputs["L1_attn_w_qkv"], "wproj": inputs["L1_attn_w_proj"],
            "fg": inputs["L1_ffn_w_gate"], "fu": inputs["L1_ffn_w_up"],
            "fd": inputs["L1_ffn_w_down"],
        })
    return {k: np.ascontiguousarray(np.asarray(v)) for k, v in m.items()}


def kernel(**inputs):
    x = np.ascontiguousarray(np.asarray(inputs["L0_attn_x"], dtype=np.float32))
    B = x.shape[0]
    layers = (0, 1)
    nc = build_nc(layers)
    in_maps = [_in_map_for(layers, np.ascontiguousarray(x[c]), inputs) for c in range(B)]
    res = run_bass_kernel_spmd(nc, in_maps, core_ids=list(range(B)))
    return np.stack([np.asarray(r["x_out"], dtype=np.float32) for r in res.results], axis=0)


# version 22
# speedup vs baseline: 1.0040x; 1.0040x over previous
import numpy as np
import ml_dtypes
from contextlib import ExitStack, contextmanager

import concourse.bass as bass
import concourse.mybir as mybir
from concourse.bass_utils import run_bass_kernel_spmd

F32 = mybir.dt.float32
BF16 = mybir.dt.bfloat16
I32 = mybir.dt.int32
AF = mybir.ActivationFunctionType
ALU = mybir.AluOpType
AX = mybir.AxisListType

D = 1024
NH = 16
DH = 64
ND = D // 128
E = 8
HMOE = 4096
H1 = 3072
QI = 1.0 / 127.0

ENGS = ("tensor", "vector", "scalar", "gpsimd", "sync")


class Em:

    def __init__(self, nc, gstack, block):
        self.nc = nc
        self.gstack = gstack
        self.block = block
        self.stack = gstack
        self.ops = {e: [] for e in ENGS}
        self._sems = {}
        self._n = {}
        self._slots = {}
        self._names = {}
        self.ph = self.sem("ph")

    def sem(self, name):
        if name not in self._sems:
            self._sems[name] = self.gstack.enter_context(self.nc.semaphore(name))
            self._n[name] = 0
        return self._sems[name]

    def bump(self, name, amt):
        self.sem(name)
        self._n[name] += amt
        return self._n[name]

    def n(self, name):
        self.sem(name)
        return self._n[name]

    def op(self, eng, fn, incs=(), drain=None):
        incs = [i for i in incs if i is not None]
        do_drain = bool(drain)
        def run(e, fn=fn, incs=tuple(incs), do_drain=do_drain):
            r = fn(e)
            if incs:
                r.then_inc(*incs[0])
            for s, a in incs[1:]:
                e.drain().then_inc(s, a)
            if do_drain and len(incs) <= 1:
                e.drain()
        self.ops[eng].append(run)

    def dr(self, eng):
        self.ops[eng].append(lambda e: e.drain())

    def wait(self, eng, sem, target):
        self.ops[eng].append(lambda e, s=sem, t=target: e.wait_ge(s, t))

    def inc(self, name, amt=1):
        h = self.sem(name)
        self.bump(name, amt)
        return (h, amt)

    def flush(self):
        for eng in ENGS:
            ops = self.ops[eng]
            if not ops:
                continue
            self.ops[eng] = []
            dec = getattr(self.block, eng)
            def body(e, ops=tuple(ops)):
                for fn in ops:
                    fn(e)
            dec(body)

    @contextmanager
    def scope(self):
        st = ExitStack()
        old = self.stack
        self.stack = st
        try:
            yield
        finally:
            try:
                self.flush()
            finally:
                st.close()
                self.stack = old

    def barrier(self, prods, waiters=ENGS):
        for p in prods:
            self.op(p, lambda e: e.drain(), incs=[self.inc("ph")], drain=False)
        t = self.n("ph")
        for w in waiters:
            self.wait(w, self.ph, t)

    def _uniq(self, name):
        n = self._names.get(name, 0)
        self._names[name] = n + 1
        return name if n == 0 else f"{name}__{n}"

    def sb(self, name, shape, dtype):
        return self.stack.enter_context(self.nc.sbuf_tensor(self._uniq(name), shape, dtype))

    def ps(self, name, shape, dtype=F32):
        return self.stack.enter_context(self.nc.psum_tensor(self._uniq(name), shape, dtype))

    def slot_state(self, name, k):
        if name not in self._slots:
            self._slots[name] = {
                "ld": [self.sem(f"{name}_ld{i}") for i in range(k)],
                "fr": [self.sem(f"{name}_fr{i}") for i in range(k)],
                "n_ld": [0] * k, "n_fr": [0] * k, "k": k,
            }
        st = self._slots[name]
        assert st["k"] == k
        return st


class DmaSlots:

    def __init__(self, em, name, k, alloc):
        self.em = em
        self.st = em.slot_state(name, k)
        self.bufs = [alloc(f"{name}_b{i}") for i in range(k)]
        self.k = k

    def load(self, qeng, i, out_fn, in_fn, transpose=False):
        st = self.st
        assert st["n_ld"][i] == st["n_fr"][i], "slot busy"
        if st["n_ld"][i] > 0:
            self.em.wait(qeng, st["fr"][i], st["n_fr"][i])
        b = self.bufs[i]
        if transpose:
            self.em.op(qeng, lambda e, b=b: e.dma_start_transpose(out_fn(b), in_fn(b)),
                       incs=[(st["ld"][i], 16)], drain=False)
        else:
            self.em.op(qeng, lambda e, b=b: e.dma_start(out=out_fn(b), in_=in_fn(b)),
                       incs=[(st["ld"][i], 16)], drain=False)
        st["n_ld"][i] += 1
        return b

    def wait_loaded(self, eng, i):
        self.em.wait(eng, self.st["ld"][i], 16 * self.st["n_ld"][i])

    def free_marker(self, i):
        self.st["n_fr"][i] += 1
        return (self.st["fr"][i], 1)


class PsumSlots:

    def __init__(self, em, name, k, alloc):
        self.em = em
        self.st = em.slot_state(name, k)
        self.bufs = [alloc(f"{name}_b{i}") for i in range(k)]
        self.k = k

    def begin(self, i):
        st = self.st
        assert st["n_ld"][i] == st["n_fr"][i], "psum slot busy"
        if st["n_ld"][i] > 0:
            self.em.wait("tensor", st["fr"][i], st["n_fr"][i])
        return self.bufs[i]

    def done_marker(self, i):
        self.st["n_ld"][i] += 1
        return (self.st["ld"][i], 1)

    def wait_full(self, eng, i):
        self.em.wait(eng, self.st["ld"][i], self.st["n_ld"][i])

    def free_marker(self, i):
        self.st["n_fr"][i] += 1
        return (self.st["fr"][i], 1)



def emit_ln(em, cst, x_ap, xnT, T, x_gate=None):
    NT = T // 128
    with em.scope():
        xs = DmaSlots(em, "lnx", 2, lambda n: em.sb(n, [128, D], F32))
        pt = PsumSlots(em, "psT", 2, lambda n: em.ps(n, [128, 128], BF16))
        xc = [em.sb(f"ln_xc{p}", [128, D], F32) for p in range(2)]
        xnt = [em.sb(f"ln_xnt{p}", [128, D], BF16) for p in range(2)]
        sq = em.sb("ln_sq", [128, D], F32)
        mean = [em.sb(f"ln_mean{p}", [128, 1], F32) for p in range(2)]
        var = [em.sb(f"ln_var{p}", [128, 1], F32) for p in range(2)]
        rstd = [em.sb(f"ln_rstd{p}", [128, 1], F32) for p in range(2)]
        for nt in range(NT):
            p = nt % 2
            if x_gate is not None:
                em.wait("sync", x_gate[nt][0], x_gate[nt][1])
            X = xs.load(
                "sync", p, lambda b: b[:, :],
                lambda b, nt=nt: x_ap[nt * 128:(nt + 1) * 128, :])
            xs.wait_loaded("vector", p)
            if nt >= 2:
                em.wait("vector", em.sem(f"lnxn{p}"), em.n(f"lnxn{p}"))
            em.op("vector", lambda e, X=X, p=p: e.reduce_sum(mean[p][:, :], X[:, :], axis=AX.X),
                  drain=True)
            em.op("vector", lambda e, p=p: e.tensor_scalar(
                mean[p][:, :], mean[p][:, :], 1.0 / D, None, op0=ALU.mult), drain=True)
            em.op("vector", lambda e, X=X, p=p: e.tensor_scalar(
                xc[p][:, :], X[:, :], mean[p][:, 0:1], None, op0=ALU.subtract),
                incs=[xs.free_marker(p), em.inc(f"lnv{p}")], drain=True)
            em.wait("scalar", em.sem(f"lnv{p}"), em.n(f"lnv{p}"))
            if nt >= 2:
                em.wait("scalar", em.sem(f"lnt{p}"), em.n(f"lnt{p}"))
            em.op("scalar", lambda e, p=p: e.activation(
                sq[:, :], xc[p][:, :], AF.Square, accum_out=var[p][:, :]), drain=True)
            em.op("scalar", lambda e, p=p: e.activation(
                var[p][:, :], var[p][:, :], AF.Ln, bias=cst["epsb"][:, 0:1], scale=1.0 / D),
                drain=True)
            em.op("scalar", lambda e, p=p: e.activation(
                rstd[p][:, :], var[p][:, :], AF.Exp, scale=-0.5), drain=True)
            em.op("scalar", lambda e, p=p: e.activation(
                xnt[p][:, :], xc[p][:, :], AF.Copy, scale=rstd[p][:, 0:1]),
                incs=[em.inc(f"lnxn{p}")], drain=True)
            em.wait("tensor", em.sem(f"lnxn{p}"), em.n(f"lnxn{p}"))
            for dt in range(ND):
                j = dt % 2
                P = pt.begin(j)
                last = (dt == ND - 1)
                em.op("tensor", lambda e, P=P, dt=dt, p=p: e.transpose(
                    P[:, :], xnt[p][:, dt * 128:(dt + 1) * 128], cst["ident_bf"][:, :]),
                    incs=[pt.done_marker(j), em.inc(f"lnt{p}") if last else None])
                eng = "vector" if j == 0 else "scalar"
                pt.wait_full(eng, j)
                if eng == "vector":
                    em.op(eng, lambda e, P=P, dt=dt, nt=nt: e.tensor_copy(
                        xnT[:, dt * T + nt * 128: dt * T + (nt + 1) * 128], P[:, :]),
                        incs=[pt.free_marker(j)])
                else:
                    em.op(eng, lambda e, P=P, dt=dt, nt=nt: e.activation(
                        xnT[:, dt * T + nt * 128: dt * T + (nt + 1) * 128], P[:, :], AF.Copy),
                        incs=[pt.free_marker(j)])


def emit_proj_T(em, cst, wT, col_w, xnT, outT, T, CW, ps, dt_waits=None):
    NC = T // CW
    for it in range(ND):
        for c in range(NC):
            j = (it * NC + c) % 3
            P = ps.begin(j)
            dm = ps.done_marker(j)
            for dt in range(ND):
                if dt_waits is not None and it == 0 and c == 0:
                    em.wait("tensor", dt_waits[dt][0], dt_waits[dt][1])
                sp = (dt == ND - 1)
                em.op("tensor", lambda e, P=P, dt=dt, it=it, c=c, sp=sp: e.matmul(
                    P[:, :],
                    wT[:, dt * col_w + it * 128: dt * col_w + (it + 1) * 128],
                    xnT[:, dt * T + c * CW: dt * T + (c + 1) * CW],
                    start=(dt == 0), stop=sp), incs=[dm] if sp else ())
            ps.wait_full("scalar", j)
            em.op("scalar", lambda e, P=P, it=it, c=c: e.activation(
                outT[:, it * T + c * CW: it * T + (c + 1) * CW], P[:, :], AF.Copy),
                incs=[ps.free_marker(j)])


def emit_v_tokmajor(em, cst, wT, xnT, v_buf, T, dt_waits=None):
    NT = T // 128
    em.op("vector", lambda e: e.memset(v_buf[:, :], 1.0), incs=[em.inc("cs_t")], drain=True)
    em.wait("vector", em.sem("cs_t"), em.n("cs_t"))
    ps = PsumSlots(em, "psD1", 2, lambda n: em.ps(n, [128, 1024], F32))
    for nt in range(NT):
        j = nt % 2
        P = ps.begin(j)
        dm = ps.done_marker(j)
        for dt in range(ND):
            if dt_waits is not None and nt == 0:
                em.wait("tensor", dt_waits[dt][0], dt_waits[dt][1])
            for half in range(2):
                last = (dt == ND - 1 and half == 1)
                em.op("tensor", lambda e, P=P, dt=dt, nt=nt, half=half, last=last: e.matmul(
                    P[:, half * 512:(half + 1) * 512],
                    xnT[:, dt * T + nt * 128: dt * T + (nt + 1) * 128],
                    wT[:, dt * 1024 + half * 512: dt * 1024 + (half + 1) * 512],
                    start=(dt == 0), stop=(dt == ND - 1)),
                    incs=[dm] if last else ())
        ps.wait_full("vector", j)
        for h in range(NH):
            mk = [ps.free_marker(j)] if h == NH - 1 else ()
            em.op("vector", lambda e, P=P, nt=nt, h=h: e.tensor_copy(
                v_buf[:, nt * (NH * 65) + 65 * h: nt * (NH * 65) + 65 * h + 64],
                P[:, h * 64:(h + 1) * 64]), incs=mk)


def emit_qk_attention(em, cst, wq_src, wk_src, xnT, v_buf, attn_buf, T, CW):
    NKT = T // 128
    NC = T // CW
    WAVE = 8
    iters = []
    gid = 0
    for it in range(ND):
        for c in range(NC):
            nki = min((c * CW + CW + 127) // 128, NKT)
            for w0 in range(0, nki, WAVE):
                iters.append((it, c, w0, min(w0 + WAVE, nki), w0 + WAVE >= nki, gid))
            gid += 1
    M = len(iters)
    with em.scope():
        wq = em.sb("wqT", [128, ND * 1024], BF16)
        wk = em.sb("wkT", [128, ND * 1024], BF16)
        qt = [em.sb(f"qt{p}", [128, T], BF16) for p in range(2)]
        kt = [em.sb(f"kt{p}", [128, T], BF16) for p in range(2)]
        sqq = em.sb("cs_sq", [128, T], BF16)
        m2b = em.sb("cs_m2", [2, 2], F32)
        mxb = em.sb("cs_mx", [2, NC], F32)
        prodb = em.sb("cs_prod", [2, 1], F32)
        negc = em.sb("negc", [128, NH], F32)
        expb = [[em.sb(f"at_e{p}{hl}", [128, WAVE * CW], BF16) for hl in range(2)]
                for p in range(2)]
        rec = [[em.sb(f"at_rec{p}{hl}", [1, CW], F32) for hl in range(2)]
               for p in range(2)]
        recb = em.sb("at_recb", [128, CW], F32)
        ps3 = PsumSlots(em, "ps3", 3, lambda n: em.ps(n, [128, CW], F32))
        psp = PsumSlots(em, "psP4", 4, lambda n: em.ps(n, [128, CW], F32))
        psc = PsumSlots(em, "psC1", 1, lambda n: em.ps(n, [128, CW], F32))
        qk_dtw = []
        for dt in range(ND):
            em.op("sync", lambda e, dt=dt: e.dma_start_transpose(
                wq[:, dt * 1024:(dt + 1) * 1024], wq_src[0:D, dt * 128:(dt + 1) * 128]),
                incs=[em.inc(f"wtrd{dt}", 16)])
            qk_dtw.append((em.sem(f"wtrd{dt}"), em.n(f"wtrd{dt}")))
            em.op("sync", lambda e, dt=dt: e.dma_start_transpose(
                wk[:, dt * 1024:(dt + 1) * 1024], wk_src[0:D, dt * 128:(dt + 1) * 128]),
                incs=[em.inc("w_pl", 16)])
        qk_total = (em.sem("w_pl"), em.n("w_pl"))

        pv_cnt = [0, 0]
        bc_cnt = [0, 0]
        qk_cnt = [0, 0]
        exp_frs = {}
        mask_tgt = {}
        pv_meta = {}

        def emit_qk_cs(it):
            p = it % 2
            if qk_cnt[p] > 0:
                em.wait("vector", em.sem(f"qdone{p}"), qk_cnt[p])
            for which, wbuf, obuf in (("q", wq, qt[p]), ("k", wk, kt[p])):
                if which == "k" and it == 0:
                    em.wait("tensor", qk_total[0], qk_total[1])
                for c4 in range(NC):
                    j = (c4 + (0 if which == "q" else NC)) % 3
                    P = ps3.begin(j)
                    dm = ps3.done_marker(j)
                    for dt in range(ND):
                        if which == "q" and it == 0 and c4 == 0:
                            em.wait("tensor", qk_dtw[dt][0], qk_dtw[dt][1])
                        sp = (dt == ND - 1)
                        em.op("tensor", lambda e, P=P, wbuf=wbuf, dt=dt, it=it, c4=c4, sp=sp: e.matmul(
                            P[:, :],
                            wbuf[:, dt * 1024 + it * 128: dt * 1024 + (it + 1) * 128],
                            xnT[:, dt * T + c4 * CW: dt * T + (c4 + 1) * CW],
                            start=(dt == 0), stop=sp), incs=[dm] if sp else ())
                    ps3.wait_full("vector", j)
                    em.op("vector", lambda e, P=P, obuf=obuf, c4=c4: e.tensor_copy(
                        obuf[:, c4 * CW:(c4 + 1) * CW], P[:, :]),
                        incs=[ps3.free_marker(j), em.inc("cs_v")])
            for which, obuf, col0 in (("q", qt[p], 0), ("k", kt[p], 1)):
                em.wait("vector", em.sem("cs_v"), em.n("cs_v"))
                em.op("vector", lambda e, obuf=obuf: e.tensor_tensor(
                    sqq[:, :], obuf[:, :], obuf[:, :], op=ALU.mult),
                    incs=[em.inc("cs_v")], drain=True)
                em.wait("tensor", em.sem("cs_v"), em.n("cs_v"))
                for c4 in range(NC):
                    j = c4 % 3
                    P = ps3.begin(j)
                    em.op("tensor", lambda e, P=P, c4=c4: e.matmul(
                        P[0:2, :], cst["sel2_bf"][:, 0:2], sqq[:, c4 * CW:(c4 + 1) * CW],
                        start=True, stop=True), incs=[ps3.done_marker(j)])
                    ps3.wait_full("vector", j)
                    em.op("vector", lambda e, P=P, c4=c4: e.reduce_max(
                        mxb[0:2, c4:c4 + 1], P[0:2, :], axis=AX.X),
                        incs=[ps3.free_marker(j)], drain=True)
                em.op("vector", lambda e, col0=col0: e.reduce_max(
                    m2b[0:2, col0:col0 + 1], mxb[0:2, 0:NC], axis=AX.X),
                    incs=[em.inc("cs_v")], drain=True)
            em.wait("vector", em.sem("cs_v"), em.n("cs_v"))
            em.op("vector", lambda e: e.tensor_tensor(
                prodb[0:2, 0:1], m2b[0:2, 0:1], m2b[0:2, 1:2], op=ALU.mult),
                incs=[em.inc("cs_v")], drain=True)
            em.wait("scalar", em.sem("cs_v"), em.n("cs_v"))
            em.op("scalar", lambda e: e.activation(prodb[0:2, 0:1], prodb[0:2, 0:1], AF.Sqrt),
                  incs=[em.inc("cs_s")], drain=True)
            em.wait("tensor", em.sem("cs_s"), em.n("cs_s"))
            for hl in range(2):
                j = hl % 3
                P = ps3.begin(j)
                sel = cst["bc2a_f32"] if hl == 0 else cst["bc2b_f32"]
                em.op("tensor", lambda e, P=P, sel=sel: e.matmul(
                    P[:, 0:1], sel[0:2, :], prodb[0:2, 0:1],
                    start=True, stop=True), incs=[ps3.done_marker(j)])
                ps3.wait_full("vector", j)
                em.op("vector", lambda e, P=P, it=it, hl=hl: e.tensor_scalar(
                    negc[:, 2 * it + hl: 2 * it + hl + 1], P[:, 0:1], -0.125, None,
                    op0=ALU.mult), incs=[ps3.free_marker(j), em.inc("cs_v")])
            em.wait("scalar", em.sem("cs_v"), em.n("cs_v"))

        def stage_S(m):
            it, c, w0, w1, lastw, gid = iters[m]
            p = m % 2
            pit = it % 2
            sc_blocks = []
            for ki in range(w0, w1):
                blk = []
                for hl in range(2):
                    lo = 64 * hl
                    h = 2 * it + hl
                    j = (2 * (ki - w0) + hl) % 3
                    qo = ki * 128 - c * CW

                    def emit_sc(j=j, ki=ki, hl=hl, lo=lo, h=h, qo=qo, it=it, c=c,
                                p=p, pit=pit, m=m, last_of_pair=(lastw and c == NC - 1 and ki == w1 - 1 and hl == 1)):
                        P = ps3.begin(j)
                        mks = [ps3.done_marker(j)]
                        if last_of_pair:
                            qk_cnt[pit] += 1
                            mks.append((em.sem(f"qdone{pit}"), 1))
                        em.op("tensor", lambda e, P=P, lo=lo: e.matmul(
                            P[:, :],
                            kt[pit][lo:lo + 64, ki * 128:(ki + 1) * 128],
                            qt[pit][lo:lo + 64, c * CW:(c + 1) * CW],
                            start=True, stop=True, tile_position=(lo, 0)), incs=mks)
                        if ki == w0 and hl == 0 and pv_cnt[p] > 0:
                            em.wait("scalar", em.sem(f"pvd{p}"), pv_cnt[p])
                            em.wait("vector", em.sem(f"pvd{p}"), pv_cnt[p])
                        ps3.wait_full("scalar", j)
                        eb = expb[p][hl]
                        kk = ki - w0
                        if qo < 0:
                            em.op("scalar", lambda e, P=P: e.activation(
                                eb[:, kk * CW:(kk + 1) * CW], P[:, :], AF.Exp,
                                bias=negc[:, h:h + 1], scale=0.125),
                                incs=[ps3.free_marker(j)])
                        else:
                            em.op("scalar", lambda e, P=P: e.activation(
                                eb[:, kk * CW + qo:(kk + 1) * CW], P[:, qo:CW], AF.Exp,
                                bias=negc[:, h:h + 1], scale=0.125),
                                incs=[ps3.free_marker(j)])
                            if qo > 0:
                                em.op("vector", lambda e: e.memset(
                                    eb[:, kk * CW: kk * CW + qo], 0.0))
                            st3 = ps3.st
                            em.wait("vector", st3["fr"][j], st3["n_fr"][j])
                            em.op("vector", lambda e: e.tensor_tensor(
                                eb[:, kk * CW + qo: kk * CW + qo + 128],
                                eb[:, kk * CW + qo: kk * CW + qo + 128],
                                cst["trimask_bf"][:, :], op=ALU.mult), incs=[em.inc("at_m")])
                        blk.append((ps3.st["fr"][j], ps3.st["n_fr"][j]))
                    emit_sc()
                sc_blocks.append((blk, em.n("at_m")))
            exp_frs[m] = sc_blocks

        def stage_P(m):
            it, c, w0, w1, lastw, gid = iters[m]
            p = m % 2
            pg = gid % 2
            blocks = exp_frs.pop(m)
            slots = []
            for hl in range(2):
                sl = 2 * pg + hl
                if w0 == 0:
                    Ppv = psp.begin(sl)
                else:
                    Ppv = psp.bufs[sl]
                slots.append(Ppv)
            for bi, (blk, mtgt) in enumerate(blocks):
                ki = w0 + bi
                for hl in range(2):
                    h = 2 * it + hl
                    s_, t_ = blk[hl]
                    em.wait("tensor", s_, t_)
                    em.wait("tensor", em.sem("at_m"), mtgt)
                    Ppv = slots[hl]
                    sl = 2 * pg + hl
                    sp_ = (lastw and ki == w1 - 1)
                    mks = []
                    if sp_:
                        mks.append(psp.done_marker(sl))
                        if hl == 1:
                            pv_cnt[p] += 1
                            mks.append((em.sem(f"pvd{p}"), 1))
                    em.op("tensor", lambda e, Ppv=Ppv, ki=ki, h=h, w0=w0, hl=hl, st_=(ki == 0), sp_=sp_, p=p: e.matmul(
                        Ppv[0:65, :],
                        v_buf[:, ki * (NH * 65) + 65 * h: ki * (NH * 65) + 65 * h + 65],
                        expb[p][hl][:, (ki - w0) * CW:(ki - w0 + 1) * CW],
                        start=st_, stop=sp_), incs=mks)
            if not lastw:
                pv_cnt[p] += 1
                em.op("tensor", lambda e: e.drain(), incs=[(em.sem(f"pvd{p}"), 1)])

        def stage_B(m):
            it, c, w0, w1, lastw, gid = iters[m]
            if not lastw:
                return
            p = gid % 2
            for hl in range(2):
                sl = 2 * p + hl
                psp.wait_full("scalar", sl)
            if bc_cnt[p] > 0:
                em.wait("scalar", em.sem(f"bcd{p}"), bc_cnt[p])
            for hl in range(2):
                sl = 2 * p + hl
                Ppv = psp.bufs[sl]
                em.op("scalar", lambda e, Ppv=Ppv, p=p, hl=hl: e.activation(
                    rec[p][hl][0:1, :], Ppv[64:65, :], AF.Ln), drain=True)
                em.op("scalar", lambda e, p=p, hl=hl: e.activation(
                    rec[p][hl][0:1, :], rec[p][hl][0:1, :], AF.Exp, scale=-1.0),
                    incs=[em.inc("at_r")], drain=True)
            em.wait("tensor", em.sem("at_r"), em.n("at_r"))
            Pbc = psc.begin(0)
            bc_cnt[p] += 1
            em.op("tensor", lambda e, Pbc=Pbc, p=p: e.matmul(
                Pbc[0:64, :], cst["ones_1x64_f32"][0:1, :], rec[p][0][0:1, :],
                start=True, stop=True))
            em.op("tensor", lambda e, Pbc=Pbc, p=p: e.matmul(
                Pbc[64:128, :], cst["ones_1x64_f32"][0:1, :], rec[p][1][0:1, :],
                start=True, stop=True), incs=[psc.done_marker(0), (em.sem(f"bcd{p}"), 1)])
            psc.wait_full("vector", 0)
            em.op("vector", lambda e, Pbc=Pbc: e.tensor_copy(recb[:, :], Pbc[:, :]),
                  incs=[psc.free_marker(0)], drain=True)
            for hl in range(2):
                sl = 2 * p + hl
                lo = 64 * hl
                Ppv = psp.bufs[sl]
                em.op("vector", lambda e, Ppv=Ppv, it=it, lo=lo, c=c: e.tensor_tensor(
                    attn_buf[lo:lo + 64, it * T + c * CW: it * T + (c + 1) * CW],
                    Ppv[0:64, :], recb[lo:lo + 64, :], op=ALU.mult),
                    incs=[psp.free_marker(sl)], drain=(hl == 1))

        emit_qk_cs(0)
        for t in range(M + 2):
            if t < M:
                it, c, w0, _, _, _ = iters[t]
                if c == 0 and w0 == 0 and it + 1 < ND:
                    emit_qk_cs(it + 1)
                stage_S(t)
            if 1 <= t <= M:
                stage_P(t - 1)
            if 2 <= t <= M + 1:
                stage_B(t - 2)


def emit_out_proj(em, cst, attn_buf, wo_aps, s_dram, T, dt_waits=None):
    NT = T // 128
    ps = PsumSlots(em, "psD1", 2, lambda n: em.ps(n, [128, 1024], F32))
    sout = DmaSlots(em, "wo_s", 2, lambda n: em.sb(n, [128, D], BF16))
    for nt in range(NT):
        j = nt % 2
        P = ps.begin(j)
        dm = ps.done_marker(j)
        for dt in range(ND):
            if dt_waits is not None and nt == 0:
                em.wait("tensor", dt_waits[dt][0], dt_waits[dt][1])
            for half in range(2):
                last = (dt == ND - 1 and half == 1)
                em.op("tensor", lambda e, P=P, dt=dt, half=half, nt=nt, last=last: e.matmul(
                    P[:, half * 512:(half + 1) * 512],
                    attn_buf[:, dt * T + nt * 128: dt * T + (nt + 1) * 128],
                    wo_aps[dt][:, half * 512:(half + 1) * 512],
                    start=(dt == 0), stop=(dt == ND - 1)), incs=[dm] if last else ())
        ps.wait_full("scalar", j)
        st = sout.st
        if st["n_ld"][j] > st["n_fr"][j]:
            em.wait("scalar", st["ld"][j], 16 * st["n_ld"][j])
            st["n_fr"][j] += 1
            em.op("scalar", lambda e: e.nop(), incs=[(st["fr"][j], 1)], drain=False)
        B = sout.bufs[j]
        em.op("scalar", lambda e, P=P, B=B: e.activation(B[:, :], P[:, :], AF.Copy),
              incs=[ps.free_marker(j)])
        em.wait("scalar", ps.st["fr"][j], ps.st["n_fr"][j])
        st["n_ld"][j] += 1
        em.op("scalar", lambda e, B=B, nt=nt: e.dma_start(
            out=s_dram[nt * 128:(nt + 1) * 128, :], in_=B[:, :]), incs=[(st["ld"][j], 16)])
    for j in range(2):
        st = sout.st
        while st["n_fr"][j] < st["n_ld"][j]:
            em.wait("scalar", st["ld"][j], 16 * st["n_ld"][j])
            st["n_fr"][j] += 1
            em.op("scalar", lambda e: e.nop(), incs=[(st["fr"][j], 1)], drain=False)


def emit_gates(em, cst, xnT, wg_bf, bg_bf, SEL, selT, T):
    NT = T // 128
    pg = PsumSlots(em, "psG", 2, lambda n: em.ps(n, [128, 8], F32))
    pt = PsumSlots(em, "psN", 2, lambda n: em.ps(n, [8, 128], F32))
    G = em.sb("gt_G", [128, 8], F32)
    Pr = em.sb("gt_P", [128, 8], F32)
    m1 = em.sb("gt_m1", [128, 1], F32)
    m1n = em.sb("gt_m1n", [128, 1], F32)
    eq = em.sb("gt_eq", [128, 8], F32)
    gm = em.sb("gt_gm", [128, 8], F32)
    m2 = em.sb("gt_m2", [128, 1], F32)
    msk = em.sb("gt_msk", [128, 8], F32)
    ssum = em.sb("gt_ss", [128, 1], F32)
    rden = em.sb("gt_rd", [128, 1], F32)
    for nt in range(NT):
        j = nt % 2
        P = pg.begin(j)
        for dt in range(ND):
            em.op("tensor", lambda e, P=P, dt=dt, nt=nt: e.matmul(
                P[:, :], xnT[:, dt * T + nt * 128: dt * T + (nt + 1) * 128],
                wg_bf[:, dt * 8:(dt + 1) * 8], start=(dt == 0), stop=False))
        em.op("tensor", lambda e, P=P: e.matmul(
            P[:, :], cst["ones_1x128_bf"][0:1, :], bg_bf[0:1, :], start=False, stop=True),
            incs=[pg.done_marker(j)])
        pg.wait_full("vector", j)
        if nt > 0:
            em.wait("vector", em.sem("cs_s"), em.n("cs_s"))
        em.op("vector", lambda e, P=P: e.tensor_copy(G[:, :], P[:, :]), incs=[pg.free_marker(j)], drain=True)
        em.op("vector", lambda e: e.reduce_max(m1n[:, :], G[:, :], axis=AX.X, negate=True), drain=True)
        em.op("vector", lambda e: e.reduce_max(m1[:, :], G[:, :], axis=AX.X), incs=[em.inc("cs_v")], drain=True)
        em.wait("scalar", em.sem("cs_v"), em.n("cs_v"))
        em.op("scalar", lambda e: e.activation(Pr[:, :], G[:, :], AF.Exp, bias=m1n[:, 0:1]),
              incs=[em.inc("cs_s")])
        em.wait("vector", em.sem("cs_s"), em.n("cs_s"))
        em.op("vector", lambda e: e.tensor_scalar(eq[:, :], G[:, :], m1[:, 0:1], None, op0=ALU.is_ge), drain=True)
        em.op("vector", lambda e: e.scalar_tensor_tensor(
            gm[:, :], eq[:, :], -1e30, G[:, :], op0=ALU.mult, op1=ALU.add), drain=True)
        em.op("vector", lambda e: e.reduce_max(m2[:, :], gm[:, :], axis=AX.X), drain=True)
        em.op("vector", lambda e: e.tensor_scalar(msk[:, :], G[:, :], m2[:, 0:1], None, op0=ALU.is_ge), drain=True)
        em.op("vector", lambda e: e.tensor_tensor(msk[:, :], msk[:, :], Pr[:, :], op=ALU.mult), drain=True)
        em.op("vector", lambda e: e.reduce_sum(ssum[:, :], msk[:, :], axis=AX.X), incs=[em.inc("cs_v")], drain=True)
        em.op("vector", lambda e: e.tensor_scalar(ssum[:, :], ssum[:, :], 1e-8, None, op0=ALU.add), drain=True)
        em.op("vector", lambda e: e.reciprocal(rden[:, :], ssum[:, :]), drain=True)
        em.op("vector", lambda e, nt=nt: e.tensor_scalar(
            SEL[:, nt * 8:(nt + 1) * 8], msk[:, :], rden[:, 0:1], None, op0=ALU.mult),
            incs=[em.inc("cs_v")], drain=True)
        em.wait("tensor", em.sem("cs_v"), em.n("cs_v"))
        Pt = pt.begin(j)
        em.op("tensor", lambda e, Pt=Pt, nt=nt: e.transpose(
            Pt[:, :], SEL[:, nt * 8:(nt + 1) * 8], cst["ident_f32"][:, :]),
            incs=[pt.done_marker(j)])
        pt.wait_full("vector", j)
        em.op("vector", lambda e, Pt=Pt, nt=nt: e.tensor_copy(
            selT[0:8, nt * 128:(nt + 1) * 128], Pt[0:8, :]), incs=[pt.free_marker(j)])


def emit_b1_prep(em, cst, w, b1_sb):
    HT = HMOE // 128
    with em.scope():
        b1n = em.sb("b1n", [8, HMOE], F32)
        pb = PsumSlots(em, "psG", 2, lambda n: em.ps(n, [128, 8], F32))
        em.op("sync", lambda e: e.dma_start(out=b1n[:, :], in_=w["b1"]),
              incs=[em.inc("w_pl", 16)])
        em.wait("tensor", em.sem("w_pl"), em.n("w_pl"))
        for t in range(HT):
            j = t % 2
            P = pb.begin(j)
            em.op("tensor", lambda e, P=P, t=t: e.transpose(
                P[:, :], b1n[0:8, t * 128:(t + 1) * 128], cst["ident_f32"][0:8, 0:8]),
                incs=[pb.done_marker(j)])
            pb.wait_full("vector", j)
            em.op("vector", lambda e, P=P, t=t: e.tensor_copy(
                b1_sb[:, t * 8:(t + 1) * 8], P[:, :]), incs=[pb.free_marker(j)])
        em.barrier(("vector",), ("tensor", "vector", "scalar"))


def _w1_src(w1_ap, e_, ht):
    r = w1_ap[e_].rearrange("(f p) h -> p f h", p=128)
    return r[:, :, ht * 128:(ht + 1) * 128]


def _w2_src(w2_ap, e_, grp):
    r = w2_ap[e_].rearrange("(g r p) f -> p g r f", p=128, g=4)
    return r[:, grp, :, :]


def emit_moe(em, cst, w, xnT, SEL, selT, b1_sb, b2_bf, g_acc, T, w1s, w2s):
    NT = T // 128
    CH = min(1024, T)
    NCH = T // CH
    NTL = CH // 128
    HT = HMOE // 128
    NHALF = CH // 512 if CH >= 512 else 1
    MOVW = min(512, CH)
    hT = em.sb("moe_hT", [128, HT * CH], BF16)
    ph1 = PsumSlots(em, "psD1", 2, lambda n: em.ps(n, [128, CH], F32))
    ph2 = PsumSlots(em, "psD2", 2, lambda n: em.ps(n, [128, 1024], F32))
    for cc in range(NCH):
        for e_ in range(E):
            if em.n("moe_b") > 0:
                em.wait("scalar", em.sem("moe_b"), em.n("moe_b"))
            h_ready = {}
            for ht in range(HT):
                i = ht % 2
                if cc == 0 and e_ == 0 and ht < 2:
                    W1b = w1s.bufs[i]
                else:
                    W1b = w1s.load(
                        "gpsimd", i,
                        lambda b: b[:, :].rearrange("p (f c) -> p f c", f=8),
                        lambda b, e_=e_, ht=ht: _w1_src(w["w1"], e_, ht))
                if ht in (2, 4):
                    grp0 = 0 if ht == 2 else 1
                    if not (cc == 0 and e_ == 0 and grp0 == 0):
                        w2s.load(
                            "gpsimd", grp0,
                            lambda b: b[:, :].rearrange("p (r f) -> p r f", r=8),
                            lambda b, e_=e_, grp0=grp0: _w2_src(w["w2"], e_, grp0))
                w1s.wait_loaded("tensor", i)
                j = ht % 2
                P = ph1.begin(j)
                dm = ph1.done_marker(j)
                fw = w1s.free_marker(i)
                for f in range(ND):
                    for half in range(NHALF):
                        last = (f == ND - 1 and half == NHALF - 1)
                        em.op("tensor", lambda e, P=P, W1b=W1b, f=f, half=half, cc=cc, last=last: e.matmul(
                            P[:, half * MOVW:(half + 1) * MOVW],
                            W1b[:, f * 128:(f + 1) * 128],
                            xnT[:, f * T + cc * CH + half * MOVW: f * T + cc * CH + (half + 1) * MOVW],
                            start=(f == 0), stop=(f == ND - 1)),
                            incs=[dm, fw] if last else ())
                ph1.wait_full("scalar", j)
                em.op("scalar", lambda e, P=P, ht=ht, e_=e_: e.activation(
                    hT[:, ht * CH:(ht + 1) * CH], P[:, :], AF.Gelu_apprx_tanh,
                    bias=b1_sb[:, ht * E + e_: ht * E + e_ + 1]),
                    incs=[ph1.free_marker(j)])
                h_ready[ht] = [(ph1.st["fr"][jj], ph1.st["n_fr"][jj]) for jj in range(2)]
            for grp in range(4):
                i = grp % 2
                for s_, t_ in h_ready[8 * grp + 7]:
                    em.wait("tensor", s_, t_)
                w2s.wait_loaded("tensor", i)
                W2b = w2s.bufs[i]
                fw2 = w2s.free_marker(i)
                em.dr("vector")
                for ntl in range(NTL):
                    nt = cc * NTL + ntl
                    j = ntl % 2
                    P = ph2.begin(j)
                    dm = ph2.done_marker(j)
                    for r in range(8):
                        hh = grp * 8 + r
                        for half in range(2):
                            last = (r == 7 and half == 1)
                            mks = [dm] if last else []
                            if last and ntl == NTL - 1:
                                mks.append(fw2)
                            if last and ntl == NTL - 1 and grp == 3:
                                mks.append(em.inc("moe_b"))
                            em.op("tensor", lambda e, P=P, W2b=W2b, hh=hh, r=r, half=half, ntl=ntl: e.matmul(
                                P[:, half * 512:(half + 1) * 512],
                                hT[:, hh * CH + ntl * 128: hh * CH + (ntl + 1) * 128],
                                W2b[:, r * 1024 + half * 512: r * 1024 + (half + 1) * 512],
                                start=(r == 0), stop=(r == 7)), incs=mks)
                    ph2.wait_full("vector", j)
                    if ntl == NTL - 1 and grp + 2 < 4:
                        w2s.load(
                            "gpsimd", i,
                            lambda b: b[:, :].rearrange("p (r f) -> p r f", r=8),
                            lambda b, e_=e_, g2=grp + 2: _w2_src(w["w2"], e_, g2))
                    first = (e_ == 0 and grp == 0)
                    for half in range(2):
                        mk = [ph2.free_marker(j)] if half == 1 else ()
                        if first:
                            em.op("vector", lambda e, P=P, nt=nt, e_=e_, half=half: e.tensor_scalar(
                                g_acc[:, nt * 1024 + half * 512: nt * 1024 + (half + 1) * 512],
                                P[:, half * 512:(half + 1) * 512],
                                SEL[:, nt * 8 + e_: nt * 8 + e_ + 1], None, op0=ALU.mult), incs=mk)
                        else:
                            em.op("vector", lambda e, P=P, nt=nt, e_=e_, half=half: e.scalar_tensor_tensor(
                                g_acc[:, nt * 1024 + half * 512: nt * 1024 + (half + 1) * 512],
                                P[:, half * 512:(half + 1) * 512],
                                SEL[:, nt * 8 + e_: nt * 8 + e_ + 1],
                                g_acc[:, nt * 1024 + half * 512: nt * 1024 + (half + 1) * 512],
                                op0=ALU.mult, op1=ALU.add), incs=mk)
    em.dr("vector")
    g_ready = []
    for nt in range(NT):
        j = nt % 2
        P = ph2.begin(j)
        dm = ph2.done_marker(j)
        for half in range(2):
            em.op("tensor", lambda e, P=P, nt=nt, half=half: e.matmul(
                P[:, half * 512:(half + 1) * 512],
                selT[0:8, nt * 128:(nt + 1) * 128],
                b2_bf[0:8, half * 512:(half + 1) * 512],
                start=True, stop=True), incs=[dm] if half == 1 else ())
        ph2.wait_full("vector", j)
        em.op("vector", lambda e, P=P, nt=nt: e.tensor_tensor(
            g_acc[:, nt * 1024:(nt + 1) * 1024], P[:, :],
            g_acc[:, nt * 1024:(nt + 1) * 1024], op=ALU.add), incs=[ph2.free_marker(j)])
        g_ready.append((ph2.st["fr"][j], ph2.st["n_fr"][j]))
    return g_ready


def emit_l1ffn(em, cst, xnT, T, bf_dram, x_res, s_dram, x_out):
    HT1 = H1 // 128
    TH = min(1024, T)
    NHALVES = T // TH
    NTH = TH // 128
    CL = min(1024, TH)
    NCL = TH // CL
    MW = min(512, CL)
    NHF = max(1, CL // 512)
    NTr = T // 128
    with em.scope():
        gs = em.sb("l1_gs", [128, HT1 * TH], BF16)
        wbuf = [em.sb(f"l1_w{b}", [128, HT1 * D], BF16) for b in range(2)]
        pp = PsumSlots(em, "psD1", 2, lambda n: em.ps(n, [128, 1024], F32))
        xs = DmaSlots(em, "lnx", 2, lambda n: em.sb(n, [128, D], F32))
        ss = DmaSlots(em, "wo_s", 2, lambda n: em.sb(n, [128, D], BF16))
        outs = DmaSlots(em, "ro", 2, lambda n: em.sb(n, [128, D], F32))
        gb = [em.sb(f"l1_gb{p}", [128, D], BF16) for p in range(2)]
        ost = outs.st
        rct = {}

        def emit_store(nt):
            i2 = nt % 2
            em.wait("scalar", rct[nt][0], rct[nt][1])
            O = outs.bufs[i2]
            ost["n_ld"][i2] += 1
            em.op("scalar", lambda e, O=O, nt=nt: e.dma_start(
                out=x_out[nt * 128:(nt + 1) * 128, :], in_=O[:, :]),
                incs=[(ost["ld"][i2], 16)])
        phases = []
        for th in range(NHALVES):
            phases += [("g", th), ("u", th), ("d", th)]
        g_done = {}
        u_done = {}
        for k, (phz, th) in enumerate(phases):
            b = k % 2
            wT = wbuf[b]
            t0 = th * TH
            if k >= 2:
                em.wait("sync", em.sem(f"l1wf{b}"), em.n(f"l1wf{b}"))
            dtw = None
            if phz in ("g", "u"):
                wname = "fg" if phz == "g" else "fu"
                if k == 0:
                    dtw = []
                for dt in range(ND):
                    semn = f"wtrd{dt}" if k == 0 else f"l1ld{b}"
                    em.op("sync", lambda e, wT=wT, dt=dt, wname=wname: e.dma_start_transpose(
                        wT[:, dt * H1:(dt + 1) * H1],
                        bf_dram[wname][0:H1, dt * 128:(dt + 1) * 128]),
                        incs=[em.inc(semn, 16)])
                    if k == 0:
                        dtw.append((em.sem(semn), em.n(semn)))
            else:
                for ht in range(HT1):
                    em.op("sync", lambda e, wT=wT, ht=ht: e.dma_start_transpose(
                        wT[:, ht * D:(ht + 1) * D],
                        bf_dram["fd"][0:D, ht * 128:(ht + 1) * 128]),
                        incs=[em.inc(f"l1ld{b}", 16)])
            if dtw is None:
                em.wait("tensor", em.sem(f"l1ld{b}"), em.n(f"l1ld{b}"))

            if phz in ("g", "u"):
                if phz == "g" and em.n("l1d") > 0:
                    em.wait("scalar", em.sem("l1d"), em.n("l1d"))
                for ht in range(HT1):
                    for cc in range(NCL):
                        j = (ht * NCL + cc) % 2
                        P = pp.begin(j)
                        dm = pp.done_marker(j)
                        for dt in range(ND):
                            if dtw is not None and ht == 0 and cc == 0:
                                em.wait("tensor", dtw[dt][0], dtw[dt][1])
                            for half in range(NHF):
                                last = (dt == ND - 1 and half == NHF - 1)
                                mks = [dm] if last else []
                                if last and ht == HT1 - 1 and cc == NCL - 1:
                                    mks.append(em.inc(f"l1wf{b}"))
                                em.op("tensor", lambda e, P=P, wT=wT, dt=dt, ht=ht, cc=cc, half=half, t0=t0: e.matmul(
                                    P[:, half * MW:(half + 1) * MW],
                                    wT[:, dt * H1 + ht * 128: dt * H1 + (ht + 1) * 128],
                                    xnT[:, dt * T + t0 + cc * CL + half * MW: dt * T + t0 + cc * CL + (half + 1) * MW],
                                    start=(dt == 0), stop=(dt == ND - 1)), incs=mks)
                        if phz == "g":
                            pp.wait_full("scalar", j)
                            em.op("scalar", lambda e, P=P, ht=ht, cc=cc: e.activation(
                                gs[:, ht * TH + cc * CL: ht * TH + (cc + 1) * CL], P[:, 0:CL],
                                AF.Silu, scale=QI), incs=[pp.free_marker(j)])
                            g_done[ht] = (pp.st["fr"][j], pp.st["n_fr"][j])
                        else:
                            pp.wait_full("vector", j)
                            s_, t_ = g_done[ht]
                            em.wait("vector", s_, t_)
                            em.op("vector", lambda e, P=P, ht=ht, cc=cc: e.tensor_tensor(
                                gs[:, ht * TH + cc * CL: ht * TH + (cc + 1) * CL], P[:, 0:CL],
                                gs[:, ht * TH + cc * CL: ht * TH + (cc + 1) * CL], op=ALU.mult),
                                incs=[pp.free_marker(j)])
                            u_done[ht] = (pp.st["fr"][j], pp.st["n_fr"][j])
            else:
                for ntl in range(NTH):
                    nt = th * NTH + ntl
                    j = ntl % 2
                    P = pp.begin(j)
                    dm = pp.done_marker(j)
                    for ht in range(HT1):
                        if ntl == 0:
                            s_, t_ = u_done[ht]
                            em.wait("tensor", s_, t_)
                        for half in range(2):
                            last = (ht == HT1 - 1 and half == 1)
                            mks = [dm] if last else []
                            if last and ntl == NTH - 1:
                                mks.append(em.inc(f"l1wf{b}"))
                                mks.append(em.inc("l1d"))
                            em.op("tensor", lambda e, P=P, wT=wT, ht=ht, ntl=ntl, half=half: e.matmul(
                                P[:, half * 512:(half + 1) * 512],
                                gs[:, ht * TH + ntl * 128: ht * TH + (ntl + 1) * 128],
                                wT[:, ht * D + half * 512: ht * D + (half + 1) * 512],
                                start=(ht == 0), stop=(ht == HT1 - 1)), incs=mks)
                    i2 = nt % 2
                    X = xs.load("sync", i2, lambda b: b[:, :],
                                lambda b, nt=nt: x_res[nt * 128:(nt + 1) * 128, :])
                    S = ss.load("sync", i2, lambda b: b[:, :],
                                lambda b, nt=nt: s_dram[nt * 128:(nt + 1) * 128, :])
                    if nt >= 1:
                        emit_store(nt - 1)
                    if ost["n_ld"][i2] > ost["n_fr"][i2]:
                        em.wait("scalar", ost["ld"][i2], 16 * ost["n_ld"][i2])
                        ost["n_fr"][i2] += 1
                        em.op("scalar", lambda e: e.nop(), incs=[(ost["fr"][i2], 1)], drain=False)
                    pp.wait_full("scalar", j)
                    if nt >= 2:
                        em.wait("scalar", rct[nt - 2][0], rct[nt - 2][1])
                    em.op("scalar", lambda e, P=P, B=gb[i2]: e.activation(
                        B[:, :], P[:, :], AF.Silu, scale=QI * QI), incs=[pp.free_marker(j)])
                    b_rdy = (pp.st["fr"][j], pp.st["n_fr"][j])
                    em.wait("vector", b_rdy[0], b_rdy[1])
                    ss.wait_loaded("vector", i2)
                    if ost["n_fr"][i2] > 0:
                        em.wait("vector", ost["fr"][i2], ost["n_fr"][i2])
                    O = outs.bufs[i2]
                    em.op("vector", lambda e, S=S, O=O, p=i2: e.tensor_tensor(
                        O[:, :], S[:, :], gb[p][:, :], op=ALU.mult),
                        incs=[ss.free_marker(i2)], drain=True)
                    xs.wait_loaded("vector", i2)
                    em.op("vector", lambda e, X=X, O=O: e.tensor_tensor(
                        O[:, :], X[:, :], O[:, :], op=ALU.add),
                        incs=[xs.free_marker(i2), em.inc("cs_t")], drain=True)
                    rct[nt] = (em.sem("cs_t"), em.n("cs_t"))
        emit_store(NTr - 1)
        for i2 in range(2):
            while ost["n_fr"][i2] < ost["n_ld"][i2]:
                em.wait("scalar", ost["ld"][i2], 16 * ost["n_ld"][i2])
                ost["n_fr"][i2] += 1
                em.op("scalar", lambda e: e.nop(), incs=[(ost["fr"][i2], 1)], drain=False)
        em.barrier(("tensor", "scalar", "vector", "sync"), ENGS)


def emit_boundary(em, cst, x_prev, s_dram, g_dram, g_tgts, x_mid, xnT, T, g_needs_silu):
    NT = T // 128
    with em.scope():
        xs = DmaSlots(em, "lnx", 2, lambda n: em.sb(n, [128, D], F32))
        ss = DmaSlots(em, "wo_s", 2, lambda n: em.sb(n, [128, D], BF16))
        gg = DmaSlots(em, "rg", 2, lambda n: em.sb(n, [128, D], BF16))
        outs = DmaSlots(em, "ro", 2, lambda n: em.sb(n, [128, D], F32))
        pt = PsumSlots(em, "psT", 2, lambda n: em.ps(n, [128, 128], BF16))
        gsil = [em.sb(f"b_gsil{p}", [128, D], F32) for p in range(2)]
        tmp = [em.sb(f"b_tmp{p}", [128, D], F32) for p in range(2)]
        xc = [em.sb(f"b_xc{p}", [128, D], F32) for p in range(2)]
        xnt = [em.sb(f"b_xnt{p}", [128, D], BF16) for p in range(2)]
        sq = em.sb("b_sq", [128, D], F32)
        mean = [em.sb(f"b_mean{p}", [128, 1], F32) for p in range(2)]
        var = [em.sb(f"b_var{p}", [128, 1], F32) for p in range(2)]
        rstd = [em.sb(f"b_rstd{p}", [128, 1], F32) for p in range(2)]
        ost = outs.st
        ct = {}
        cssn = {}

        def emit_silu(k):
            i = k % 2
            if k >= 2:
                em.wait("scalar", ct[k - 2][0], ct[k - 2][1])
            if g_tgts is not None:
                em.wait("sync", g_tgts[k][0], g_tgts[k][1])
            G = gg.load("sync", i, lambda b: b[:, :],
                        lambda b, k=k: g_dram[k * 128:(k + 1) * 128, :])
            gg.wait_loaded("scalar", i)
            em.op("scalar", lambda e, G=G, p=i: e.activation(gsil[p][:, :], G[:, :], AF.Silu),
                  incs=[gg.free_marker(i), em.inc("cs_s")])
            cssn[k] = (em.sem("cs_s"), em.n("cs_s"))

        def stage_a(nt):
            i = nt % 2
            if ost["n_ld"][i] > ost["n_fr"][i]:
                em.wait("scalar", ost["ld"][i], 16 * ost["n_ld"][i])
                ost["n_fr"][i] += 1
                em.op("scalar", lambda e: e.nop(), incs=[(ost["fr"][i], 1)], drain=False)
            X = xs.load("sync", i, lambda b: b[:, :],
                        lambda b, nt=nt: x_prev[nt * 128:(nt + 1) * 128, :])
            S = ss.load("sync", i, lambda b: b[:, :],
                        lambda b, nt=nt: s_dram[nt * 128:(nt + 1) * 128, :])
            if g_needs_silu:
                if nt == 0:
                    emit_silu(0)
                if nt + 1 < NT:
                    emit_silu(nt + 1)
                em.wait("vector", cssn[nt][0], cssn[nt][1])
                ss.wait_loaded("vector", i)
                em.op("vector", lambda e, S=S, p=i: e.tensor_tensor(
                    tmp[p][:, :], S[:, :], gsil[p][:, :], op=ALU.mult),
                    incs=[ss.free_marker(i)], drain=True)
            else:
                if g_tgts is not None:
                    em.wait("sync", g_tgts[nt][0], g_tgts[nt][1])
                G = gg.load("sync", i, lambda b: b[:, :],
                            lambda b, nt=nt: g_dram[nt * 128:(nt + 1) * 128, :])
                ss.wait_loaded("vector", i)
                gg.wait_loaded("vector", i)
                em.op("vector", lambda e, S=S, G=G, p=i: e.tensor_tensor(
                    tmp[p][:, :], S[:, :], G[:, :], op=ALU.mult),
                    incs=[ss.free_marker(i), gg.free_marker(i)], drain=True)
            xs.wait_loaded("vector", i)
            if ost["n_fr"][i] > 0:
                em.wait("vector", ost["fr"][i], ost["n_fr"][i])
            O = outs.bufs[i]
            em.op("vector", lambda e, X=X, O=O, p=i: e.tensor_tensor(
                O[:, :], X[:, :], tmp[p][:, :], op=ALU.add),
                incs=[xs.free_marker(i), em.inc("cs_t")], drain=True)
            ct[nt] = (em.sem("cs_t"), em.n("cs_t"))

        def stage_b(nt):
            i = nt % 2
            O = outs.bufs[i]
            em.wait("scalar", ct[nt][0], ct[nt][1])
            ost["n_ld"][i] += 1
            em.op("scalar", lambda e, O=O, nt=nt: e.dma_start(
                out=x_mid[nt * 128:(nt + 1) * 128, :], in_=O[:, :]), incs=[(ost["ld"][i], 16)])
            em.op("vector", lambda e, O=O, p=i: e.reduce_sum(mean[p][:, :], O[:, :], axis=AX.X),
                  drain=True)
            em.op("vector", lambda e, p=i: e.tensor_scalar(
                mean[p][:, :], mean[p][:, :], 1.0 / D, None, op0=ALU.mult), drain=True)
            if nt >= 2:
                em.wait("vector", em.sem(f"lnxn{i}"), em.n(f"lnxn{i}"))
            em.op("vector", lambda e, O=O, p=i: e.tensor_scalar(
                xc[p][:, :], O[:, :], mean[p][:, 0:1], None, op0=ALU.subtract),
                incs=[em.inc(f"lnv{i}")], drain=True)

        def stage_c(nt):
            i = nt % 2
            em.wait("scalar", em.sem(f"lnv{i}"), em.n(f"lnv{i}"))
            if nt >= 2:
                em.wait("scalar", em.sem(f"lnt{i}"), em.n(f"lnt{i}"))
            em.op("scalar", lambda e, p=i: e.activation(
                sq[:, :], xc[p][:, :], AF.Square, accum_out=var[p][:, :]), drain=True)
            em.op("scalar", lambda e, p=i: e.activation(
                var[p][:, :], var[p][:, :], AF.Ln, bias=cst["epsb"][:, 0:1], scale=1.0 / D),
                drain=True)
            em.op("scalar", lambda e, p=i: e.activation(
                rstd[p][:, :], var[p][:, :], AF.Exp, scale=-0.5), drain=True)
            em.op("scalar", lambda e, p=i: e.activation(
                xnt[p][:, :], xc[p][:, :], AF.Copy, scale=rstd[p][:, 0:1]),
                incs=[em.inc(f"lnxn{i}")], drain=True)
            em.wait("tensor", em.sem(f"lnxn{i}"), em.n(f"lnxn{i}"))
            for dt in range(ND):
                j = dt % 2
                P = pt.begin(j)
                last = (dt == ND - 1)
                em.op("tensor", lambda e, P=P, dt=dt, p=i: e.transpose(
                    P[:, :], xnt[p][:, dt * 128:(dt + 1) * 128], cst["ident_bf"][:, :]),
                    incs=[pt.done_marker(j), em.inc(f"lnt{i}") if last else None])
                eng = "vector" if j == 0 else "scalar"
                pt.wait_full(eng, j)
                if eng == "vector":
                    em.op(eng, lambda e, P=P, dt=dt, nt=nt: e.tensor_copy(
                        xnT[:, dt * T + nt * 128: dt * T + (nt + 1) * 128], P[:, :]),
                        incs=[pt.free_marker(j)])
                else:
                    em.op(eng, lambda e, P=P, dt=dt, nt=nt: e.activation(
                        xnT[:, dt * T + nt * 128: dt * T + (nt + 1) * 128], P[:, :], AF.Copy),
                        incs=[pt.free_marker(j)])

        for it in range(NT + 2):
            if it < NT:
                stage_a(it)
            if 1 <= it <= NT:
                stage_b(it - 1)
            if 2 <= it <= NT + 1:
                stage_c(it - 2)
        for i in range(2):
            while ost["n_fr"][i] < ost["n_ld"][i]:
                em.wait("scalar", ost["ld"][i], 16 * ost["n_ld"][i])
                ost["n_fr"][i] += 1
                em.op("scalar", lambda e: e.nop(), incs=[(ost["fr"][i], 1)], drain=False)


def emit_residual(em, cst, x_ap, s_dram, g_buf, g_dram, x_out_ap, T, g_needs_silu):
    NT = T // 128
    xs = DmaSlots(em, "lnx", 2, lambda n: em.sb(n, [128, D], F32))
    ss = DmaSlots(em, "wo_s", 2, lambda n: em.sb(n, [128, D], BF16))
    outs = DmaSlots(em, "ro", 2, lambda n: em.sb(n, [128, D], F32))
    gg = DmaSlots(em, "rg", 2, lambda n: em.sb(n, [128, D], BF16)) if g_dram is not None else None
    gsil = em.sb("r_gsil", [128, D], F32)
    tmp = em.sb("r_tmp", [128, D], F32)
    ost = outs.st
    res_done = []
    for nt in range(NT):
        i = nt % 2
        if ost["n_ld"][i] > 0:
            em.wait("scalar", ost["ld"][i], 16 * ost["n_ld"][i])
            ost["n_fr"][i] += 1
            em.op("scalar", lambda e: e.nop(), incs=[(ost["fr"][i], 1)], drain=False)
        X = xs.load("sync", i, lambda b: b[:, :],
                    lambda b, nt=nt: x_ap[nt * 128:(nt + 1) * 128, :])
        S = ss.load("sync", i, lambda b: b[:, :],
                    lambda b, nt=nt: s_dram[nt * 128:(nt + 1) * 128, :])
        if gg is not None:
            G = gg.load("sync", i, lambda b: b[:, :],
                        lambda b, nt=nt: g_dram[nt * 128:(nt + 1) * 128, :])
        if g_needs_silu:
            if nt > 0:
                em.wait("scalar", em.sem("cs_v"), em.n("cs_v"))
            em.op("scalar", lambda e, nt=nt: e.activation(
                gsil[:, :], g_buf[:, nt * D:(nt + 1) * D], AF.Silu), incs=[em.inc("cs_s")])
            em.wait("vector", em.sem("cs_s"), em.n("cs_s"))
        ss.wait_loaded("vector", i)
        if g_needs_silu:
            em.op("vector", lambda e, S=S: e.tensor_tensor(
                tmp[:, :], S[:, :], gsil[:, :], op=ALU.mult),
                incs=[ss.free_marker(i), em.inc("cs_v")], drain=True)
        elif gg is not None:
            gg.wait_loaded("vector", i)
            em.op("vector", lambda e, S=S, G=G: e.tensor_tensor(
                tmp[:, :], S[:, :], G[:, :], op=ALU.mult),
                incs=[ss.free_marker(i), gg.free_marker(i)], drain=True)
        else:
            em.op("vector", lambda e, S=S, nt=nt: e.tensor_tensor(
                tmp[:, :], S[:, :], g_buf[:, nt * D:(nt + 1) * D], op=ALU.mult),
                incs=[ss.free_marker(i)], drain=True)
        xs.wait_loaded("vector", i)
        if ost["n_ld"][i] > 0:
            em.wait("vector", ost["fr"][i], ost["n_fr"][i])
        O = outs.bufs[i]
        em.op("vector", lambda e, X=X, O=O: e.tensor_tensor(
            O[:, :], X[:, :], tmp[:, :], op=ALU.add),
            incs=[xs.free_marker(i), em.inc("cs_t")], drain=True)
        em.wait("scalar", em.sem("cs_t"), em.n("cs_t"))
        ost["n_ld"][i] += 1
        em.op("scalar", lambda e, O=O, nt=nt: e.dma_start(
            out=x_out_ap[nt * 128:(nt + 1) * 128, :], in_=O[:, :]), incs=[(ost["ld"][i], 16)])
        res_done.append((ost["ld"][i], 16 * ost["n_ld"][i]))
    for i in range(2):
        em.wait("scalar", ost["ld"][i], 16 * ost["n_ld"][i])
    return res_done



def emit_weight_casts(em, w_all, layers, phase="all", prep=None):
    nc = em.nc
    prep = prep if prep is not None else {}
    for lay in layers:
        w = w_all[lay]
        sem = f"w_cast{lay}"
        if lay == 0 and phase == "post":
            sem = "w_cast0p"
        p = prep.setdefault(lay, {"bf_dram": {}, "sem": f"w_cast{lay}"})
        p.setdefault("sem_post", "w_cast0p" if lay == 0 else f"w_cast{lay}")
        bf_dram = p["bf_dram"]

        def cast_dma(dst, src, sem=sem):
            em.op("gpsimd", lambda e, dst=dst, src=src: e.dma_start(out=dst, in_=src),
                  incs=[em.inc(sem, 16)], drain=False)

        if lay == 0:
            if phase in ("pre", "all"):
                bf_dram["wv"] = _dram(nc, "wv_bf", [D, D], BF16)
                cast_dma(bf_dram["wv"], w["wv"])
                p["b2_bf"] = em.sb("b2c", [8, D], BF16)
                p["wg_bf"] = em.sb("wgc", [128, ND * 8], BF16)
                p["bg_bf"] = em.sb("bgc", [1, 8], BF16)
                cast_dma(p["b2_bf"][:, :], w["b2"])
                cast_dma(p["wg_bf"][:, :].rearrange("p (d e) -> p d e", d=ND),
                         w["wg"].rearrange("(d p) e -> p d e", p=128))
                cast_dma(p["bg_bf"][:, :], w["bg"])
            if phase in ("post", "all"):
                for nm in ("wq", "wk", "wo"):
                    bf_dram[nm] = _dram(nc, f"{nm}_bf", [D, D], BF16)
                    cast_dma(bf_dram[nm], w[nm])
        else:
            if phase in ("post", "all"):
                bf_dram["wqkv"] = _dram(nc, "wqkv_bf", [3 * D, D], BF16)
                cast_dma(bf_dram["wqkv"], w["wqkv"])
                bf_dram["wproj"] = _dram(nc, "wproj_bf", [D, D], BF16)
                cast_dma(bf_dram["wproj"], w["wproj"])
                for nm in ("fg", "fu"):
                    bf_dram[nm] = _dram(nc, f"{nm}_bf", [H1, D], BF16)
                    cast_dma(bf_dram[nm], w[nm])
                bf_dram["fd"] = _dram(nc, "fd_bf", [D, H1], BF16)
                cast_dma(bf_dram["fd"], w["fd"])
    return prep


def emit_layer(em, cst, w, lay, x_in, x_out, T, prep, boundary_in=None,
               post_ln=None, post_attn=None, is_last=True):
    nc = em.nc
    tag = f"L{lay}"
    NT = T // 128
    CW = min(512, T)
    bf_dram = prep["bf_dram"]
    cast_sem = prep["sem"]

    if lay == 0:
        b1_sb = em.sb(f"b1{tag}", [128, (HMOE // 128) * E], F32)
        b2_bf = prep["b2_bf"]
        wg_bf = prep["wg_bf"]
        bg_bf = prep["bg_bf"]

    xnT = em.sb(f"xnT{tag}", [128, ND * T], BF16)
    s_dram = _dram(nc, f"s_scr{tag}", [T, D], BF16)

    if boundary_in is None:
        emit_ln(em, cst, x_in, xnT, T)
    else:
        emit_boundary(em, cst, boundary_in["x_prev"], boundary_in["s_dram"],
                      boundary_in["g_dram"], boundary_in["g_tgts"], x_in, xnT, T,
                      boundary_in["silu"])
    if post_ln is not None:
        post_ln()
    em.barrier(("vector", "tensor", "scalar", "sync"), ENGS)

    with em.scope():
        v_buf = em.sb("vbuf", [128, NT * (NH * 65)], BF16)
        attn_buf = em.sb("attnT", [128, ND * T], BF16)
        woT = em.sb("woT", [128, 4 * D], BF16)
        with em.scope():
            wT = em.sb("wvT", [128, ND * 1024], BF16)
            em.wait("sync", em.sem(cast_sem), em.n(cast_sem))
            for dt in range(ND):
                if lay == 0:
                    srcw = bf_dram["wv"][0:D, dt * 128:(dt + 1) * 128]
                else:
                    srcw = bf_dram["wqkv"][2 * D:3 * D, dt * 128:(dt + 1) * 128]
                em.op("sync", lambda e, dt=dt, srcw=srcw: e.dma_start_transpose(
                    wT[:, dt * 1024:(dt + 1) * 1024], srcw), incs=[em.inc("w_pl", 16)])
            em.wait("tensor", em.sem("w_pl"), em.n("w_pl"))
            emit_v_tokmajor(em, cst, wT, xnT, v_buf, T, None)
            em.barrier(("tensor", "scalar", "vector", "gpsimd"),
                       ("tensor", "scalar", "vector", "sync", "gpsimd"))
        with em.scope():
            if lay == 0:
                wq_src, wk_src = bf_dram["wq"], bf_dram["wk"]
            else:
                wq_src = bf_dram["wqkv"][0:D, :]
                wk_src = bf_dram["wqkv"][D:2 * D, :]
            sp = prep["sem_post"]
            em.wait("sync", em.sem(sp), em.n(sp))
            emit_qk_attention(em, cst, wq_src, wk_src, xnT, v_buf, attn_buf, T, CW)
            em.wait("sync", em.sem("wo_tr"), em.n("wo_tr"))
            srcw = bf_dram["wo"] if lay == 0 else bf_dram["wproj"]
            for dt in range(4):
                em.op("sync", lambda e, dt=dt, srcw=srcw: e.dma_start_transpose(
                    woT[:, dt * D:(dt + 1) * D], srcw[0:D, dt * 128:(dt + 1) * 128]),
                    incs=[em.inc("wo_tr", 16)])
            wo_wA = (em.sem("wo_tr"), em.n("wo_tr"))
            if post_attn is not None:
                post_attn()
            em.barrier(("tensor", "vector", "scalar"), ENGS)
        with em.scope():
            woT2 = em.sb("woT2", [128, 4 * D], BF16)
            em.wait("sync", em.sem("wo_tr"), em.n("wo_tr"))
            for dt in range(4, ND):
                em.op("sync", lambda e, dt=dt, srcw=srcw: e.dma_start_transpose(
                    woT2[:, (dt - 4) * D:(dt - 3) * D], srcw[0:D, dt * 128:(dt + 1) * 128]),
                    incs=[em.inc("wo_tr", 16)])
            wo_wB = (em.sem("wo_tr"), em.n("wo_tr"))
            wo_aps = [woT[:, dt * D:(dt + 1) * D] for dt in range(4)] + \
                     [woT2[:, (dt - 4) * D:(dt - 3) * D] for dt in range(4, ND)]
            wo_w = [wo_wA] * 4 + [wo_wB] * 4
            emit_out_proj(em, cst, attn_buf, wo_aps, s_dram, T, wo_w)
            em.barrier(("tensor", "scalar", "sync", "vector"), ENGS)

    g_buf = g_dram = None
    g_tgts = None
    if lay == 0:
        g_buf = em.sb(f"gbuf{tag}", [128, NT * D], BF16)
        if not is_last:
            g_dram = _dram(nc, f"g_scr{tag}", [T, D], BF16)
        em.wait("tensor", em.sem(cast_sem), em.n(cast_sem))
        with em.scope():
            SEL = em.sb("SEL", [128, NT * 8], F32)
            selT = em.sb("selT", [8, T], BF16)
            w1s = DmaSlots(em, "w1s", 2, lambda n: em.sb(n, [128, 8 * 128], BF16))
            w2s = DmaSlots(em, "w2s", 2, lambda n: em.sb(n, [128, 8 * 1024], BF16))
            for ht0 in range(2):
                w1s.load("gpsimd", ht0,
                         lambda b: b[:, :].rearrange("p (f c) -> p f c", f=8),
                         lambda b, ht0=ht0: _w1_src(w["w1"], 0, ht0))
            w2s.load("gpsimd", 0,
                     lambda b: b[:, :].rearrange("p (r f) -> p r f", r=8),
                     lambda b: _w2_src(w["w2"], 0, 0))
            emit_b1_prep(em, cst, w, b1_sb)
            with em.scope():
                emit_gates(em, cst, xnT, wg_bf, bg_bf, SEL, selT, T)
                em.barrier(("vector", "tensor", "scalar"), ENGS)
            with em.scope():
                g_ready = emit_moe(em, cst, w, xnT, SEL, selT, b1_sb, b2_bf, g_buf, T, w1s, w2s)
                if not is_last:
                    g_tgts = []
                    gscnt = [0, 0]
                    for nt in range(NT):
                        i = nt % 2
                        s_, t_ = g_ready[nt]
                        em.wait("scalar", s_, t_)
                        if gscnt[i] > 0:
                            em.wait("scalar", em.sem(f"gs{i}"), 16 * gscnt[i])
                        em.op("scalar", lambda e, nt=nt: e.dma_start(
                            out=g_dram[nt * 128:(nt + 1) * 128, :],
                            in_=g_buf[:, nt * D:(nt + 1) * D]),
                            incs=[em.inc(f"gs{i}", 16)])
                        gscnt[i] += 1
                        g_tgts.append((em.sem(f"gs{i}"), 16 * gscnt[i]))
                    for i in range(2):
                        if gscnt[i]:
                            em.wait("scalar", em.sem(f"gs{i}"), 16 * gscnt[i])
                em.barrier(("tensor", "vector", "scalar", "gpsimd"), ENGS)
        g_needs_silu = True
    else:
        assert is_last
        emit_l1ffn(em, cst, xnT, T, bf_dram, x_in, s_dram, x_out)
        return None

    if is_last:
        emit_residual(em, cst, x_in, s_dram, g_buf, g_dram, x_out, T, g_needs_silu)
        em.barrier(("sync", "vector", "scalar"), ENGS)
        return None
    return {"x_prev": x_in, "s_dram": s_dram, "g_dram": g_dram,
            "g_tgts": g_tgts, "silu": g_needs_silu}



def _dram(nc, name, shape, dtype, kind="Internal"):
    return nc.dram_tensor(name, shape, dtype, kind=kind).ap()


_CONST_SPECS = {
    "ident_bf": ([128, 128], BF16), "ident_f32": ([128, 128], F32),
    "trimask_bf": ([128, 128], BF16),
    "selA_bf": ([128, 1], BF16), "selB_bf": ([128, 1], BF16),
    "ones_1x128_bf": ([1, 128], BF16), "ones_1x128_f32": ([1, 128], F32),
    "ones_1x64_f32": ([1, 64], F32), "epsb": ([128, 1], F32),
    "sel2_bf": ([128, 2], BF16), "bc2a_f32": ([2, 128], F32), "bc2b_f32": ([2, 128], F32),
}


def _const_arrays():
    bf = ml_dtypes.bfloat16
    tri = np.triu(np.ones((128, 128), dtype=np.float32)).astype(bf)
    selA = np.zeros((128, 1), dtype=np.float32); selA[:64] = 1.0
    selB = np.zeros((128, 1), dtype=np.float32); selB[64:] = 1.0
    return {
        "ident_bf": np.eye(128, dtype=np.float32).astype(bf),
        "ident_f32": np.eye(128, dtype=np.float32),
        "trimask_bf": tri,
        "selA_bf": selA.astype(bf),
        "selB_bf": selB.astype(bf),
        "ones_1x128_bf": np.ones((1, 128), dtype=np.float32).astype(bf),
        "ones_1x128_f32": np.ones((1, 128), dtype=np.float32),
        "ones_1x64_f32": np.ones((1, 64), dtype=np.float32),
        "epsb": np.full((128, 1), 1e-5, dtype=np.float32),
        "sel2_bf": np.stack([np.r_[np.ones(64), np.zeros(64)],
                             np.r_[np.zeros(64), np.ones(64)]], axis=1
                            ).astype(np.float32).astype(bf),
        "bc2a_f32": np.stack([np.ones(128), np.zeros(128)]).astype(np.float32),
        "bc2b_f32": np.stack([np.zeros(128), np.ones(128)]).astype(np.float32),
    }


def build_nc(layers, T=2048):
    nc = bass.Bass("TRN2", target_bir_lowering=False)
    cst_dram = {k: _dram(nc, k, shp, dt, kind="ExternalInput")
                for k, (shp, dt) in _CONST_SPECS.items()}
    x_in = _dram(nc, "x_in", [T, D], F32, kind="ExternalInput")
    x_out = _dram(nc, "x_out", [T, D], F32, kind="ExternalOutput")
    x_mid = _dram(nc, "x_mid", [T, D], F32) if len(layers) > 1 else None

    w_all = {}
    for lay in layers:
        if lay == 0:
            w_all[0] = {
                "wq": _dram(nc, "wq", [D, D], F32, kind="ExternalInput"),
                "wk": _dram(nc, "wk", [D, D], F32, kind="ExternalInput"),
                "wv": _dram(nc, "wv", [D, D], F32, kind="ExternalInput"),
                "wo": _dram(nc, "wo", [D, D], F32, kind="ExternalInput"),
                "wg": _dram(nc, "wg", [D, E], F32, kind="ExternalInput"),
                "bg": _dram(nc, "bg", [1, E], F32, kind="ExternalInput"),
                "w1": _dram(nc, "w1", [E, D, HMOE], F32, kind="ExternalInput"),
                "b1": _dram(nc, "b1", [E, HMOE], F32, kind="ExternalInput"),
                "w2": _dram(nc, "w2", [E, HMOE, D], F32, kind="ExternalInput"),
                "b2": _dram(nc, "b2", [E, D], F32, kind="ExternalInput"),
            }
        else:
            w_all[1] = {
                "wqkv": _dram(nc, "wqkv", [3 * D, D], F32, kind="ExternalInput"),
                "wproj": _dram(nc, "wproj", [D, D], F32, kind="ExternalInput"),
                "fg": _dram(nc, "fg", [H1, D], I32, kind="ExternalInput"),
                "fu": _dram(nc, "fu", [H1, D], I32, kind="ExternalInput"),
                "fd": _dram(nc, "fd", [D, H1], I32, kind="ExternalInput"),
            }

    with ExitStack() as gstack:
        block = gstack.enter_context(nc.Block())
        em = Em(nc, gstack, block)
        cst = {k: em.sb(f"c_{k}", shp, dt) for k, (shp, dt) in _CONST_SPECS.items()}
        for k in _CONST_SPECS:
            em.op("sync", lambda e, k=k: e.dma_start(out=cst[k][:, :], in_=cst_dram[k]),
                  incs=[em.inc("w_pl", 16)])
        em.wait("sync", em.sem("w_pl"), em.n("w_pl"))
        em.barrier(("sync",), ENGS)

        prep = emit_weight_casts(em, w_all, layers[:1], phase="pre")
        post_done = [False]
        if len(layers) == 1 and layers[0] == 1:
            emit_weight_casts(em, w_all, layers, phase="post", prep=prep)
            post_done[0] = True

        def post_ln():
            if post_done[0]:
                return
            post_done[0] = True
            tgt = max(0, em.n("lnv0") - 2)
            if tgt:
                em.wait("gpsimd", em.sem("lnv0"), tgt)
            emit_weight_casts(em, w_all, layers[:1], phase="post", prep=prep)

        def post_attn():
            if len(layers) <= 1 or 1 in prep and prep[1]["bf_dram"]:
                return
            tgt = em.n("at_r") // 2
            if tgt:
                em.wait("gpsimd", em.sem("at_r"), tgt)
            emit_weight_casts(em, w_all, layers[1:], phase="post", prep=prep)

        cur_in = x_in
        binfo = None
        for idx, lay in enumerate(layers):
            last = (idx == len(layers) - 1)
            out_ap = x_out if last else x_mid
            with em.scope():
                binfo = emit_layer(em, cst, w_all[lay], lay, cur_in, out_ap, T,
                                   prep[lay], boundary_in=binfo,
                                   post_ln=(post_ln if idx == 0 else None),
                                   post_attn=(post_attn if idx == 0 else None),
                                   is_last=last)
            cur_in = out_ap
        em.flush()
    return nc


def _in_map_for(layers, x_core, inputs):
    m = dict(_const_arrays())
    m["x_in"] = x_core
    if 0 in layers:
        m.update({
            "wq": inputs["L0_attn_Wq"], "wk": inputs["L0_attn_Wk"],
            "wv": inputs["L0_attn_Wv"], "wo": inputs["L0_attn_Wo"],
            "wg": inputs["L0_ffn_Wg"],
            "bg": np.ascontiguousarray(np.asarray(inputs["L0_ffn_bg"]).reshape(1, E)),
            "w1": inputs["L0_ffn_W1"], "b1": inputs["L0_ffn_b1"],
            "w2": inputs["L0_ffn_W2"], "b2": inputs["L0_ffn_b2"],
        })
    if 1 in layers:
        m.update({
            "wqkv": inputs["L1_attn_w_qkv"], "wproj": inputs["L1_attn_w_proj"],
            "fg": inputs["L1_ffn_w_gate"], "fu": inputs["L1_ffn_w_up"],
            "fd": inputs["L1_ffn_w_down"],
        })
    return {k: np.ascontiguousarray(np.asarray(v)) for k, v in m.items()}


def kernel(**inputs):
    x = np.ascontiguousarray(np.asarray(inputs["L0_attn_x"], dtype=np.float32))
    B = x.shape[0]
    layers = (0, 1)
    nc = build_nc(layers)
    in_maps = [_in_map_for(layers, np.ascontiguousarray(x[c]), inputs) for c in range(B)]
    res = run_bass_kernel_spmd(nc, in_maps, core_ids=list(range(B)))
    return np.stack([np.asarray(r["x_out"], dtype=np.float32) for r in res.results], axis=0)


# version 23
# speedup vs baseline: 1.0872x; 1.0828x over previous
import numpy as np
import ml_dtypes
from contextlib import ExitStack, contextmanager

import concourse.bass as bass
import concourse.mybir as mybir
from concourse.bass_utils import run_bass_kernel_spmd

F32 = mybir.dt.float32
BF16 = mybir.dt.bfloat16
I32 = mybir.dt.int32
AF = mybir.ActivationFunctionType
ALU = mybir.AluOpType
AX = mybir.AxisListType

D = 1024
NH = 16
DH = 64
ND = D // 128
E = 8
HMOE = 4096
H1 = 3072
QI = 1.0 / 127.0

ENGS = ("tensor", "vector", "scalar", "gpsimd", "sync")


class Em:

    def __init__(self, nc, gstack, block):
        self.nc = nc
        self.gstack = gstack
        self.block = block
        self.stack = gstack
        self.ops = {e: [] for e in ENGS}
        self._sems = {}
        self._n = {}
        self._slots = {}
        self._names = {}
        self.ph = self.sem("ph")

    def sem(self, name):
        if name not in self._sems:
            self._sems[name] = self.gstack.enter_context(self.nc.semaphore(name))
            self._n[name] = 0
        return self._sems[name]

    def bump(self, name, amt):
        self.sem(name)
        self._n[name] += amt
        return self._n[name]

    def n(self, name):
        self.sem(name)
        return self._n[name]

    def op(self, eng, fn, incs=(), drain=None):
        incs = [i for i in incs if i is not None]
        do_drain = bool(drain)
        def run(e, fn=fn, incs=tuple(incs), do_drain=do_drain):
            r = fn(e)
            if incs:
                r.then_inc(*incs[0])
            for s, a in incs[1:]:
                e.drain().then_inc(s, a)
            if do_drain and len(incs) <= 1:
                e.drain()
        self.ops[eng].append(run)

    def dr(self, eng):
        self.ops[eng].append(lambda e: e.drain())

    def wait(self, eng, sem, target):
        self.ops[eng].append(lambda e, s=sem, t=target: e.wait_ge(s, t))

    def inc(self, name, amt=1):
        h = self.sem(name)
        self.bump(name, amt)
        return (h, amt)

    def flush(self):
        for eng in ENGS:
            ops = self.ops[eng]
            if not ops:
                continue
            self.ops[eng] = []
            dec = getattr(self.block, eng)
            def body(e, ops=tuple(ops)):
                for fn in ops:
                    fn(e)
            dec(body)

    @contextmanager
    def scope(self):
        st = ExitStack()
        old = self.stack
        self.stack = st
        try:
            yield
        finally:
            try:
                self.flush()
            finally:
                st.close()
                self.stack = old

    def barrier(self, prods, waiters=ENGS):
        for p in prods:
            self.op(p, lambda e: e.drain(), incs=[self.inc("ph")], drain=False)
        t = self.n("ph")
        for w in waiters:
            self.wait(w, self.ph, t)

    def _uniq(self, name):
        n = self._names.get(name, 0)
        self._names[name] = n + 1
        return name if n == 0 else f"{name}__{n}"

    def sb(self, name, shape, dtype):
        return self.stack.enter_context(self.nc.sbuf_tensor(self._uniq(name), shape, dtype))

    def ps(self, name, shape, dtype=F32):
        return self.stack.enter_context(self.nc.psum_tensor(self._uniq(name), shape, dtype))

    def slot_state(self, name, k):
        if name not in self._slots:
            self._slots[name] = {
                "ld": [self.sem(f"{name}_ld{i}") for i in range(k)],
                "fr": [self.sem(f"{name}_fr{i}") for i in range(k)],
                "n_ld": [0] * k, "n_fr": [0] * k, "k": k,
            }
        st = self._slots[name]
        assert st["k"] == k
        return st


class DmaSlots:

    def __init__(self, em, name, k, alloc):
        self.em = em
        self.st = em.slot_state(name, k)
        self.bufs = [alloc(f"{name}_b{i}") for i in range(k)]
        self.k = k

    def load(self, qeng, i, out_fn, in_fn, transpose=False):
        st = self.st
        assert st["n_ld"][i] == st["n_fr"][i], "slot busy"
        if st["n_ld"][i] > 0:
            self.em.wait(qeng, st["fr"][i], st["n_fr"][i])
        b = self.bufs[i]
        if transpose:
            self.em.op(qeng, lambda e, b=b: e.dma_start_transpose(out_fn(b), in_fn(b)),
                       incs=[(st["ld"][i], 16)], drain=False)
        else:
            self.em.op(qeng, lambda e, b=b: e.dma_start(out=out_fn(b), in_=in_fn(b)),
                       incs=[(st["ld"][i], 16)], drain=False)
        st["n_ld"][i] += 1
        return b

    def wait_loaded(self, eng, i):
        self.em.wait(eng, self.st["ld"][i], 16 * self.st["n_ld"][i])

    def free_marker(self, i):
        self.st["n_fr"][i] += 1
        return (self.st["fr"][i], 1)


class PsumSlots:

    def __init__(self, em, name, k, alloc):
        self.em = em
        self.st = em.slot_state(name, k)
        self.bufs = [alloc(f"{name}_b{i}") for i in range(k)]
        self.k = k

    def begin(self, i):
        st = self.st
        assert st["n_ld"][i] == st["n_fr"][i], "psum slot busy"
        if st["n_ld"][i] > 0:
            self.em.wait("tensor", st["fr"][i], st["n_fr"][i])
        return self.bufs[i]

    def done_marker(self, i):
        self.st["n_ld"][i] += 1
        return (self.st["ld"][i], 1)

    def wait_full(self, eng, i):
        self.em.wait(eng, self.st["ld"][i], self.st["n_ld"][i])

    def free_marker(self, i):
        self.st["n_fr"][i] += 1
        return (self.st["fr"][i], 1)



def emit_ln(em, cst, x_ap, xnT, T, x_gate=None):
    NT = T // 128
    with em.scope():
        xs = DmaSlots(em, "lnx", 2, lambda n: em.sb(n, [128, D], F32))
        pt = PsumSlots(em, "psT", 2, lambda n: em.ps(n, [128, 128], BF16))
        xc = [em.sb(f"ln_xc{p}", [128, D], F32) for p in range(2)]
        xnt = [em.sb(f"ln_xnt{p}", [128, D], BF16) for p in range(2)]
        sq = em.sb("ln_sq", [128, D], F32)
        mean = [em.sb(f"ln_mean{p}", [128, 1], F32) for p in range(2)]
        var = [em.sb(f"ln_var{p}", [128, 1], F32) for p in range(2)]
        rstd = [em.sb(f"ln_rstd{p}", [128, 1], F32) for p in range(2)]
        xb = {}

        def load_x(k):
            if x_gate is not None:
                em.wait("sync", x_gate[k][0], x_gate[k][1])
            xb[k] = xs.load(
                "sync", k % 2, lambda b: b[:, :],
                lambda b, k=k: x_ap[k * 128:(k + 1) * 128, :])

        def ln_stats(nt):
            p = nt % 2
            X = xb.pop(nt)
            xs.wait_loaded("vector", p)
            if nt >= 2:
                em.wait("vector", em.sem(f"lnxn{p}"), em.n(f"lnxn{p}"))
            em.op("vector", lambda e, X=X, p=p: e.reduce_sum(mean[p][:, :], X[:, :], axis=AX.X),
                  drain=True)
            em.op("vector", lambda e, p=p: e.tensor_scalar(
                mean[p][:, :], mean[p][:, :], 1.0 / D, None, op0=ALU.mult), drain=True)
            em.op("vector", lambda e, X=X, p=p: e.tensor_scalar(
                xc[p][:, :], X[:, :], mean[p][:, 0:1], None, op0=ALU.subtract),
                incs=[xs.free_marker(p), em.inc(f"lnv{p}")], drain=True)

        def ln_norm(nt):
            p = nt % 2
            em.wait("scalar", em.sem(f"lnv{p}"), em.n(f"lnv{p}"))
            if nt >= 2:
                em.wait("scalar", em.sem(f"lnt{p}"), em.n(f"lnt{p}"))
            em.op("scalar", lambda e, p=p: e.activation(
                sq[:, :], xc[p][:, :], AF.Square, accum_out=var[p][:, :]), drain=True)
            em.op("scalar", lambda e, p=p: e.activation(
                var[p][:, :], var[p][:, :], AF.Ln, bias=cst["epsb"][:, 0:1], scale=1.0 / D),
                drain=True)
            em.op("scalar", lambda e, p=p: e.activation(
                rstd[p][:, :], var[p][:, :], AF.Exp, scale=-0.5), drain=True)
            em.op("scalar", lambda e, p=p: e.activation(
                xnt[p][:, :], xc[p][:, :], AF.Copy, scale=rstd[p][:, 0:1]),
                incs=[em.inc(f"lnxn{p}")], drain=True)
            em.wait("tensor", em.sem(f"lnxn{p}"), em.n(f"lnxn{p}"))
            for dt in range(ND):
                j = dt % 2
                P = pt.begin(j)
                last = (dt == ND - 1)
                em.op("tensor", lambda e, P=P, dt=dt, p=p: e.transpose(
                    P[:, :], xnt[p][:, dt * 128:(dt + 1) * 128], cst["ident_bf"][:, :]),
                    incs=[pt.done_marker(j), em.inc(f"lnt{p}") if last else None])
                eng = "vector" if j == 0 else "scalar"
                pt.wait_full(eng, j)
                if eng == "vector":
                    em.op(eng, lambda e, P=P, dt=dt, nt=nt: e.tensor_copy(
                        xnT[:, dt * T + nt * 128: dt * T + (nt + 1) * 128], P[:, :]),
                        incs=[pt.free_marker(j)])
                else:
                    em.op(eng, lambda e, P=P, dt=dt, nt=nt: e.activation(
                        xnT[:, dt * T + nt * 128: dt * T + (nt + 1) * 128], P[:, :], AF.Copy),
                        incs=[pt.free_marker(j)])

        for it in range(NT + 1):
            if it < NT:
                if it == 0:
                    load_x(0)
                if it + 1 < NT:
                    load_x(it + 1)
                ln_stats(it)
            if 1 <= it <= NT:
                ln_norm(it - 1)


def emit_proj_T(em, cst, wT, col_w, xnT, outT, T, CW, ps, dt_waits=None):
    NC = T // CW
    for it in range(ND):
        for c in range(NC):
            j = (it * NC + c) % 3
            P = ps.begin(j)
            dm = ps.done_marker(j)
            for dt in range(ND):
                if dt_waits is not None and it == 0 and c == 0:
                    em.wait("tensor", dt_waits[dt][0], dt_waits[dt][1])
                sp = (dt == ND - 1)
                em.op("tensor", lambda e, P=P, dt=dt, it=it, c=c, sp=sp: e.matmul(
                    P[:, :],
                    wT[:, dt * col_w + it * 128: dt * col_w + (it + 1) * 128],
                    xnT[:, dt * T + c * CW: dt * T + (c + 1) * CW],
                    start=(dt == 0), stop=sp), incs=[dm] if sp else ())
            ps.wait_full("scalar", j)
            em.op("scalar", lambda e, P=P, it=it, c=c: e.activation(
                outT[:, it * T + c * CW: it * T + (c + 1) * CW], P[:, :], AF.Copy),
                incs=[ps.free_marker(j)])


def emit_v_tokmajor(em, cst, wT, xnT, v_buf, T, dt_waits=None):
    NT = T // 128
    em.op("vector", lambda e: e.memset(v_buf[:, :], 1.0), incs=[em.inc("cs_t")], drain=True)
    em.wait("vector", em.sem("cs_t"), em.n("cs_t"))
    ps = PsumSlots(em, "psD1", 2, lambda n: em.ps(n, [128, 1024], F32))
    for nt in range(NT):
        j = nt % 2
        P = ps.begin(j)
        dm = ps.done_marker(j)
        for dt in range(ND):
            if dt_waits is not None and nt == 0:
                em.wait("tensor", dt_waits[dt][0], dt_waits[dt][1])
            for half in range(2):
                last = (dt == ND - 1 and half == 1)
                em.op("tensor", lambda e, P=P, dt=dt, nt=nt, half=half, last=last: e.matmul(
                    P[:, half * 512:(half + 1) * 512],
                    xnT[:, dt * T + nt * 128: dt * T + (nt + 1) * 128],
                    wT[:, dt * 1024 + half * 512: dt * 1024 + (half + 1) * 512],
                    start=(dt == 0), stop=(dt == ND - 1)),
                    incs=[dm] if last else ())
        ps.wait_full("vector", j)
        for h in range(NH):
            mk = [ps.free_marker(j)] if h == NH - 1 else ()
            em.op("vector", lambda e, P=P, nt=nt, h=h: e.tensor_copy(
                v_buf[:, nt * (NH * 65) + 65 * h: nt * (NH * 65) + 65 * h + 64],
                P[:, h * 64:(h + 1) * 64]), incs=mk)


def emit_qk_attention(em, cst, wq_src, wk_src, xnT, v_buf, attn_buf, T, CW):
    NKT = T // 128
    NC = T // CW
    WAVE = 8
    iters = []
    gid = 0
    for it in range(ND):
        for c in range(NC):
            nki = min((c * CW + CW + 127) // 128, NKT)
            for w0 in range(0, nki, WAVE):
                iters.append((it, c, w0, min(w0 + WAVE, nki), w0 + WAVE >= nki, gid))
            gid += 1
    M = len(iters)
    with em.scope():
        wq = em.sb("wqT", [128, ND * 1024], BF16)
        wk = em.sb("wkT", [128, ND * 1024], BF16)
        qt = [em.sb(f"qt{p}", [128, T], BF16) for p in range(2)]
        kt = [em.sb(f"kt{p}", [128, T], BF16) for p in range(2)]
        sqq = em.sb("cs_sq", [128, T], BF16)
        m2b = em.sb("cs_m2", [2, 2], F32)
        mxb = em.sb("cs_mx", [2, NC], F32)
        prodb = em.sb("cs_prod", [2, 1], F32)
        negc = em.sb("negc", [128, NH], F32)
        expb = [[em.sb(f"at_e{p}{hl}", [128, WAVE * CW], BF16) for hl in range(2)]
                for p in range(2)]
        rec = [[em.sb(f"at_rec{p}{hl}", [1, CW], F32) for hl in range(2)]
               for p in range(2)]
        recb = em.sb("at_recb", [128, CW], F32)
        ps3 = PsumSlots(em, "ps3", 3, lambda n: em.ps(n, [128, CW], F32))
        psp = PsumSlots(em, "psP4", 4, lambda n: em.ps(n, [128, CW], F32))
        psc = PsumSlots(em, "psC1", 1, lambda n: em.ps(n, [128, CW], F32))
        qk_dtw = []
        for dt in range(ND):
            em.op("sync", lambda e, dt=dt: e.dma_start_transpose(
                wq[:, dt * 1024:(dt + 1) * 1024], wq_src[0:D, dt * 128:(dt + 1) * 128]),
                incs=[em.inc(f"wtrd{dt}", 16)])
            qk_dtw.append((em.sem(f"wtrd{dt}"), em.n(f"wtrd{dt}")))
            em.op("sync", lambda e, dt=dt: e.dma_start_transpose(
                wk[:, dt * 1024:(dt + 1) * 1024], wk_src[0:D, dt * 128:(dt + 1) * 128]),
                incs=[em.inc("w_pl", 16)])
        qk_total = (em.sem("w_pl"), em.n("w_pl"))

        pv_cnt = [0, 0]
        bc_cnt = [0, 0]
        qk_cnt = [0, 0]
        exp_frs = {}
        mask_tgt = {}
        pv_meta = {}

        def emit_qk_cs(it):
            p = it % 2
            if qk_cnt[p] > 0:
                em.wait("vector", em.sem(f"qdone{p}"), qk_cnt[p])
            for which, wbuf, obuf in (("q", wq, qt[p]), ("k", wk, kt[p])):
                if which == "k" and it == 0:
                    em.wait("tensor", qk_total[0], qk_total[1])
                for c4 in range(NC):
                    j = (c4 + (0 if which == "q" else NC)) % 3
                    P = ps3.begin(j)
                    dm = ps3.done_marker(j)
                    for dt in range(ND):
                        if which == "q" and it == 0 and c4 == 0:
                            em.wait("tensor", qk_dtw[dt][0], qk_dtw[dt][1])
                        sp = (dt == ND - 1)
                        em.op("tensor", lambda e, P=P, wbuf=wbuf, dt=dt, it=it, c4=c4, sp=sp: e.matmul(
                            P[:, :],
                            wbuf[:, dt * 1024 + it * 128: dt * 1024 + (it + 1) * 128],
                            xnT[:, dt * T + c4 * CW: dt * T + (c4 + 1) * CW],
                            start=(dt == 0), stop=sp), incs=[dm] if sp else ())
                    ps3.wait_full("vector", j)
                    em.op("vector", lambda e, P=P, obuf=obuf, c4=c4: e.tensor_copy(
                        obuf[:, c4 * CW:(c4 + 1) * CW], P[:, :]),
                        incs=[ps3.free_marker(j), em.inc("cs_v")])
            for which, obuf, col0 in (("q", qt[p], 0), ("k", kt[p], 1)):
                em.wait("vector", em.sem("cs_v"), em.n("cs_v"))
                em.op("vector", lambda e, obuf=obuf: e.tensor_tensor(
                    sqq[:, :], obuf[:, :], obuf[:, :], op=ALU.mult),
                    incs=[em.inc("cs_v")], drain=True)
                em.wait("tensor", em.sem("cs_v"), em.n("cs_v"))
                for c4 in range(NC):
                    j = c4 % 3
                    P = ps3.begin(j)
                    em.op("tensor", lambda e, P=P, c4=c4: e.matmul(
                        P[0:2, :], cst["sel2_bf"][:, 0:2], sqq[:, c4 * CW:(c4 + 1) * CW],
                        start=True, stop=True), incs=[ps3.done_marker(j)])
                    ps3.wait_full("vector", j)
                    em.op("vector", lambda e, P=P, c4=c4: e.reduce_max(
                        mxb[0:2, c4:c4 + 1], P[0:2, :], axis=AX.X),
                        incs=[ps3.free_marker(j)], drain=True)
                em.op("vector", lambda e, col0=col0: e.reduce_max(
                    m2b[0:2, col0:col0 + 1], mxb[0:2, 0:NC], axis=AX.X),
                    incs=[em.inc("cs_v")], drain=True)
            em.wait("vector", em.sem("cs_v"), em.n("cs_v"))
            em.op("vector", lambda e: e.tensor_tensor(
                prodb[0:2, 0:1], m2b[0:2, 0:1], m2b[0:2, 1:2], op=ALU.mult),
                incs=[em.inc("cs_v")], drain=True)
            em.wait("scalar", em.sem("cs_v"), em.n("cs_v"))
            em.op("scalar", lambda e: e.activation(prodb[0:2, 0:1], prodb[0:2, 0:1], AF.Sqrt),
                  incs=[em.inc("cs_s")], drain=True)
            em.wait("tensor", em.sem("cs_s"), em.n("cs_s"))
            for hl in range(2):
                j = hl % 3
                P = ps3.begin(j)
                sel = cst["bc2a_f32"] if hl == 0 else cst["bc2b_f32"]
                em.op("tensor", lambda e, P=P, sel=sel: e.matmul(
                    P[:, 0:1], sel[0:2, :], prodb[0:2, 0:1],
                    start=True, stop=True), incs=[ps3.done_marker(j)])
                ps3.wait_full("vector", j)
                em.op("vector", lambda e, P=P, it=it, hl=hl: e.tensor_scalar(
                    negc[:, 2 * it + hl: 2 * it + hl + 1], P[:, 0:1], -0.125, None,
                    op0=ALU.mult), incs=[ps3.free_marker(j), em.inc("cs_v")])
            em.wait("scalar", em.sem("cs_v"), em.n("cs_v"))

        def stage_S(m):
            it, c, w0, w1, lastw, gid = iters[m]
            p = m % 2
            pit = it % 2
            sc_blocks = []
            for ki in range(w0, w1):
                blk = []
                for hl in range(2):
                    lo = 64 * hl
                    h = 2 * it + hl
                    j = (2 * (ki - w0) + hl) % 3
                    qo = ki * 128 - c * CW

                    def emit_sc(j=j, ki=ki, hl=hl, lo=lo, h=h, qo=qo, it=it, c=c,
                                p=p, pit=pit, m=m, last_of_pair=(lastw and c == NC - 1 and ki == w1 - 1 and hl == 1)):
                        P = ps3.begin(j)
                        mks = [ps3.done_marker(j)]
                        if last_of_pair:
                            qk_cnt[pit] += 1
                            mks.append((em.sem(f"qdone{pit}"), 1))
                        em.op("tensor", lambda e, P=P, lo=lo: e.matmul(
                            P[:, :],
                            kt[pit][lo:lo + 64, ki * 128:(ki + 1) * 128],
                            qt[pit][lo:lo + 64, c * CW:(c + 1) * CW],
                            start=True, stop=True, tile_position=(lo, 0)), incs=mks)
                        if ki == w0 and hl == 0 and pv_cnt[p] > 0:
                            em.wait("scalar", em.sem(f"pvd{p}"), pv_cnt[p])
                            em.wait("vector", em.sem(f"pvd{p}"), pv_cnt[p])
                        ps3.wait_full("scalar", j)
                        eb = expb[p][hl]
                        kk = ki - w0
                        if qo < 0:
                            em.op("scalar", lambda e, P=P: e.activation(
                                eb[:, kk * CW:(kk + 1) * CW], P[:, :], AF.Exp,
                                bias=negc[:, h:h + 1], scale=0.125),
                                incs=[ps3.free_marker(j)])
                        else:
                            em.op("scalar", lambda e, P=P: e.activation(
                                eb[:, kk * CW + qo:(kk + 1) * CW], P[:, qo:CW], AF.Exp,
                                bias=negc[:, h:h + 1], scale=0.125),
                                incs=[ps3.free_marker(j)])
                            if qo > 0:
                                em.op("vector", lambda e: e.memset(
                                    eb[:, kk * CW: kk * CW + qo], 0.0))
                            st3 = ps3.st
                            em.wait("vector", st3["fr"][j], st3["n_fr"][j])
                            em.op("vector", lambda e: e.tensor_tensor(
                                eb[:, kk * CW + qo: kk * CW + qo + 128],
                                eb[:, kk * CW + qo: kk * CW + qo + 128],
                                cst["trimask_bf"][:, :], op=ALU.mult), incs=[em.inc("at_m")])
                        blk.append((ps3.st["fr"][j], ps3.st["n_fr"][j]))
                    emit_sc()
                sc_blocks.append((blk, em.n("at_m")))
            exp_frs[m] = sc_blocks

        def stage_P(m):
            it, c, w0, w1, lastw, gid = iters[m]
            p = m % 2
            pg = gid % 2
            blocks = exp_frs.pop(m)
            slots = []
            for hl in range(2):
                sl = 2 * pg + hl
                if w0 == 0:
                    Ppv = psp.begin(sl)
                else:
                    Ppv = psp.bufs[sl]
                slots.append(Ppv)
            for bi, (blk, mtgt) in enumerate(blocks):
                ki = w0 + bi
                for hl in range(2):
                    h = 2 * it + hl
                    s_, t_ = blk[hl]
                    em.wait("tensor", s_, t_)
                    em.wait("tensor", em.sem("at_m"), mtgt)
                    Ppv = slots[hl]
                    sl = 2 * pg + hl
                    sp_ = (lastw and ki == w1 - 1)
                    mks = []
                    if sp_:
                        mks.append(psp.done_marker(sl))
                        if hl == 1:
                            pv_cnt[p] += 1
                            mks.append((em.sem(f"pvd{p}"), 1))
                    em.op("tensor", lambda e, Ppv=Ppv, ki=ki, h=h, w0=w0, hl=hl, st_=(ki == 0), sp_=sp_, p=p: e.matmul(
                        Ppv[0:65, :],
                        v_buf[:, ki * (NH * 65) + 65 * h: ki * (NH * 65) + 65 * h + 65],
                        expb[p][hl][:, (ki - w0) * CW:(ki - w0 + 1) * CW],
                        start=st_, stop=sp_), incs=mks)
            if not lastw:
                pv_cnt[p] += 1
                em.op("tensor", lambda e: e.drain(), incs=[(em.sem(f"pvd{p}"), 1)])

        def stage_B(m):
            it, c, w0, w1, lastw, gid = iters[m]
            if not lastw:
                return
            p = gid % 2
            for hl in range(2):
                sl = 2 * p + hl
                psp.wait_full("scalar", sl)
            if bc_cnt[p] > 0:
                em.wait("scalar", em.sem(f"bcd{p}"), bc_cnt[p])
            for hl in range(2):
                sl = 2 * p + hl
                Ppv = psp.bufs[sl]
                em.op("scalar", lambda e, Ppv=Ppv, p=p, hl=hl: e.activation(
                    rec[p][hl][0:1, :], Ppv[64:65, :], AF.Ln), drain=True)
                em.op("scalar", lambda e, p=p, hl=hl: e.activation(
                    rec[p][hl][0:1, :], rec[p][hl][0:1, :], AF.Exp, scale=-1.0),
                    incs=[em.inc("at_r")], drain=True)
            em.wait("tensor", em.sem("at_r"), em.n("at_r"))
            Pbc = psc.begin(0)
            bc_cnt[p] += 1
            em.op("tensor", lambda e, Pbc=Pbc, p=p: e.matmul(
                Pbc[0:64, :], cst["ones_1x64_f32"][0:1, :], rec[p][0][0:1, :],
                start=True, stop=True))
            em.op("tensor", lambda e, Pbc=Pbc, p=p: e.matmul(
                Pbc[64:128, :], cst["ones_1x64_f32"][0:1, :], rec[p][1][0:1, :],
                start=True, stop=True), incs=[psc.done_marker(0), (em.sem(f"bcd{p}"), 1)])
            psc.wait_full("vector", 0)
            em.op("vector", lambda e, Pbc=Pbc: e.tensor_copy(recb[:, :], Pbc[:, :]),
                  incs=[psc.free_marker(0)], drain=True)
            for hl in range(2):
                sl = 2 * p + hl
                lo = 64 * hl
                Ppv = psp.bufs[sl]
                em.op("vector", lambda e, Ppv=Ppv, it=it, lo=lo, c=c: e.tensor_tensor(
                    attn_buf[lo:lo + 64, it * T + c * CW: it * T + (c + 1) * CW],
                    Ppv[0:64, :], recb[lo:lo + 64, :], op=ALU.mult),
                    incs=[psp.free_marker(sl)], drain=(hl == 1))

        emit_qk_cs(0)
        for t in range(M + 2):
            if t < M:
                it, c, w0, _, _, _ = iters[t]
                if c == 0 and w0 == 0 and it + 1 < ND:
                    emit_qk_cs(it + 1)
                stage_S(t)
            if 1 <= t <= M:
                stage_P(t - 1)
            if 2 <= t <= M + 1:
                stage_B(t - 2)


def emit_out_proj(em, cst, attn_buf, wo_aps, s_dram, T, dt_waits=None):
    NT = T // 128
    ps = PsumSlots(em, "psD1", 2, lambda n: em.ps(n, [128, 1024], F32))
    sout = DmaSlots(em, "wo_s", 2, lambda n: em.sb(n, [128, D], BF16))
    for nt in range(NT):
        j = nt % 2
        P = ps.begin(j)
        dm = ps.done_marker(j)
        for dt in range(ND):
            if dt_waits is not None and nt == 0:
                em.wait("tensor", dt_waits[dt][0], dt_waits[dt][1])
            for half in range(2):
                last = (dt == ND - 1 and half == 1)
                em.op("tensor", lambda e, P=P, dt=dt, half=half, nt=nt, last=last: e.matmul(
                    P[:, half * 512:(half + 1) * 512],
                    attn_buf[:, dt * T + nt * 128: dt * T + (nt + 1) * 128],
                    wo_aps[dt][:, half * 512:(half + 1) * 512],
                    start=(dt == 0), stop=(dt == ND - 1)), incs=[dm] if last else ())
        ps.wait_full("scalar", j)
        st = sout.st
        if st["n_ld"][j] > st["n_fr"][j]:
            em.wait("scalar", st["ld"][j], 16 * st["n_ld"][j])
            st["n_fr"][j] += 1
            em.op("scalar", lambda e: e.nop(), incs=[(st["fr"][j], 1)], drain=False)
        B = sout.bufs[j]
        em.op("scalar", lambda e, P=P, B=B: e.activation(B[:, :], P[:, :], AF.Copy),
              incs=[ps.free_marker(j)])
        em.wait("scalar", ps.st["fr"][j], ps.st["n_fr"][j])
        st["n_ld"][j] += 1
        em.op("scalar", lambda e, B=B, nt=nt: e.dma_start(
            out=s_dram[nt * 128:(nt + 1) * 128, :], in_=B[:, :]), incs=[(st["ld"][j], 16)])
    for j in range(2):
        st = sout.st
        while st["n_fr"][j] < st["n_ld"][j]:
            em.wait("scalar", st["ld"][j], 16 * st["n_ld"][j])
            st["n_fr"][j] += 1
            em.op("scalar", lambda e: e.nop(), incs=[(st["fr"][j], 1)], drain=False)


def emit_gates(em, cst, xnT, wg_bf, bg_bf, SEL, selT, T):
    NT = T // 128
    pg = PsumSlots(em, "psG", 2, lambda n: em.ps(n, [128, 8], F32))
    pt = PsumSlots(em, "psN", 2, lambda n: em.ps(n, [8, 128], F32))
    G = em.sb("gt_G", [128, 8], F32)
    Pr = em.sb("gt_P", [128, 8], F32)
    m1 = em.sb("gt_m1", [128, 1], F32)
    m1n = em.sb("gt_m1n", [128, 1], F32)
    eq = em.sb("gt_eq", [128, 8], F32)
    gm = em.sb("gt_gm", [128, 8], F32)
    m2 = em.sb("gt_m2", [128, 1], F32)
    msk = em.sb("gt_msk", [128, 8], F32)
    ssum = em.sb("gt_ss", [128, 1], F32)
    rden = em.sb("gt_rd", [128, 1], F32)
    for nt in range(NT):
        j = nt % 2
        P = pg.begin(j)
        for dt in range(ND):
            em.op("tensor", lambda e, P=P, dt=dt, nt=nt: e.matmul(
                P[:, :], xnT[:, dt * T + nt * 128: dt * T + (nt + 1) * 128],
                wg_bf[:, dt * 8:(dt + 1) * 8], start=(dt == 0), stop=False))
        em.op("tensor", lambda e, P=P: e.matmul(
            P[:, :], cst["ones_1x128_bf"][0:1, :], bg_bf[0:1, :], start=False, stop=True),
            incs=[pg.done_marker(j)])
        pg.wait_full("vector", j)
        if nt > 0:
            em.wait("vector", em.sem("cs_s"), em.n("cs_s"))
        em.op("vector", lambda e, P=P: e.tensor_copy(G[:, :], P[:, :]), incs=[pg.free_marker(j)], drain=True)
        em.op("vector", lambda e: e.reduce_max(m1n[:, :], G[:, :], axis=AX.X, negate=True), drain=True)
        em.op("vector", lambda e: e.reduce_max(m1[:, :], G[:, :], axis=AX.X), incs=[em.inc("cs_v")], drain=True)
        em.wait("scalar", em.sem("cs_v"), em.n("cs_v"))
        em.op("scalar", lambda e: e.activation(Pr[:, :], G[:, :], AF.Exp, bias=m1n[:, 0:1]),
              incs=[em.inc("cs_s")])
        em.wait("vector", em.sem("cs_s"), em.n("cs_s"))
        em.op("vector", lambda e: e.tensor_scalar(eq[:, :], G[:, :], m1[:, 0:1], None, op0=ALU.is_ge), drain=True)
        em.op("vector", lambda e: e.scalar_tensor_tensor(
            gm[:, :], eq[:, :], -1e30, G[:, :], op0=ALU.mult, op1=ALU.add), drain=True)
        em.op("vector", lambda e: e.reduce_max(m2[:, :], gm[:, :], axis=AX.X), drain=True)
        em.op("vector", lambda e: e.tensor_scalar(msk[:, :], G[:, :], m2[:, 0:1], None, op0=ALU.is_ge), drain=True)
        em.op("vector", lambda e: e.tensor_tensor(msk[:, :], msk[:, :], Pr[:, :], op=ALU.mult), drain=True)
        em.op("vector", lambda e: e.reduce_sum(ssum[:, :], msk[:, :], axis=AX.X), incs=[em.inc("cs_v")], drain=True)
        em.op("vector", lambda e: e.tensor_scalar(ssum[:, :], ssum[:, :], 1e-8, None, op0=ALU.add), drain=True)
        em.op("vector", lambda e: e.reciprocal(rden[:, :], ssum[:, :]), drain=True)
        em.op("vector", lambda e, nt=nt: e.tensor_scalar(
            SEL[:, nt * 8:(nt + 1) * 8], msk[:, :], rden[:, 0:1], None, op0=ALU.mult),
            incs=[em.inc("cs_v")], drain=True)
        em.wait("tensor", em.sem("cs_v"), em.n("cs_v"))
        Pt = pt.begin(j)
        em.op("tensor", lambda e, Pt=Pt, nt=nt: e.transpose(
            Pt[:, :], SEL[:, nt * 8:(nt + 1) * 8], cst["ident_f32"][:, :]),
            incs=[pt.done_marker(j)])
        pt.wait_full("vector", j)
        em.op("vector", lambda e, Pt=Pt, nt=nt: e.tensor_copy(
            selT[0:8, nt * 128:(nt + 1) * 128], Pt[0:8, :]), incs=[pt.free_marker(j)])


def emit_b1_prep(em, cst, w, b1_sb):
    HT = HMOE // 128
    with em.scope():
        b1n = em.sb("b1n", [8, HMOE], F32)
        pb = PsumSlots(em, "psG", 2, lambda n: em.ps(n, [128, 8], F32))
        em.op("sync", lambda e: e.dma_start(out=b1n[:, :], in_=w["b1"]),
              incs=[em.inc("w_pl", 16)])
        em.wait("tensor", em.sem("w_pl"), em.n("w_pl"))
        for t in range(HT):
            j = t % 2
            P = pb.begin(j)
            em.op("tensor", lambda e, P=P, t=t: e.transpose(
                P[:, :], b1n[0:8, t * 128:(t + 1) * 128], cst["ident_f32"][0:8, 0:8]),
                incs=[pb.done_marker(j)])
            pb.wait_full("vector", j)
            em.op("vector", lambda e, P=P, t=t: e.tensor_copy(
                b1_sb[:, t * 8:(t + 1) * 8], P[:, :]), incs=[pb.free_marker(j)])
        em.barrier(("vector",), ("tensor", "vector", "scalar"))


def _w1_src(w1_ap, e_, ht):
    r = w1_ap[e_].rearrange("(f p) h -> p f h", p=128)
    return r[:, :, ht * 128:(ht + 1) * 128]


def _w2_src(w2_ap, e_, grp):
    r = w2_ap[e_].rearrange("(g r p) f -> p g r f", p=128, g=4)
    return r[:, grp, :, :]


def emit_moe(em, cst, w, xnT, SEL, selT, b1_sb, b2_bf, g_acc, T, w1s, w2s):
    NT = T // 128
    CH = min(1024, T)
    NCH = T // CH
    NTL = CH // 128
    HT = HMOE // 128
    NHALF = CH // 512 if CH >= 512 else 1
    MOVW = min(512, CH)
    hT = em.sb("moe_hT", [128, HT * CH], BF16)
    ph1 = PsumSlots(em, "psD1", 2, lambda n: em.ps(n, [128, CH], F32))
    ph2 = PsumSlots(em, "psD2", 2, lambda n: em.ps(n, [128, 1024], F32))
    for cc in range(NCH):
        for e_ in range(E):
            if em.n("moe_b") > 0:
                em.wait("scalar", em.sem("moe_b"), em.n("moe_b"))
            h_ready = {}
            for ht in range(HT):
                i = ht % 2
                if cc == 0 and e_ == 0 and ht < 2:
                    W1b = w1s.bufs[i]
                else:
                    W1b = w1s.load(
                        "gpsimd", i,
                        lambda b: b[:, :].rearrange("p (f c) -> p f c", f=8),
                        lambda b, e_=e_, ht=ht: _w1_src(w["w1"], e_, ht))
                if ht in (2, 4):
                    grp0 = 0 if ht == 2 else 1
                    if not (cc == 0 and e_ == 0 and grp0 == 0):
                        w2s.load(
                            "gpsimd", grp0,
                            lambda b: b[:, :].rearrange("p (r f) -> p r f", r=8),
                            lambda b, e_=e_, grp0=grp0: _w2_src(w["w2"], e_, grp0))
                w1s.wait_loaded("tensor", i)
                j = ht % 2
                P = ph1.begin(j)
                dm = ph1.done_marker(j)
                fw = w1s.free_marker(i)
                for f in range(ND):
                    for half in range(NHALF):
                        last = (f == ND - 1 and half == NHALF - 1)
                        em.op("tensor", lambda e, P=P, W1b=W1b, f=f, half=half, cc=cc, last=last: e.matmul(
                            P[:, half * MOVW:(half + 1) * MOVW],
                            W1b[:, f * 128:(f + 1) * 128],
                            xnT[:, f * T + cc * CH + half * MOVW: f * T + cc * CH + (half + 1) * MOVW],
                            start=(f == 0), stop=(f == ND - 1)),
                            incs=[dm, fw] if last else ())
                ph1.wait_full("scalar", j)
                em.op("scalar", lambda e, P=P, ht=ht, e_=e_: e.activation(
                    hT[:, ht * CH:(ht + 1) * CH], P[:, :], AF.Gelu_apprx_tanh,
                    bias=b1_sb[:, ht * E + e_: ht * E + e_ + 1]),
                    incs=[ph1.free_marker(j)])
                h_ready[ht] = [(ph1.st["fr"][jj], ph1.st["n_fr"][jj]) for jj in range(2)]
            for grp in range(4):
                i = grp % 2
                for s_, t_ in h_ready[8 * grp + 7]:
                    em.wait("tensor", s_, t_)
                w2s.wait_loaded("tensor", i)
                W2b = w2s.bufs[i]
                fw2 = w2s.free_marker(i)
                em.dr("vector")
                for ntl in range(NTL):
                    nt = cc * NTL + ntl
                    j = ntl % 2
                    P = ph2.begin(j)
                    dm = ph2.done_marker(j)
                    for r in range(8):
                        hh = grp * 8 + r
                        for half in range(2):
                            last = (r == 7 and half == 1)
                            mks = [dm] if last else []
                            if last and ntl == NTL - 1:
                                mks.append(fw2)
                            if last and ntl == NTL - 1 and grp == 3:
                                mks.append(em.inc("moe_b"))
                            em.op("tensor", lambda e, P=P, W2b=W2b, hh=hh, r=r, half=half, ntl=ntl: e.matmul(
                                P[:, half * 512:(half + 1) * 512],
                                hT[:, hh * CH + ntl * 128: hh * CH + (ntl + 1) * 128],
                                W2b[:, r * 1024 + half * 512: r * 1024 + (half + 1) * 512],
                                start=(r == 0), stop=(r == 7)), incs=mks)
                    ph2.wait_full("vector", j)
                    if ntl == NTL - 1 and grp + 2 < 4:
                        w2s.load(
                            "gpsimd", i,
                            lambda b: b[:, :].rearrange("p (r f) -> p r f", r=8),
                            lambda b, e_=e_, g2=grp + 2: _w2_src(w["w2"], e_, g2))
                    first = (e_ == 0 and grp == 0)
                    for half in range(2):
                        mk = [ph2.free_marker(j)] if half == 1 else ()
                        if first:
                            em.op("vector", lambda e, P=P, nt=nt, e_=e_, half=half: e.tensor_scalar(
                                g_acc[:, nt * 1024 + half * 512: nt * 1024 + (half + 1) * 512],
                                P[:, half * 512:(half + 1) * 512],
                                SEL[:, nt * 8 + e_: nt * 8 + e_ + 1], None, op0=ALU.mult), incs=mk)
                        else:
                            em.op("vector", lambda e, P=P, nt=nt, e_=e_, half=half: e.scalar_tensor_tensor(
                                g_acc[:, nt * 1024 + half * 512: nt * 1024 + (half + 1) * 512],
                                P[:, half * 512:(half + 1) * 512],
                                SEL[:, nt * 8 + e_: nt * 8 + e_ + 1],
                                g_acc[:, nt * 1024 + half * 512: nt * 1024 + (half + 1) * 512],
                                op0=ALU.mult, op1=ALU.add), incs=mk)
    em.dr("vector")
    g_ready = []
    for nt in range(NT):
        j = nt % 2
        P = ph2.begin(j)
        dm = ph2.done_marker(j)
        for half in range(2):
            em.op("tensor", lambda e, P=P, nt=nt, half=half: e.matmul(
                P[:, half * 512:(half + 1) * 512],
                selT[0:8, nt * 128:(nt + 1) * 128],
                b2_bf[0:8, half * 512:(half + 1) * 512],
                start=True, stop=True), incs=[dm] if half == 1 else ())
        ph2.wait_full("vector", j)
        em.op("vector", lambda e, P=P, nt=nt: e.tensor_tensor(
            g_acc[:, nt * 1024:(nt + 1) * 1024], P[:, :],
            g_acc[:, nt * 1024:(nt + 1) * 1024], op=ALU.add), incs=[ph2.free_marker(j)])
        g_ready.append((ph2.st["fr"][j], ph2.st["n_fr"][j]))
    return g_ready


def emit_l1ffn(em, cst, xnT, T, bf_dram, x_res, s_dram, x_out):
    HT1 = H1 // 128
    TH = min(1024, T)
    NHALVES = T // TH
    NTH = TH // 128
    CL = min(1024, TH)
    NCL = TH // CL
    MW = min(512, CL)
    NHF = max(1, CL // 512)
    NTr = T // 128
    with em.scope():
        gs = em.sb("l1_gs", [128, HT1 * TH], BF16)
        wbuf = [em.sb(f"l1_w{b}", [128, HT1 * D], BF16) for b in range(2)]
        pp = PsumSlots(em, "psD1", 2, lambda n: em.ps(n, [128, 1024], F32))
        xs = DmaSlots(em, "lnx", 2, lambda n: em.sb(n, [128, D], F32))
        ss = DmaSlots(em, "wo_s", 2, lambda n: em.sb(n, [128, D], BF16))
        outs = DmaSlots(em, "ro", 2, lambda n: em.sb(n, [128, D], F32))
        gb = [em.sb(f"l1_gb{p}", [128, D], BF16) for p in range(2)]
        ost = outs.st
        rct = {}

        def emit_store(nt):
            i2 = nt % 2
            em.wait("scalar", rct[nt][0], rct[nt][1])
            O = outs.bufs[i2]
            ost["n_ld"][i2] += 1
            em.op("scalar", lambda e, O=O, nt=nt: e.dma_start(
                out=x_out[nt * 128:(nt + 1) * 128, :], in_=O[:, :]),
                incs=[(ost["ld"][i2], 16)])
        phases = []
        for th in range(NHALVES):
            phases += [("g", th), ("u", th), ("d", th)]
        g_done = {}
        u_done = {}
        for k, (phz, th) in enumerate(phases):
            b = k % 2
            wT = wbuf[b]
            t0 = th * TH
            if k >= 2:
                em.wait("sync", em.sem(f"l1wf{b}"), em.n(f"l1wf{b}"))
            dtw = None
            if phz in ("g", "u"):
                wname = "fg" if phz == "g" else "fu"
                if k == 0:
                    dtw = []
                for dt in range(ND):
                    semn = f"wtrd{dt}" if k == 0 else f"l1ld{b}"
                    em.op("sync", lambda e, wT=wT, dt=dt, wname=wname: e.dma_start_transpose(
                        wT[:, dt * H1:(dt + 1) * H1],
                        bf_dram[wname][0:H1, dt * 128:(dt + 1) * 128]),
                        incs=[em.inc(semn, 16)])
                    if k == 0:
                        dtw.append((em.sem(semn), em.n(semn)))
            else:
                for ht in range(HT1):
                    em.op("sync", lambda e, wT=wT, ht=ht: e.dma_start_transpose(
                        wT[:, ht * D:(ht + 1) * D],
                        bf_dram["fd"][0:D, ht * 128:(ht + 1) * 128]),
                        incs=[em.inc(f"l1ld{b}", 16)])
            if dtw is None:
                em.wait("tensor", em.sem(f"l1ld{b}"), em.n(f"l1ld{b}"))

            if phz in ("g", "u"):
                if phz == "g" and em.n("l1d") > 0:
                    em.wait("scalar", em.sem("l1d"), em.n("l1d"))
                for ht in range(HT1):
                    for cc in range(NCL):
                        j = (ht * NCL + cc) % 2
                        P = pp.begin(j)
                        dm = pp.done_marker(j)
                        for dt in range(ND):
                            if dtw is not None and ht == 0 and cc == 0:
                                em.wait("tensor", dtw[dt][0], dtw[dt][1])
                            for half in range(NHF):
                                last = (dt == ND - 1 and half == NHF - 1)
                                mks = [dm] if last else []
                                if last and ht == HT1 - 1 and cc == NCL - 1:
                                    mks.append(em.inc(f"l1wf{b}"))
                                em.op("tensor", lambda e, P=P, wT=wT, dt=dt, ht=ht, cc=cc, half=half, t0=t0: e.matmul(
                                    P[:, half * MW:(half + 1) * MW],
                                    wT[:, dt * H1 + ht * 128: dt * H1 + (ht + 1) * 128],
                                    xnT[:, dt * T + t0 + cc * CL + half * MW: dt * T + t0 + cc * CL + (half + 1) * MW],
                                    start=(dt == 0), stop=(dt == ND - 1)), incs=mks)
                        if phz == "g":
                            pp.wait_full("scalar", j)
                            em.op("scalar", lambda e, P=P, ht=ht, cc=cc: e.activation(
                                gs[:, ht * TH + cc * CL: ht * TH + (cc + 1) * CL], P[:, 0:CL],
                                AF.Silu, scale=QI), incs=[pp.free_marker(j)])
                            g_done[ht] = (pp.st["fr"][j], pp.st["n_fr"][j])
                        else:
                            pp.wait_full("vector", j)
                            s_, t_ = g_done[ht]
                            em.wait("vector", s_, t_)
                            em.op("vector", lambda e, P=P, ht=ht, cc=cc: e.tensor_tensor(
                                gs[:, ht * TH + cc * CL: ht * TH + (cc + 1) * CL], P[:, 0:CL],
                                gs[:, ht * TH + cc * CL: ht * TH + (cc + 1) * CL], op=ALU.mult),
                                incs=[pp.free_marker(j)])
                            u_done[ht] = (pp.st["fr"][j], pp.st["n_fr"][j])
            else:
                for ntl in range(NTH):
                    nt = th * NTH + ntl
                    j = ntl % 2
                    P = pp.begin(j)
                    dm = pp.done_marker(j)
                    for ht in range(HT1):
                        if ntl == 0:
                            s_, t_ = u_done[ht]
                            em.wait("tensor", s_, t_)
                        for half in range(2):
                            last = (ht == HT1 - 1 and half == 1)
                            mks = [dm] if last else []
                            if last and ntl == NTH - 1:
                                mks.append(em.inc(f"l1wf{b}"))
                                mks.append(em.inc("l1d"))
                            em.op("tensor", lambda e, P=P, wT=wT, ht=ht, ntl=ntl, half=half: e.matmul(
                                P[:, half * 512:(half + 1) * 512],
                                gs[:, ht * TH + ntl * 128: ht * TH + (ntl + 1) * 128],
                                wT[:, ht * D + half * 512: ht * D + (half + 1) * 512],
                                start=(ht == 0), stop=(ht == HT1 - 1)), incs=mks)
                    i2 = nt % 2
                    X = xs.load("sync", i2, lambda b: b[:, :],
                                lambda b, nt=nt: x_res[nt * 128:(nt + 1) * 128, :])
                    S = ss.load("sync", i2, lambda b: b[:, :],
                                lambda b, nt=nt: s_dram[nt * 128:(nt + 1) * 128, :])
                    if nt >= 1:
                        emit_store(nt - 1)
                    if ost["n_ld"][i2] > ost["n_fr"][i2]:
                        em.wait("scalar", ost["ld"][i2], 16 * ost["n_ld"][i2])
                        ost["n_fr"][i2] += 1
                        em.op("scalar", lambda e: e.nop(), incs=[(ost["fr"][i2], 1)], drain=False)
                    pp.wait_full("scalar", j)
                    if nt >= 2:
                        em.wait("scalar", rct[nt - 2][0], rct[nt - 2][1])
                    em.op("scalar", lambda e, P=P, B=gb[i2]: e.activation(
                        B[:, :], P[:, :], AF.Silu, scale=QI * QI), incs=[pp.free_marker(j)])
                    b_rdy = (pp.st["fr"][j], pp.st["n_fr"][j])
                    em.wait("vector", b_rdy[0], b_rdy[1])
                    ss.wait_loaded("vector", i2)
                    if ost["n_fr"][i2] > 0:
                        em.wait("vector", ost["fr"][i2], ost["n_fr"][i2])
                    O = outs.bufs[i2]
                    em.op("vector", lambda e, S=S, O=O, p=i2: e.tensor_tensor(
                        O[:, :], S[:, :], gb[p][:, :], op=ALU.mult),
                        incs=[ss.free_marker(i2)], drain=True)
                    xs.wait_loaded("vector", i2)
                    em.op("vector", lambda e, X=X, O=O: e.tensor_tensor(
                        O[:, :], X[:, :], O[:, :], op=ALU.add),
                        incs=[xs.free_marker(i2), em.inc("cs_t")], drain=True)
                    rct[nt] = (em.sem("cs_t"), em.n("cs_t"))
        emit_store(NTr - 1)
        for i2 in range(2):
            while ost["n_fr"][i2] < ost["n_ld"][i2]:
                em.wait("scalar", ost["ld"][i2], 16 * ost["n_ld"][i2])
                ost["n_fr"][i2] += 1
                em.op("scalar", lambda e: e.nop(), incs=[(ost["fr"][i2], 1)], drain=False)
        em.barrier(("tensor", "scalar", "vector", "sync"), ENGS)


def emit_boundary(em, cst, x_prev, s_dram, g_dram, g_tgts, x_mid, xnT, T, g_needs_silu):
    NT = T // 128
    with em.scope():
        xs = DmaSlots(em, "lnx", 2, lambda n: em.sb(n, [128, D], F32))
        ss = DmaSlots(em, "wo_s", 2, lambda n: em.sb(n, [128, D], BF16))
        gg = DmaSlots(em, "rg", 2, lambda n: em.sb(n, [128, D], BF16))
        outs = DmaSlots(em, "ro", 2, lambda n: em.sb(n, [128, D], F32))
        pt = PsumSlots(em, "psT", 2, lambda n: em.ps(n, [128, 128], BF16))
        gsil = [em.sb(f"b_gsil{p}", [128, D], F32) for p in range(2)]
        tmp = [em.sb(f"b_tmp{p}", [128, D], F32) for p in range(2)]
        xc = [em.sb(f"b_xc{p}", [128, D], F32) for p in range(2)]
        xnt = [em.sb(f"b_xnt{p}", [128, D], BF16) for p in range(2)]
        sq = em.sb("b_sq", [128, D], F32)
        mean = [em.sb(f"b_mean{p}", [128, 1], F32) for p in range(2)]
        var = [em.sb(f"b_var{p}", [128, 1], F32) for p in range(2)]
        rstd = [em.sb(f"b_rstd{p}", [128, 1], F32) for p in range(2)]
        ost = outs.st
        ct = {}
        cssn = {}

        def emit_silu(k):
            i = k % 2
            if k >= 2:
                em.wait("scalar", ct[k - 2][0], ct[k - 2][1])
            if g_tgts is not None:
                em.wait("sync", g_tgts[k][0], g_tgts[k][1])
            G = gg.load("sync", i, lambda b: b[:, :],
                        lambda b, k=k: g_dram[k * 128:(k + 1) * 128, :])
            gg.wait_loaded("scalar", i)
            em.op("scalar", lambda e, G=G, p=i: e.activation(gsil[p][:, :], G[:, :], AF.Silu),
                  incs=[gg.free_marker(i), em.inc("cs_s")])
            cssn[k] = (em.sem("cs_s"), em.n("cs_s"))

        def stage_a(nt):
            i = nt % 2
            if ost["n_ld"][i] > ost["n_fr"][i]:
                em.wait("scalar", ost["ld"][i], 16 * ost["n_ld"][i])
                ost["n_fr"][i] += 1
                em.op("scalar", lambda e: e.nop(), incs=[(ost["fr"][i], 1)], drain=False)
            X = xs.load("sync", i, lambda b: b[:, :],
                        lambda b, nt=nt: x_prev[nt * 128:(nt + 1) * 128, :])
            S = ss.load("sync", i, lambda b: b[:, :],
                        lambda b, nt=nt: s_dram[nt * 128:(nt + 1) * 128, :])
            if g_needs_silu:
                if nt == 0:
                    emit_silu(0)
                if nt + 1 < NT:
                    emit_silu(nt + 1)
                em.wait("vector", cssn[nt][0], cssn[nt][1])
                ss.wait_loaded("vector", i)
                em.op("vector", lambda e, S=S, p=i: e.tensor_tensor(
                    tmp[p][:, :], S[:, :], gsil[p][:, :], op=ALU.mult),
                    incs=[ss.free_marker(i)], drain=True)
            else:
                if g_tgts is not None:
                    em.wait("sync", g_tgts[nt][0], g_tgts[nt][1])
                G = gg.load("sync", i, lambda b: b[:, :],
                            lambda b, nt=nt: g_dram[nt * 128:(nt + 1) * 128, :])
                ss.wait_loaded("vector", i)
                gg.wait_loaded("vector", i)
                em.op("vector", lambda e, S=S, G=G, p=i: e.tensor_tensor(
                    tmp[p][:, :], S[:, :], G[:, :], op=ALU.mult),
                    incs=[ss.free_marker(i), gg.free_marker(i)], drain=True)
            xs.wait_loaded("vector", i)
            if ost["n_fr"][i] > 0:
                em.wait("vector", ost["fr"][i], ost["n_fr"][i])
            O = outs.bufs[i]
            em.op("vector", lambda e, X=X, O=O, p=i: e.tensor_tensor(
                O[:, :], X[:, :], tmp[p][:, :], op=ALU.add),
                incs=[xs.free_marker(i), em.inc("cs_t")], drain=True)
            ct[nt] = (em.sem("cs_t"), em.n("cs_t"))

        def stage_b(nt):
            i = nt % 2
            O = outs.bufs[i]
            em.wait("scalar", ct[nt][0], ct[nt][1])
            ost["n_ld"][i] += 1
            em.op("scalar", lambda e, O=O, nt=nt: e.dma_start(
                out=x_mid[nt * 128:(nt + 1) * 128, :], in_=O[:, :]), incs=[(ost["ld"][i], 16)])
            em.op("vector", lambda e, O=O, p=i: e.reduce_sum(mean[p][:, :], O[:, :], axis=AX.X),
                  drain=True)
            em.op("vector", lambda e, p=i: e.tensor_scalar(
                mean[p][:, :], mean[p][:, :], 1.0 / D, None, op0=ALU.mult), drain=True)
            if nt >= 2:
                em.wait("vector", em.sem(f"lnxn{i}"), em.n(f"lnxn{i}"))
            em.op("vector", lambda e, O=O, p=i: e.tensor_scalar(
                xc[p][:, :], O[:, :], mean[p][:, 0:1], None, op0=ALU.subtract),
                incs=[em.inc(f"lnv{i}")], drain=True)

        def stage_c(nt):
            i = nt % 2
            em.wait("scalar", em.sem(f"lnv{i}"), em.n(f"lnv{i}"))
            if nt >= 2:
                em.wait("scalar", em.sem(f"lnt{i}"), em.n(f"lnt{i}"))
            em.op("scalar", lambda e, p=i: e.activation(
                sq[:, :], xc[p][:, :], AF.Square, accum_out=var[p][:, :]), drain=True)
            em.op("scalar", lambda e, p=i: e.activation(
                var[p][:, :], var[p][:, :], AF.Ln, bias=cst["epsb"][:, 0:1], scale=1.0 / D),
                drain=True)
            em.op("scalar", lambda e, p=i: e.activation(
                rstd[p][:, :], var[p][:, :], AF.Exp, scale=-0.5), drain=True)
            em.op("scalar", lambda e, p=i: e.activation(
                xnt[p][:, :], xc[p][:, :], AF.Copy, scale=rstd[p][:, 0:1]),
                incs=[em.inc(f"lnxn{i}")], drain=True)
            em.wait("tensor", em.sem(f"lnxn{i}"), em.n(f"lnxn{i}"))
            for dt in range(ND):
                j = dt % 2
                P = pt.begin(j)
                last = (dt == ND - 1)
                em.op("tensor", lambda e, P=P, dt=dt, p=i: e.transpose(
                    P[:, :], xnt[p][:, dt * 128:(dt + 1) * 128], cst["ident_bf"][:, :]),
                    incs=[pt.done_marker(j), em.inc(f"lnt{i}") if last else None])
                eng = "vector" if j == 0 else "scalar"
                pt.wait_full(eng, j)
                if eng == "vector":
                    em.op(eng, lambda e, P=P, dt=dt, nt=nt: e.tensor_copy(
                        xnT[:, dt * T + nt * 128: dt * T + (nt + 1) * 128], P[:, :]),
                        incs=[pt.free_marker(j)])
                else:
                    em.op(eng, lambda e, P=P, dt=dt, nt=nt: e.activation(
                        xnT[:, dt * T + nt * 128: dt * T + (nt + 1) * 128], P[:, :], AF.Copy),
                        incs=[pt.free_marker(j)])

        for it in range(NT + 2):
            if it < NT:
                stage_a(it)
            if 1 <= it <= NT:
                stage_b(it - 1)
            if 2 <= it <= NT + 1:
                stage_c(it - 2)
        for i in range(2):
            while ost["n_fr"][i] < ost["n_ld"][i]:
                em.wait("scalar", ost["ld"][i], 16 * ost["n_ld"][i])
                ost["n_fr"][i] += 1
                em.op("scalar", lambda e: e.nop(), incs=[(ost["fr"][i], 1)], drain=False)


def emit_residual(em, cst, x_ap, s_dram, g_buf, g_dram, x_out_ap, T, g_needs_silu):
    NT = T // 128
    xs = DmaSlots(em, "lnx", 2, lambda n: em.sb(n, [128, D], F32))
    ss = DmaSlots(em, "wo_s", 2, lambda n: em.sb(n, [128, D], BF16))
    outs = DmaSlots(em, "ro", 2, lambda n: em.sb(n, [128, D], F32))
    gg = DmaSlots(em, "rg", 2, lambda n: em.sb(n, [128, D], BF16)) if g_dram is not None else None
    gsil = em.sb("r_gsil", [128, D], F32)
    tmp = em.sb("r_tmp", [128, D], F32)
    ost = outs.st
    res_done = []
    for nt in range(NT):
        i = nt % 2
        if ost["n_ld"][i] > 0:
            em.wait("scalar", ost["ld"][i], 16 * ost["n_ld"][i])
            ost["n_fr"][i] += 1
            em.op("scalar", lambda e: e.nop(), incs=[(ost["fr"][i], 1)], drain=False)
        X = xs.load("sync", i, lambda b: b[:, :],
                    lambda b, nt=nt: x_ap[nt * 128:(nt + 1) * 128, :])
        S = ss.load("sync", i, lambda b: b[:, :],
                    lambda b, nt=nt: s_dram[nt * 128:(nt + 1) * 128, :])
        if gg is not None:
            G = gg.load("sync", i, lambda b: b[:, :],
                        lambda b, nt=nt: g_dram[nt * 128:(nt + 1) * 128, :])
        if g_needs_silu:
            if nt > 0:
                em.wait("scalar", em.sem("cs_v"), em.n("cs_v"))
            em.op("scalar", lambda e, nt=nt: e.activation(
                gsil[:, :], g_buf[:, nt * D:(nt + 1) * D], AF.Silu), incs=[em.inc("cs_s")])
            em.wait("vector", em.sem("cs_s"), em.n("cs_s"))
        ss.wait_loaded("vector", i)
        if g_needs_silu:
            em.op("vector", lambda e, S=S: e.tensor_tensor(
                tmp[:, :], S[:, :], gsil[:, :], op=ALU.mult),
                incs=[ss.free_marker(i), em.inc("cs_v")], drain=True)
        elif gg is not None:
            gg.wait_loaded("vector", i)
            em.op("vector", lambda e, S=S, G=G: e.tensor_tensor(
                tmp[:, :], S[:, :], G[:, :], op=ALU.mult),
                incs=[ss.free_marker(i), gg.free_marker(i)], drain=True)
        else:
            em.op("vector", lambda e, S=S, nt=nt: e.tensor_tensor(
                tmp[:, :], S[:, :], g_buf[:, nt * D:(nt + 1) * D], op=ALU.mult),
                incs=[ss.free_marker(i)], drain=True)
        xs.wait_loaded("vector", i)
        if ost["n_ld"][i] > 0:
            em.wait("vector", ost["fr"][i], ost["n_fr"][i])
        O = outs.bufs[i]
        em.op("vector", lambda e, X=X, O=O: e.tensor_tensor(
            O[:, :], X[:, :], tmp[:, :], op=ALU.add),
            incs=[xs.free_marker(i), em.inc("cs_t")], drain=True)
        em.wait("scalar", em.sem("cs_t"), em.n("cs_t"))
        ost["n_ld"][i] += 1
        em.op("scalar", lambda e, O=O, nt=nt: e.dma_start(
            out=x_out_ap[nt * 128:(nt + 1) * 128, :], in_=O[:, :]), incs=[(ost["ld"][i], 16)])
        res_done.append((ost["ld"][i], 16 * ost["n_ld"][i]))
    for i in range(2):
        em.wait("scalar", ost["ld"][i], 16 * ost["n_ld"][i])
    return res_done



def emit_weight_casts(em, w_all, layers, phase="all", prep=None):
    nc = em.nc
    prep = prep if prep is not None else {}
    for lay in layers:
        w = w_all[lay]
        sem = f"w_cast{lay}"
        if lay == 0 and phase == "post":
            sem = "w_cast0p"
        p = prep.setdefault(lay, {"bf_dram": {}, "sem": f"w_cast{lay}"})
        p.setdefault("sem_post", "w_cast0p" if lay == 0 else f"w_cast{lay}")
        bf_dram = p["bf_dram"]

        def cast_dma(dst, src, sem=sem):
            em.op("gpsimd", lambda e, dst=dst, src=src: e.dma_start(out=dst, in_=src),
                  incs=[em.inc(sem, 16)], drain=False)

        if lay == 0:
            if phase in ("pre", "all"):
                bf_dram["wv"] = _dram(nc, "wv_bf", [D, D], BF16)
                cast_dma(bf_dram["wv"], w["wv"])
                p["b2_bf"] = em.sb("b2c", [8, D], BF16)
                p["wg_bf"] = em.sb("wgc", [128, ND * 8], BF16)
                p["bg_bf"] = em.sb("bgc", [1, 8], BF16)
                cast_dma(p["b2_bf"][:, :], w["b2"])
                cast_dma(p["wg_bf"][:, :].rearrange("p (d e) -> p d e", d=ND),
                         w["wg"].rearrange("(d p) e -> p d e", p=128))
                cast_dma(p["bg_bf"][:, :], w["bg"])
            if phase in ("post", "all"):
                for nm in ("wq", "wk", "wo"):
                    bf_dram[nm] = _dram(nc, f"{nm}_bf", [D, D], BF16)
                    cast_dma(bf_dram[nm], w[nm])
        else:
            if phase in ("post", "all"):
                bf_dram["wqkv"] = _dram(nc, "wqkv_bf", [3 * D, D], BF16)
                cast_dma(bf_dram["wqkv"], w["wqkv"])
                bf_dram["wproj"] = _dram(nc, "wproj_bf", [D, D], BF16)
                cast_dma(bf_dram["wproj"], w["wproj"])
                for nm in ("fg", "fu"):
                    bf_dram[nm] = _dram(nc, f"{nm}_bf", [H1, D], BF16)
                    cast_dma(bf_dram[nm], w[nm])
                bf_dram["fd"] = _dram(nc, "fd_bf", [D, H1], BF16)
                cast_dma(bf_dram["fd"], w["fd"])
    return prep


def emit_layer(em, cst, w, lay, x_in, x_out, T, prep, boundary_in=None,
               post_ln=None, post_attn=None, is_last=True):
    nc = em.nc
    tag = f"L{lay}"
    NT = T // 128
    CW = min(512, T)
    bf_dram = prep["bf_dram"]
    cast_sem = prep["sem"]

    if lay == 0:
        b1_sb = em.sb(f"b1{tag}", [128, (HMOE // 128) * E], F32)
        b2_bf = prep["b2_bf"]
        wg_bf = prep["wg_bf"]
        bg_bf = prep["bg_bf"]

    xnT = em.sb(f"xnT{tag}", [128, ND * T], BF16)
    s_dram = _dram(nc, f"s_scr{tag}", [T, D], BF16)

    if boundary_in is None:
        emit_ln(em, cst, x_in, xnT, T)
    else:
        emit_boundary(em, cst, boundary_in["x_prev"], boundary_in["s_dram"],
                      boundary_in["g_dram"], boundary_in["g_tgts"], x_in, xnT, T,
                      boundary_in["silu"])
    if post_ln is not None:
        post_ln()
    em.barrier(("vector", "tensor", "scalar", "sync"), ENGS)

    with em.scope():
        v_buf = em.sb("vbuf", [128, NT * (NH * 65)], BF16)
        attn_buf = em.sb("attnT", [128, ND * T], BF16)
        woT = em.sb("woT", [128, 4 * D], BF16)
        with em.scope():
            wT = em.sb("wvT", [128, ND * 1024], BF16)
            em.wait("sync", em.sem(cast_sem), em.n(cast_sem))
            for dt in range(ND):
                if lay == 0:
                    srcw = bf_dram["wv"][0:D, dt * 128:(dt + 1) * 128]
                else:
                    srcw = bf_dram["wqkv"][2 * D:3 * D, dt * 128:(dt + 1) * 128]
                em.op("sync", lambda e, dt=dt, srcw=srcw: e.dma_start_transpose(
                    wT[:, dt * 1024:(dt + 1) * 1024], srcw), incs=[em.inc("w_pl", 16)])
            em.wait("tensor", em.sem("w_pl"), em.n("w_pl"))
            emit_v_tokmajor(em, cst, wT, xnT, v_buf, T, None)
            em.barrier(("tensor", "scalar", "vector", "gpsimd"),
                       ("tensor", "scalar", "vector", "sync", "gpsimd"))
        with em.scope():
            if lay == 0:
                wq_src, wk_src = bf_dram["wq"], bf_dram["wk"]
            else:
                wq_src = bf_dram["wqkv"][0:D, :]
                wk_src = bf_dram["wqkv"][D:2 * D, :]
            sp = prep["sem_post"]
            em.wait("sync", em.sem(sp), em.n(sp))
            emit_qk_attention(em, cst, wq_src, wk_src, xnT, v_buf, attn_buf, T, CW)
            em.wait("sync", em.sem("wo_tr"), em.n("wo_tr"))
            srcw = bf_dram["wo"] if lay == 0 else bf_dram["wproj"]
            for dt in range(4):
                em.op("sync", lambda e, dt=dt, srcw=srcw: e.dma_start_transpose(
                    woT[:, dt * D:(dt + 1) * D], srcw[0:D, dt * 128:(dt + 1) * 128]),
                    incs=[em.inc("wo_tr", 16)])
            wo_wA = (em.sem("wo_tr"), em.n("wo_tr"))
            if post_attn is not None:
                post_attn()
            em.barrier(("tensor", "vector", "scalar"), ENGS)
        with em.scope():
            woT2 = em.sb("woT2", [128, 4 * D], BF16)
            em.wait("sync", em.sem("wo_tr"), em.n("wo_tr"))
            for dt in range(4, ND):
                em.op("sync", lambda e, dt=dt, srcw=srcw: e.dma_start_transpose(
                    woT2[:, (dt - 4) * D:(dt - 3) * D], srcw[0:D, dt * 128:(dt + 1) * 128]),
                    incs=[em.inc("wo_tr", 16)])
            wo_wB = (em.sem("wo_tr"), em.n("wo_tr"))
            wo_aps = [woT[:, dt * D:(dt + 1) * D] for dt in range(4)] + \
                     [woT2[:, (dt - 4) * D:(dt - 3) * D] for dt in range(4, ND)]
            wo_w = [wo_wA] * 4 + [wo_wB] * 4
            emit_out_proj(em, cst, attn_buf, wo_aps, s_dram, T, wo_w)
            em.barrier(("tensor", "scalar", "sync", "vector"), ENGS)

    g_buf = g_dram = None
    g_tgts = None
    if lay == 0:
        g_buf = em.sb(f"gbuf{tag}", [128, NT * D], BF16)
        if not is_last:
            g_dram = _dram(nc, f"g_scr{tag}", [T, D], BF16)
        em.wait("tensor", em.sem(cast_sem), em.n(cast_sem))
        with em.scope():
            SEL = em.sb("SEL", [128, NT * 8], F32)
            selT = em.sb("selT", [8, T], BF16)
            w1s = DmaSlots(em, "w1s", 2, lambda n: em.sb(n, [128, 8 * 128], BF16))
            w2s = DmaSlots(em, "w2s", 2, lambda n: em.sb(n, [128, 8 * 1024], BF16))
            for ht0 in range(2):
                w1s.load("gpsimd", ht0,
                         lambda b: b[:, :].rearrange("p (f c) -> p f c", f=8),
                         lambda b, ht0=ht0: _w1_src(w["w1"], 0, ht0))
            w2s.load("gpsimd", 0,
                     lambda b: b[:, :].rearrange("p (r f) -> p r f", r=8),
                     lambda b: _w2_src(w["w2"], 0, 0))
            emit_b1_prep(em, cst, w, b1_sb)
            with em.scope():
                emit_gates(em, cst, xnT, wg_bf, bg_bf, SEL, selT, T)
                em.barrier(("vector", "tensor", "scalar"), ENGS)
            with em.scope():
                g_ready = emit_moe(em, cst, w, xnT, SEL, selT, b1_sb, b2_bf, g_buf, T, w1s, w2s)
                if not is_last:
                    g_tgts = []
                    gscnt = [0, 0]
                    for nt in range(NT):
                        i = nt % 2
                        s_, t_ = g_ready[nt]
                        em.wait("scalar", s_, t_)
                        if gscnt[i] > 0:
                            em.wait("scalar", em.sem(f"gs{i}"), 16 * gscnt[i])
                        em.op("scalar", lambda e, nt=nt: e.dma_start(
                            out=g_dram[nt * 128:(nt + 1) * 128, :],
                            in_=g_buf[:, nt * D:(nt + 1) * D]),
                            incs=[em.inc(f"gs{i}", 16)])
                        gscnt[i] += 1
                        g_tgts.append((em.sem(f"gs{i}"), 16 * gscnt[i]))
                    for i in range(2):
                        if gscnt[i]:
                            em.wait("scalar", em.sem(f"gs{i}"), 16 * gscnt[i])
                em.barrier(("tensor", "vector", "scalar", "gpsimd"), ENGS)
        g_needs_silu = True
    else:
        assert is_last
        emit_l1ffn(em, cst, xnT, T, bf_dram, x_in, s_dram, x_out)
        return None

    if is_last:
        emit_residual(em, cst, x_in, s_dram, g_buf, g_dram, x_out, T, g_needs_silu)
        em.barrier(("sync", "vector", "scalar"), ENGS)
        return None
    return {"x_prev": x_in, "s_dram": s_dram, "g_dram": g_dram,
            "g_tgts": g_tgts, "silu": g_needs_silu}



def _dram(nc, name, shape, dtype, kind="Internal"):
    return nc.dram_tensor(name, shape, dtype, kind=kind).ap()


_CONST_SPECS = {
    "ident_bf": ([128, 128], BF16), "ident_f32": ([128, 128], F32),
    "trimask_bf": ([128, 128], BF16),
    "selA_bf": ([128, 1], BF16), "selB_bf": ([128, 1], BF16),
    "ones_1x128_bf": ([1, 128], BF16), "ones_1x128_f32": ([1, 128], F32),
    "ones_1x64_f32": ([1, 64], F32), "epsb": ([128, 1], F32),
    "sel2_bf": ([128, 2], BF16), "bc2a_f32": ([2, 128], F32), "bc2b_f32": ([2, 128], F32),
}


def _const_arrays():
    bf = ml_dtypes.bfloat16
    tri = np.triu(np.ones((128, 128), dtype=np.float32)).astype(bf)
    selA = np.zeros((128, 1), dtype=np.float32); selA[:64] = 1.0
    selB = np.zeros((128, 1), dtype=np.float32); selB[64:] = 1.0
    return {
        "ident_bf": np.eye(128, dtype=np.float32).astype(bf),
        "ident_f32": np.eye(128, dtype=np.float32),
        "trimask_bf": tri,
        "selA_bf": selA.astype(bf),
        "selB_bf": selB.astype(bf),
        "ones_1x128_bf": np.ones((1, 128), dtype=np.float32).astype(bf),
        "ones_1x128_f32": np.ones((1, 128), dtype=np.float32),
        "ones_1x64_f32": np.ones((1, 64), dtype=np.float32),
        "epsb": np.full((128, 1), 1e-5, dtype=np.float32),
        "sel2_bf": np.stack([np.r_[np.ones(64), np.zeros(64)],
                             np.r_[np.zeros(64), np.ones(64)]], axis=1
                            ).astype(np.float32).astype(bf),
        "bc2a_f32": np.stack([np.ones(128), np.zeros(128)]).astype(np.float32),
        "bc2b_f32": np.stack([np.zeros(128), np.ones(128)]).astype(np.float32),
    }


def build_nc(layers, T=2048):
    nc = bass.Bass("TRN2", target_bir_lowering=False)
    cst_dram = {k: _dram(nc, k, shp, dt, kind="ExternalInput")
                for k, (shp, dt) in _CONST_SPECS.items()}
    x_in = _dram(nc, "x_in", [T, D], F32, kind="ExternalInput")
    x_out = _dram(nc, "x_out", [T, D], F32, kind="ExternalOutput")
    x_mid = _dram(nc, "x_mid", [T, D], F32) if len(layers) > 1 else None

    w_all = {}
    for lay in layers:
        if lay == 0:
            w_all[0] = {
                "wq": _dram(nc, "wq", [D, D], F32, kind="ExternalInput"),
                "wk": _dram(nc, "wk", [D, D], F32, kind="ExternalInput"),
                "wv": _dram(nc, "wv", [D, D], F32, kind="ExternalInput"),
                "wo": _dram(nc, "wo", [D, D], F32, kind="ExternalInput"),
                "wg": _dram(nc, "wg", [D, E], F32, kind="ExternalInput"),
                "bg": _dram(nc, "bg", [1, E], F32, kind="ExternalInput"),
                "w1": _dram(nc, "w1", [E, D, HMOE], F32, kind="ExternalInput"),
                "b1": _dram(nc, "b1", [E, HMOE], F32, kind="ExternalInput"),
                "w2": _dram(nc, "w2", [E, HMOE, D], F32, kind="ExternalInput"),
                "b2": _dram(nc, "b2", [E, D], F32, kind="ExternalInput"),
            }
        else:
            w_all[1] = {
                "wqkv": _dram(nc, "wqkv", [3 * D, D], F32, kind="ExternalInput"),
                "wproj": _dram(nc, "wproj", [D, D], F32, kind="ExternalInput"),
                "fg": _dram(nc, "fg", [H1, D], I32, kind="ExternalInput"),
                "fu": _dram(nc, "fu", [H1, D], I32, kind="ExternalInput"),
                "fd": _dram(nc, "fd", [D, H1], I32, kind="ExternalInput"),
            }

    with ExitStack() as gstack:
        block = gstack.enter_context(nc.Block())
        em = Em(nc, gstack, block)
        cst = {k: em.sb(f"c_{k}", shp, dt) for k, (shp, dt) in _CONST_SPECS.items()}
        for k in _CONST_SPECS:
            em.op("sync", lambda e, k=k: e.dma_start(out=cst[k][:, :], in_=cst_dram[k]),
                  incs=[em.inc("w_pl", 16)])
        em.wait("sync", em.sem("w_pl"), em.n("w_pl"))
        em.barrier(("sync",), ENGS)

        prep = emit_weight_casts(em, w_all, layers[:1], phase="pre")
        post_done = [False]
        if len(layers) == 1 and layers[0] == 1:
            emit_weight_casts(em, w_all, layers, phase="post", prep=prep)
            post_done[0] = True

        def post_ln():
            if post_done[0]:
                return
            post_done[0] = True
            tgt = max(0, em.n("lnv0") - 2)
            if tgt:
                em.wait("gpsimd", em.sem("lnv0"), tgt)
            emit_weight_casts(em, w_all, layers[:1], phase="post", prep=prep)

        def post_attn():
            if len(layers) <= 1 or 1 in prep and prep[1]["bf_dram"]:
                return
            tgt = em.n("at_r") // 2
            if tgt:
                em.wait("gpsimd", em.sem("at_r"), tgt)
            emit_weight_casts(em, w_all, layers[1:], phase="post", prep=prep)

        cur_in = x_in
        binfo = None
        for idx, lay in enumerate(layers):
            last = (idx == len(layers) - 1)
            out_ap = x_out if last else x_mid
            with em.scope():
                binfo = emit_layer(em, cst, w_all[lay], lay, cur_in, out_ap, T,
                                   prep[lay], boundary_in=binfo,
                                   post_ln=(post_ln if idx == 0 else None),
                                   post_attn=(post_attn if idx == 0 else None),
                                   is_last=last)
            cur_in = out_ap
        em.flush()
    return nc


def _in_map_for(layers, x_core, inputs):
    m = dict(_const_arrays())
    m["x_in"] = x_core
    if 0 in layers:
        m.update({
            "wq": inputs["L0_attn_Wq"], "wk": inputs["L0_attn_Wk"],
            "wv": inputs["L0_attn_Wv"], "wo": inputs["L0_attn_Wo"],
            "wg": inputs["L0_ffn_Wg"],
            "bg": np.ascontiguousarray(np.asarray(inputs["L0_ffn_bg"]).reshape(1, E)),
            "w1": inputs["L0_ffn_W1"], "b1": inputs["L0_ffn_b1"],
            "w2": inputs["L0_ffn_W2"], "b2": inputs["L0_ffn_b2"],
        })
    if 1 in layers:
        m.update({
            "wqkv": inputs["L1_attn_w_qkv"], "wproj": inputs["L1_attn_w_proj"],
            "fg": inputs["L1_ffn_w_gate"], "fu": inputs["L1_ffn_w_up"],
            "fd": inputs["L1_ffn_w_down"],
        })
    return {k: np.ascontiguousarray(np.asarray(v)) for k, v in m.items()}


def kernel(**inputs):
    x = np.ascontiguousarray(np.asarray(inputs["L0_attn_x"], dtype=np.float32))
    B = x.shape[0]
    layers = (0, 1)
    nc = build_nc(layers)
    in_maps = [_in_map_for(layers, np.ascontiguousarray(x[c]), inputs) for c in range(B)]
    res = run_bass_kernel_spmd(nc, in_maps, core_ids=list(range(B)))
    return np.stack([np.asarray(r["x_out"], dtype=np.float32) for r in res.results], axis=0)
